# version 01
# baseline (speedup 1.0000x reference)
import numpy as np
import ml_dtypes
import concourse.bass as bass
import concourse.mybir as mybir
from concourse.bass_utils import run_bass_kernel_spmd
from contextlib import ExitStack

F32 = mybir.dt.float32
BF16 = mybir.dt.bfloat16
AF = mybir.ActivationFunctionType
ALU = mybir.AluOpType
AX = mybir.AxisListType
BF = ml_dtypes.bfloat16

ENGINES = ("sync", "scalar", "vector", "tensor", "gpsimd")
NCORES = 8
S = 2048
NB = 2
NT = S // 128
D = 1024


class Prog:
    def __init__(self):
        self.ops = []

    def emit(self, eng, fn, delta=1):
        self.ops.append((eng, fn, delta))

    def dma(self, out, in_, eng="sync", **kw):
        self.emit(eng, lambda E, o=out, i=in_, kw=kw: E.dma_start(o, i, **kw), 16)

    def op(self, eng, name, *args, **kw):
        self.emit(eng, lambda E, n=name, a=args, kw=kw: getattr(E, n)(*a, **kw), 1)

    def mm(self, out, lhsT, rhs, start, stop):
        self.op("tensor", "matmul", out, lhsT, rhs, start=start, stop=stop)

    def T(self, out, in_, ident):
        self.op("tensor", "transpose", out, in_, ident)

    def cp(self, dst, src):
        self.op("scalar", "copy", dst, src)

    def act(self, out, in_, func, **kw):
        self.op("scalar", "activation", out, in_, func, **kw)

    def v(self, name, *args, **kw):
        self.op("vector", name, *args, **kw)


def finalize(nc, P, sem):
    pre = []
    v = 0
    for (e, fn, d) in P.ops:
        pre.append(v)
        v += d
    total = v
    ops = P.ops
    with nc.Block() as block:
        def mk(eng_name):
            def body(E):
                for i, (e, fn, d) in enumerate(ops):
                    if e != eng_name:
                        continue
                    if i > 0:
                        E.wait_ge(sem, pre[i])
                    fn(E).then_inc(sem, d)
                if eng_name == "sync":
                    E.wait_ge(sem, total)
            return body
        for en in ENGINES:
            getattr(block, en)(mk(en))


def _tables():
    t = {}
    t["identb"] = np.eye(128, dtype=np.float32).astype(BF)
    t["identf"] = np.eye(128, dtype=np.float32)
    t["onesb"] = np.ones((1, 128), dtype=np.float32).astype(BF)
    t["onesfr"] = np.ones((1, 128), dtype=np.float32)
    t["onesfc"] = np.ones((128, 1), dtype=np.float32)

    hd = 64
    inv = 1.0 / 10000.0 ** (np.arange(0, hd, 2, dtype=np.float64) / hd)
    pos = np.arange(S, dtype=np.float64)
    fr = np.outer(pos, inv)
    emb = np.concatenate([fr, fr], -1)
    cos = np.cos(emb)
    sin = np.sin(emb)
    sgn = np.where(np.arange(hd) % 2 == 0, -1.0, 1.0)
    c = 8.0 ** -0.5
    t["ropeA_cos"] = np.tile(cos * c, (1, 16)).astype(np.float32)
    t["ropeA_sin"] = np.tile(sin * sgn * c, (1, 16)).astype(np.float32)

    WIN = 512
    half = np.arange(0, hd, 2, dtype=np.float64)
    invf = 1.0 / 10000.0 ** (half / hd)
    tt = np.arange(S, dtype=np.float64)
    fr2 = tt[:, None] * invf[None, :]
    fr2 = np.concatenate([fr2, fr2], -1)
    base = (half + 0.4 * hd) / (1.4 * hd)
    power = (tt - S // 2) / float(WIN // 2)
    scale = base[None, :] ** power[:, None]
    scale = np.concatenate([scale, scale], -1)
    cos2 = np.cos(fr2)
    sin2 = np.sin(fr2)
    sgn2 = np.where(np.arange(hd) < 32, -1.0, 1.0)
    t["xpos_cq"] = np.tile(cos2 * scale * 8.0, (1, 8)).astype(np.float32)
    t["xpos_sq"] = np.tile(sin2 * scale * sgn2 * 8.0, (1, 8)).astype(np.float32)
    t["xpos_ck"] = np.tile(cos2 / scale, (1, 8)).astype(np.float32)
    t["xpos_sk"] = np.tile(sin2 * sgn2 / scale, (1, 8)).astype(np.float32)

    NEG = -1.0e30
    n = S
    w = n // WIN
    bt = np.arange(n).reshape(w, WIN)
    btp = np.pad(bt, ((1, 0), (0, 0)), constant_values=-1)
    btk = np.concatenate([btp[:w], btp[1:]], axis=1)
    m_list = []
    for wi in range(w):
        i = bt[wi][:, None]
        j = btk[wi][None, :]
        mask = (i < j) | (i > j + WIN) | (j == -1)
        m_list.append(np.where(mask, NEG, 0.0).astype(np.float32))
    assert np.array_equal(m_list[1], m_list[2])
    t["maskB0"] = m_list[0][:, 512:].astype(BF)
    t["maskB"] = m_list[1].astype(BF)
    t["maskB0T"] = np.ascontiguousarray(t["maskB0"].T)
    t["maskBT"] = np.ascontiguousarray(t["maskB"].T)

    H = 16
    slopes = 2.0 ** (-8.0 * np.arange(1, H + 1, dtype=np.float64) / H)
    j = np.arange(S, dtype=np.float64)
    t["C_Kb"] = np.stack([j, np.ones_like(j)], 0).astype(np.float32)
    Qb = np.zeros((2, H, S))
    for h in range(H):
        Qb[0, h, :] = slopes[h]
        Qb[1, h, :] = -slopes[h] * j
    t["C_Qb"] = Qb.reshape(2, H * S).astype(np.float32)
    CT = np.zeros((16, 128, 512), dtype=np.float32)
    for kt in range(16):
        qc = kt // 4
        jj = kt * 128 + np.arange(128)[:, None]
        ii = qc * 512 + np.arange(512)[None, :]
        CT[kt] = np.where(jj > ii, NEG, 0.0)
    t["C_CT"] = CT
    return t


_T = _tables()


def emit_ln(P, sb, out, in_, cols, gain=None, bias_t=None):
    st = sb["stats"]
    scr = sb["lnscr"]
    P.act(scr[:, 0:cols], in_, AF.Square, accum_out=st[:, 1:2])
    P.v("reduce_sum", st[:, 0:1], in_, axis=AX.X)
    P.v("tensor_scalar_mul", st[:, 2:3], st[:, 0:1], 1.0 / cols)
    P.v("tensor_mul", st[:, 3:4], st[:, 2:3], st[:, 2:3])
    P.v("tensor_scalar", st[:, 4:5], st[:, 1:2], 1.0 / cols, None, ALU.mult)
    P.v("tensor_sub", st[:, 4:5], st[:, 4:5], st[:, 3:4])
    P.v("tensor_scalar_add", st[:, 4:5], st[:, 4:5], 1e-5)
    P.act(st[:, 5:6], st[:, 4:5], AF.Sqrt)
    P.v("reciprocal", st[:, 6:7], st[:, 5:6])
    P.v("tensor_scalar", out, in_, st[:, 2:3], st[:, 6:7], ALU.subtract, ALU.mult)
    if gain is not None:
        P.v("tensor_mul", out, out, gain)
        P.v("tensor_add", out, out, bias_t)


def emit_transpose_row(P, sb, lnb_bf, lnT_bf, nk):
    for kc in range(nk):
        P.T(sb["psT"][:, :], lnb_bf[:, kc * 128:(kc + 1) * 128], sb["identb"][:, :])
        P.cp(lnT_bf[:, kc * 128:(kc + 1) * 128], sb["psT"][:, :])


def alloc_common(E, nc, sb):
    sb["identb"] = E(nc.sbuf_tensor("identb_s", [128, 128], BF16))
    sb["identf"] = E(nc.sbuf_tensor("identf_s", [128, 128], F32))
    sb["onesb"] = E(nc.sbuf_tensor("onesb_s", [1, 128], BF16))
    sb["onesfr"] = E(nc.sbuf_tensor("onesfr_s", [1, 128], F32))
    sb["onesfc"] = E(nc.sbuf_tensor("onesfc_s", [128, 1], F32))
    sb["stats"] = E(nc.sbuf_tensor("stats", [128, 16], F32))
    sb["lnscr"] = E(nc.sbuf_tensor("lnscr", [128, 1024], F32))
    for i in range(6):
        sb[f"G{i}"] = E(nc.psum_tensor(f"G{i}", [128, 512], F32))
    sb["psT"] = E(nc.psum_tensor("psT", [128, 128], BF16))
    sb["psTf"] = E(nc.psum_tensor("psTf", [128, 128], F32))


def load_common(P, sb, aps):
    for nm in ("identb", "identf", "onesb", "onesfr", "onesfc"):
        P.dma(sb[nm][:, :], aps[nm])


def decl_common_inputs(nc, aps):
    for nm in ("identb", "identf", "onesb", "onesfr", "onesfc"):
        arr = _T[nm]
        dt = BF16 if arr.dtype == BF else F32
        aps[nm] = nc.dram_tensor(nm, list(arr.shape), dt, kind="ExternalInput").ap()


def common_in_map():
    return {nm: _T[nm] for nm in ("identb", "identf", "onesb", "onesfr", "onesfc")}


def build_A():
    nc = bass.Bass("TRN2", target_bir_lowering=False)
    aps = {}
    decl_common_inputs(nc, aps)
    x = nc.dram_tensor("x", [NB * S, D], F32, kind="ExternalInput").ap()
    W = nc.dram_tensor("W", [D, 3 * D], F32, kind="ExternalInput").ap()
    bqkv = nc.dram_tensor("bqkv", [1, 3 * D], F32, kind="ExternalInput").ap()
    WoT = nc.dram_tensor("WoT", [D, D], F32, kind="ExternalInput").ap()
    bo = nc.dram_tensor("bo", [1, D], F32, kind="ExternalInput").ap()
    rc = nc.dram_tensor("ropec", [S, D], F32, kind="ExternalInput").ap()
    rs = nc.dram_tensor("ropes", [S, D], F32, kind="ExternalInput").ap()
    y = nc.dram_tensor("y", [NB * S, D], F32, kind="ExternalOutput").ap()
    qT_h = nc.dram_tensor("qT_h", [D, S], BF16, kind="Internal").ap()
    kT_h = nc.dram_tensor("kT_h", [D, S], BF16, kind="Internal").ap()
    v_h = nc.dram_tensor("v_h", [S, D], BF16, kind="Internal").ap()
    oT_h = nc.dram_tensor("oT_h", [D, S], BF16, kind="Internal").ap()

    P = Prog()
    with ExitStack() as stk:
        E = stk.enter_context
        sb = {}
        alloc_common(E, nc, sb)
        Wsb = E(nc.sbuf_tensor("Wsb", [128, 8 * 3 * D], BF16))
        Wfs = E(nc.sbuf_tensor("Wfs", [128, 3 * D], F32))
        WoTsb = E(nc.sbuf_tensor("WoTsb", [128, 8 * D], BF16))
        bsb = E(nc.sbuf_tensor("bsb", [1, 3 * D], BF16))
        bosb = E(nc.sbuf_tensor("bosb", [1, D], BF16))
        bscr = E(nc.sbuf_tensor("bscr", [1, 3 * D], F32))
        xt = E(nc.sbuf_tensor("xt", [128, D], F32))
        lnt = E(nc.sbuf_tensor("lnt", [128, D], F32))
        lnb = E(nc.sbuf_tensor("lnb", [128, D], BF16))
        lnT = E(nc.sbuf_tensor("lnT", [128, 8 * 128], BF16))
        cost = E(nc.sbuf_tensor("cost", [128, D], F32))
        sint = E(nc.sbuf_tensor("sint", [128, D], F32))
        qs0 = E(nc.sbuf_tensor("qs0", [128, D], F32))
        qa = E(nc.sbuf_tensor("qa", [128, D], F32))
        shq = E(nc.sbuf_tensor("shq", [128, D], F32))
        qf = E(nc.sbuf_tensor("qf", [128, D], F32))
        qfb = E(nc.sbuf_tensor("qfb", [128, D], BF16))
        tstage = E(nc.sbuf_tensor("tstage", [128, 8 * 128], BF16))
        vb = E(nc.sbuf_tensor("vb", [128, D], BF16))
        qsb = E(nc.sbuf_tensor("qsb", [64, S], BF16))
        ksb = E(nc.sbuf_tensor("ksb", [64, S], BF16))
        vg = E(nc.sbuf_tensor("vg", [128, 16 * 65], BF16))
        expb = E(nc.sbuf_tensor("expb", [128, 512], BF16))
        oTsb = E(nc.sbuf_tensor("oTsb", [65, 512], F32))
        r0 = E(nc.sbuf_tensor("r0", [1, 512], F32))
        rb = E(nc.sbuf_tensor("rb", [64, 512], F32))
        oTf = E(nc.sbuf_tensor("oTf", [64, 512], BF16))
        oTl = E(nc.sbuf_tensor("oTl", [128, 8 * 128], BF16))
        yt = E(nc.sbuf_tensor("yt", [128, D], F32))
        sem = E(nc.semaphore("gsem"))

        load_common(P, sb, aps)
        for kc in range(8):
            P.dma(Wfs[:, :], W[kc * 128:(kc + 1) * 128, :])
            P.v("tensor_copy", Wsb[:, kc * 3 * D:(kc + 1) * 3 * D], Wfs[:, :])
        for kc in range(8):
            P.dma(Wfs[:, 0:D], WoT[kc * 128:(kc + 1) * 128, :])
            P.v("tensor_copy", WoTsb[:, kc * D:(kc + 1) * D], Wfs[:, 0:D])
        P.dma(bscr[:, :], bqkv)
        P.v("tensor_copy", bsb[:, :], bscr[:, :])
        P.dma(bscr[:, 0:D], bo)
        P.v("tensor_copy", bosb[:, :], bscr[:, 0:D])

        G = [sb[f"G{i}"] for i in range(6)]
        vgv = vg[:, :].rearrange("p (t c) -> p t c", c=65)
        for b in range(NB):
            r0b = b * S
            for i in range(NT):
                rr = r0b + i * 128
                P.dma(xt[:, :], x[rr:rr + 128, :])
                emit_ln(P, sb, lnt[:, :], xt[:, :], D)
                P.v("tensor_copy", lnb[:, :], lnt[:, :])
                emit_transpose_row(P, sb, lnb, lnT, 8)
                P.dma(cost[:, :], rc[i * 128:(i + 1) * 128, :])
                P.dma(sint[:, :], rs[i * 128:(i + 1) * 128, :])
                for kc in range(8):
                    for n in range(6):
                        P.mm(G[n][:, :], lnT[:, kc * 128:(kc + 1) * 128],
                             Wsb[:, kc * 3 * D + n * 512: kc * 3 * D + (n + 1) * 512],
                             start=(kc == 0), stop=False)
                for n in range(6):
                    P.mm(G[n][:, :], sb["onesb"][:, :], bsb[:, n * 512:(n + 1) * 512],
                         start=False, stop=True)
                for (g0, g1, Tdst) in ((0, 1, qT_h), (2, 3, kT_h)):
                    P.cp(qs0[:, 0:512], G[g0][:, :])
                    P.cp(qs0[:, 512:1024], G[g1][:, :])
                    P.v("tensor_mul", qa[:, :], qs0[:, :], cost[:, :])
                    P.v("tensor_copy", shq[:, 0:D:2], qs0[:, 1:D:2])
                    P.v("tensor_copy", shq[:, 1:D:2], qs0[:, 0:D:2])
                    P.v("tensor_mul", shq[:, :], shq[:, :], sint[:, :])
                    P.v("tensor_add", qf[:, :], qa[:, :], shq[:, :])
                    P.v("tensor_copy", qfb[:, :], qf[:, :])
                    for hp in range(8):
                        P.T(sb["psT"][:, :], qfb[:, hp * 128:(hp + 1) * 128],
                            sb["identb"][:, :])
                        P.cp(tstage[:, hp * 128:(hp + 1) * 128], sb["psT"][:, :])
                    P.dma(Tdst[:, i * 128:(i + 1) * 128]
                          .rearrange("(a p) c -> p a c", p=128),
                          tstage[:, :].rearrange("p (a c) -> p a c", c=128))
                P.cp(vb[:, 0:512], G[4][:, :])
                P.cp(vb[:, 512:1024], G[5][:, :])
                P.dma(v_h[i * 128:(i + 1) * 128, :], vb[:, :])
            for h in range(16):
                P.dma(qsb[:, :], qT_h[h * 64:(h + 1) * 64, :])
                P.dma(ksb[:, :], kT_h[h * 64:(h + 1) * 64, :])
                P.dma(vgv[:, :, 0:64],
                      v_h[:, h * 64:(h + 1) * 64].rearrange("(t p) c -> p t c", p=128))
                P.v("memset", vgv[:, :, 64:65], 1.0)
                for qc in range(4):
                    for kt in range(16):
                        P.mm(G[0][:, :], ksb[:, kt * 128:(kt + 1) * 128],
                             qsb[:, qc * 512:(qc + 1) * 512], start=True, stop=True)
                        P.act(expb[:, :], G[0][:, :], AF.Exp)
                        P.mm(G[2 + qc][0:65, :], vgv[:, kt, :],
                             expb[:, :], start=(kt == 0), stop=(kt == 15))
                for qc in range(4):
                    P.cp(oTsb[:, :], G[2 + qc][0:65, :])
                    P.dma(r0[:, :], oTsb[64:65, :])
                    P.v("reciprocal", r0[:, :], r0[:, :])
                    P.mm(G[0][0:64, :], sb["onesfr"][:, 0:64], r0[:, :],
                         start=True, stop=True)
                    P.cp(rb[:, :], G[0][0:64, :])
                    P.v("tensor_mul", oTf[:, :], oTsb[0:64, :], rb[:, :])
                    P.dma(oT_h[h * 64:(h + 1) * 64, qc * 512:(qc + 1) * 512],
                          oTf[:, :])
            for i in range(NT):
                rr = r0b + i * 128
                P.dma(oTl[:, :].rearrange("p (a c) -> p a c", c=128),
                      oT_h[:, i * 128:(i + 1) * 128].rearrange("(a p) c -> p a c", p=128))
                for kc in range(8):
                    for n in range(2):
                        P.mm(G[n][:, :], oTl[:, kc * 128:(kc + 1) * 128],
                             WoTsb[:, kc * D + n * 512: kc * D + (n + 1) * 512],
                             start=(kc == 0), stop=False)
                for n in range(2):
                    P.mm(G[n][:, :], sb["onesb"][:, :], bosb[:, n * 512:(n + 1) * 512],
                         start=False, stop=True)
                P.dma(xt[:, :], x[rr:rr + 128, :])
                for n in range(2):
                    P.v("tensor_add", yt[:, n * 512:(n + 1) * 512], G[n][:, :],
                        xt[:, n * 512:(n + 1) * 512])
                P.dma(y[rr:rr + 128, :], yt[:, :])
        finalize(nc, P, sem)
    return nc


def map_A(xc, inp):
    m = common_in_map()
    m["x"] = xc
    m["W"] = np.ascontiguousarray(
        np.concatenate([np.asarray(inp["L0_attn_Wq"]).T, np.asarray(inp["L0_attn_Wk"]).T,
                        np.asarray(inp["L0_attn_Wv"]).T], axis=1))
    m["bqkv"] = np.ascontiguousarray(
        np.concatenate([inp["L0_attn_bq"], inp["L0_attn_bk"],
                        inp["L0_attn_bv"]])[None, :])
    m["WoT"] = np.ascontiguousarray(np.asarray(inp["L0_attn_Wo"]).T)
    m["bo"] = np.ascontiguousarray(np.asarray(inp["L0_attn_bo"])[None, :])
    m["ropec"] = _T["ropeA_cos"]
    m["ropes"] = _T["ropeA_sin"]
    return m


def build_D():
    nc = bass.Bass("TRN2", target_bir_lowering=False)
    aps = {}
    decl_common_inputs(nc, aps)
    x = nc.dram_tensor("x", [NB * S, D], F32, kind="ExternalInput").ap()
    Win = nc.dram_tensor("Win", [D, 64], F32, kind="ExternalInput").ap()
    bin_ = nc.dram_tensor("bin", [1, 64], F32, kind="ExternalInput").ap()
    W1r = nc.dram_tensor("W1r", [1, 1024], F32, kind="ExternalInput").ap()
    b1r = nc.dram_tensor("b1r", [1, 1024], F32, kind="ExternalInput").ap()
    W2r = nc.dram_tensor("W2r", [1, 1024], F32, kind="ExternalInput").ap()
    b2r = nc.dram_tensor("b2r", [1, 64], F32, kind="ExternalInput").ap()
    Wout = nc.dram_tensor("Wout", [64, D], F32, kind="ExternalInput").ap()
    bout = nc.dram_tensor("bout", [1, D], F32, kind="ExternalInput").ap()
    y = nc.dram_tensor("y", [NB * S, D], F32, kind="ExternalOutput").ap()

    P = Prog()
    with ExitStack() as stk:
        E = stk.enter_context
        sb = {}
        alloc_common(E, nc, sb)
        Wfs = E(nc.sbuf_tensor("Wfs", [128, 1024], F32))
        Winb = E(nc.sbuf_tensor("Winb", [128, 8 * 64], BF16))
        Woutb = E(nc.sbuf_tensor("Woutb", [64, D], BF16))
        binb = E(nc.sbuf_tensor("binb", [1, 64], BF16))
        boutb = E(nc.sbuf_tensor("boutb", [1, D], BF16))
        rowf = E(nc.sbuf_tensor("rowf", [1, 1024], F32))
        W1b = E(nc.sbuf_tensor("W1b", [128, 1024], F32))
        b1b = E(nc.sbuf_tensor("b1b", [128, 1024], F32))
        W2b = E(nc.sbuf_tensor("W2b", [128, 1024], F32))
        b2b = E(nc.sbuf_tensor("b2b", [128, 64], F32))
        xt = E(nc.sbuf_tensor("xt", [128, D], F32))
        lnt = E(nc.sbuf_tensor("lnt", [128, D], F32))
        lnb = E(nc.sbuf_tensor("lnb", [128, D], BF16))
        lnT = E(nc.sbuf_tensor("lnT", [128, 8 * 128], BF16))
        st_ = E(nc.sbuf_tensor("st_", [128, 64], F32))
        htmp = E(nc.sbuf_tensor("htmp", [128, 64], F32))
        gacc = E(nc.sbuf_tensor("gacc", [128, 64], F32))
        gb = E(nc.sbuf_tensor("gb", [128, 64], BF16))
        gT = E(nc.sbuf_tensor("gT", [64, 128], BF16))
        yt = E(nc.sbuf_tensor("yt", [128, D], F32))
        sem = E(nc.semaphore("gsem"))

        load_common(P, sb, aps)
        G = [sb[f"G{i}"] for i in range(6)]
        for kc in range(8):
            P.dma(Wfs[:, 0:64], Win[kc * 128:(kc + 1) * 128, :])
            P.v("tensor_copy", Winb[:, kc * 64:(kc + 1) * 64], Wfs[:, 0:64])
        P.dma(Wfs[0:64, 0:D], Wout)
        P.v("tensor_copy", Woutb[:, :], Wfs[0:64, 0:D])
        P.dma(rowf[:, 0:64], bin_)
        P.v("tensor_copy", binb[:, :], rowf[:, 0:64])
        P.dma(rowf[:, 0:D], bout)
        P.v("tensor_copy", boutb[:, :], rowf[:, 0:D])
        for src, dst in ((W1r, W1b), (b1r, b1b), (W2r, W2b)):
            P.dma(rowf[:, :], src)
            for n in range(2):
                P.mm(G[0][:, :], sb["onesfr"][:, :], rowf[:, n * 512:(n + 1) * 512],
                     start=True, stop=True)
                P.cp(dst[:, n * 512:(n + 1) * 512], G[0][:, :])
        P.dma(rowf[:, 0:64], b2r)
        P.mm(G[0][:, 0:64], sb["onesfr"][:, :], rowf[:, 0:64], start=True, stop=True)
        P.cp(b2b[:, :], G[0][:, 0:64])

        for b in range(NB):
            for i in range(NT):
                rr = b * S + i * 128
                P.dma(xt[:, :], x[rr:rr + 128, :])
                emit_ln(P, sb, lnt[:, :], xt[:, :], D)
                P.v("tensor_copy", lnb[:, :], lnt[:, :])
                emit_transpose_row(P, sb, lnb, lnT, 8)
                for kc in range(8):
                    P.mm(G[0][:, 0:64], lnT[:, kc * 128:(kc + 1) * 128],
                         Winb[:, kc * 64:(kc + 1) * 64], start=(kc == 0), stop=False)
                P.mm(G[0][:, 0:64], sb["onesb"][:, :], binb[:, :],
                     start=False, stop=True)
                P.cp(st_[:, :], G[0][:, 0:64])
                for ii in range(16):
                    P.v("tensor_mul", htmp[:, :], st_[:, :], W1b[:, ii * 64:(ii + 1) * 64])
                    P.v("tensor_add", htmp[:, :], htmp[:, :], b1b[:, ii * 64:(ii + 1) * 64])
                    P.v("tensor_scalar_max", htmp[:, :], htmp[:, :], 0.0)
                    P.v("tensor_mul", htmp[:, :], htmp[:, :], W2b[:, ii * 64:(ii + 1) * 64])
                    if ii == 0:
                        P.v("tensor_copy", gacc[:, :], htmp[:, :])
                    else:
                        P.v("tensor_add", gacc[:, :], gacc[:, :], htmp[:, :])
                P.v("tensor_add", gacc[:, :], gacc[:, :], b2b[:, :])
                P.v("tensor_copy", gb[:, :], gacc[:, :])
                P.T(sb["psT"][0:64, :], gb[:, :], sb["identb"][:, :])
                P.cp(gT[:, :], sb["psT"][0:64, :])
                for n in range(2):
                    P.mm(G[1 + n][:, :], gT[:, :], Woutb[:, n * 512:(n + 1) * 512],
                         start=True, stop=False)
                    P.mm(G[1 + n][:, :], sb["onesb"][:, :], boutb[:, n * 512:(n + 1) * 512],
                         start=False, stop=True)
                    P.v("tensor_add", yt[:, n * 512:(n + 1) * 512], G[1 + n][:, :],
                        xt[:, n * 512:(n + 1) * 512])
                P.dma(y[rr:rr + 128, :], yt[:, :])
        finalize(nc, P, sem)
    return nc


def map_D(xc, inp):
    m = common_in_map()
    m["x"] = xc
    m["Win"] = np.ascontiguousarray(inp["L1_ffn_Win"])
    m["bin"] = np.ascontiguousarray(np.asarray(inp["L1_ffn_bin_"])[None, :])
    m["W1r"] = np.ascontiguousarray(np.asarray(inp["L1_ffn_W1"]).T.reshape(1, 1024))
    m["b1r"] = np.ascontiguousarray(np.asarray(inp["L1_ffn_b1"]).T.reshape(1, 1024))
    m["W2r"] = np.ascontiguousarray(np.asarray(inp["L1_ffn_W2"]).T.reshape(1, 1024))
    m["b2r"] = np.ascontiguousarray(np.asarray(inp["L1_ffn_b2"])[None, :])
    m["Wout"] = np.ascontiguousarray(inp["L1_ffn_Wout"])
    m["bout"] = np.ascontiguousarray(np.asarray(inp["L1_ffn_bout"])[None, :])
    return m


def build_C():
    nc = bass.Bass("TRN2", target_bir_lowering=False)
    aps = {}
    decl_common_inputs(nc, aps)
    x = nc.dram_tensor("x", [NB * S, D], F32, kind="ExternalInput").ap()
    W = nc.dram_tensor("W", [D, 5 * D], F32, kind="ExternalInput").ap()
    Wo = nc.dram_tensor("Wo", [D, D], F32, kind="ExternalInput").ap()
    lvec = nc.dram_tensor("lvec", [64, 4], F32, kind="ExternalInput").ap()
    lnwT = nc.dram_tensor("lnwT", [64, 16], F32, kind="ExternalInput").ap()
    lnbT = nc.dram_tensor("lnbT", [64, 16], F32, kind="ExternalInput").ap()
    Kb = nc.dram_tensor("Kb", [2, S], F32, kind="ExternalInput").ap()
    Qb = nc.dram_tensor("Qb", [2, 16 * S], F32, kind="ExternalInput").ap()
    CT = nc.dram_tensor("CT", [16, 128, 512], F32, kind="ExternalInput").ap()
    y = nc.dram_tensor("y", [NB * S, D], F32, kind="ExternalOutput").ap()
    pT_h = nc.dram_tensor("pT_h", [4 * D, S], BF16, kind="Internal").ap()
    v_h = nc.dram_tensor("v_h", [S, D], BF16, kind="Internal").ap()
    oT_h = nc.dram_tensor("oT_h", [D, S], BF16, kind="Internal").ap()

    P = Prog()
    with ExitStack() as stk:
        E = stk.enter_context
        sb = {}
        alloc_common(E, nc, sb)
        Wsb = E(nc.sbuf_tensor("Wsb", [128, 8 * 5120], BF16))
        Wfs = E(nc.sbuf_tensor("Wfs", [128, 5120], F32))
        WoSb = E(nc.sbuf_tensor("WoSb", [128, 8 * D], BF16))
        Kbs = E(nc.sbuf_tensor("Kbs", [2, S], F32))
        Qhs = E(nc.sbuf_tensor("Qhs", [2, S], F32))
        CTs = E(nc.sbuf_tensor("CTs", [128, 512], F32))
        lnwS = E(nc.sbuf_tensor("lnwS", [64, 16], F32))
        lnbS = E(nc.sbuf_tensor("lnbS", [64, 16], F32))
        lvs = E(nc.sbuf_tensor("lvs", [64, 4], F32))
        neglam = E(nc.sbuf_tensor("neglam", [1, 4], F32))
        xt = E(nc.sbuf_tensor("xt", [128, D], F32))
        lnt = E(nc.sbuf_tensor("lnt", [128, D], F32))
        lnb = E(nc.sbuf_tensor("lnb", [128, D], BF16))
        lnT = E(nc.sbuf_tensor("lnT", [128, 8 * 128], BF16))
        pstage = E(nc.sbuf_tensor("pstage", [128, 512], BF16))
        tstage = E(nc.sbuf_tensor("tstage", [128, 4 * 128], BF16))
        vb = E(nc.sbuf_tensor("vb", [128, D], BF16))
        q0sb = E(nc.sbuf_tensor("q0sb", [64, S], BF16))
        q1sb = E(nc.sbuf_tensor("q1sb", [64, S], BF16))
        k0sb = E(nc.sbuf_tensor("k0sb", [64, S], BF16))
        k1sb = E(nc.sbuf_tensor("k1sb", [64, S], BF16))
        vg = E(nc.sbuf_tensor("vg", [128, 16 * 65], BF16))
        ssb = E(nc.sbuf_tensor("ssb", [128, 512], F32))
        expb = E(nc.sbuf_tensor("expb", [128, 512], BF16))
        oT0 = E(nc.sbuf_tensor("oT0", [65, 512], F32))
        oT1 = E(nc.sbuf_tensor("oT1", [65, 512], F32))
        r0 = E(nc.sbuf_tensor("r0", [1, 512], F32))
        r1 = E(nc.sbuf_tensor("r1", [1, 512], F32))
        rb0 = E(nc.sbuf_tensor("rb0", [64, 512], F32))
        rb1 = E(nc.sbuf_tensor("rb1", [64, 512], F32))
        hsb = E(nc.sbuf_tensor("hsb", [64, 512], F32))
        hscr = E(nc.sbuf_tensor("hscr", [64, 512], F32))
        rowa = E(nc.sbuf_tensor("rowa", [1, 512], F32))
        rowb = E(nc.sbuf_tensor("rowb", [1, 512], F32))
        mub = E(nc.sbuf_tensor("mub", [64, 512], F32))
        oTf = E(nc.sbuf_tensor("oTf", [64, 512], BF16))
        oTl = E(nc.sbuf_tensor("oTl", [128, 8 * 128], BF16))
        yt = E(nc.sbuf_tensor("yt", [128, D], F32))
        sem = E(nc.semaphore("gsem"))

        load_common(P, sb, aps)
        for kc in range(8):
            P.dma(Wfs[:, :], W[kc * 128:(kc + 1) * 128, :])
            P.v("tensor_copy", Wsb[:, kc * 5120:(kc + 1) * 5120], Wfs[:, :])
        for kc in range(8):
            P.dma(Wfs[:, 0:D], Wo[kc * 128:(kc + 1) * 128, :])
            P.v("tensor_copy", WoSb[:, kc * D:(kc + 1) * D], Wfs[:, 0:D])
        P.dma(Kbs[:, :], Kb)
        P.dma(lnwS[:, :], lnwT)
        P.dma(lnbS[:, :], lnbT)
        P.dma(lvs[:, :], lvec)
        P.mm(sb["G0"][0:1, 0:1], lvs[:, 0:1], lvs[:, 1:2], start=True, stop=True)
        P.mm(sb["G0"][0:1, 1:2], lvs[:, 2:3], lvs[:, 3:4], start=True, stop=True)
        P.cp(neglam[:, 0:2], sb["G0"][0:1, 0:2])
        P.act(neglam[:, 2:4], neglam[:, 0:2], AF.Exp)
        P.v("tensor_add", neglam[:, 0:1], neglam[:, 2:3], neglam[:, 3:4])
        P.v("tensor_scalar_add", neglam[:, 0:1], neglam[:, 0:1], -1.6)

        G = [sb[f"G{i}"] for i in range(6)]
        vgv = vg[:, :].rearrange("p (t c) -> p t c", c=65)
        SC = 0.125 ** 0.5
        for b in range(NB):
            r0b = b * S
            for i in range(NT):
                rr = r0b + i * 128
                P.dma(xt[:, :], x[rr:rr + 128, :])
                emit_ln(P, sb, lnt[:, :], xt[:, :], D)
                P.v("tensor_copy", lnb[:, :], lnt[:, :])
                emit_transpose_row(P, sb, lnb, lnT, 8)
                for half in range(2):
                    for kc in range(8):
                        for n in range(5):
                            nn = half * 5 + n
                            P.mm(G[n][:, :], lnT[:, kc * 128:(kc + 1) * 128],
                                 Wsb[:, kc * 5120 + nn * 512: kc * 5120 + (nn + 1) * 512],
                                 start=(kc == 0), stop=(kc == 7))
                    for n in range(5):
                        nn = half * 5 + n
                        proj = nn // 2
                        if proj < 4:
                            P.v("tensor_scalar", pstage[:, :], G[n][:, :], SC, None,
                                ALU.mult)
                            for hp in range(4):
                                P.T(sb["psT"][:, :], pstage[:, hp * 128:(hp + 1) * 128],
                                    sb["identb"][:, :])
                                P.cp(tstage[:, hp * 128:(hp + 1) * 128], sb["psT"][:, :])
                            ro = proj * D + (nn % 2) * 512
                            P.dma(pT_h[ro:ro + 512, i * 128:(i + 1) * 128]
                                  .rearrange("(a p) c -> p a c", p=128),
                                  tstage[:, :].rearrange("p (a c) -> p a c", c=128))
                        else:
                            P.v("tensor_copy", vb[:, (nn % 2) * 512:(nn % 2) * 512 + 512],
                                G[n][:, :])
                P.dma(v_h[i * 128:(i + 1) * 128, :], vb[:, :])
            for h in range(16):
                P.dma(q0sb[:, :], pT_h[0 * D + h * 64: 0 * D + (h + 1) * 64, :])
                P.dma(q1sb[:, :], pT_h[1 * D + h * 64: 1 * D + (h + 1) * 64, :])
                P.dma(k0sb[:, :], pT_h[2 * D + h * 64: 2 * D + (h + 1) * 64, :])
                P.dma(k1sb[:, :], pT_h[3 * D + h * 64: 3 * D + (h + 1) * 64, :])
                P.dma(Qhs[:, :], Qb[:, h * S:(h + 1) * S])
                P.dma(vgv[:, :, 0:64],
                      v_h[:, h * 64:(h + 1) * 64].rearrange("(t p) c -> p t c", p=128))
                P.v("memset", vgv[:, :, 64:65], 1.0)
                for qc in range(4):
                    nkt = 4 * (qc + 1)
                    for kt in range(nkt):
                        diag = (kt // 4 == qc)
                        if diag:
                            P.dma(CTs[:, :], CT[kt, :, :])
                        for (qq, kk, gi) in ((q0sb, k0sb, 1), (q1sb, k1sb, 2)):
                            P.mm(G[0][:, :], kk[:, kt * 128:(kt + 1) * 128],
                                 qq[:, qc * 512:(qc + 1) * 512], start=True, stop=False)
                            P.mm(G[0][:, :], Kbs[:, kt * 128:(kt + 1) * 128],
                                 Qhs[:, qc * 512:(qc + 1) * 512], start=False, stop=True)
                            if diag:
                                P.v("tensor_add", ssb[:, :], G[0][:, :], CTs[:, :])
                                P.act(expb[:, :], ssb[:, :], AF.Exp)
                            else:
                                P.act(expb[:, :], G[0][:, :], AF.Exp)
                            P.mm(G[gi][0:65, :], vgv[:, kt, :], expb[:, :],
                                 start=(kt == 0), stop=(kt == nkt - 1))
                    P.cp(oT0[:, :], G[1][0:65, :])
                    P.cp(oT1[:, :], G[2][0:65, :])
                    P.dma(r0[:, :], oT0[64:65, :])
                    P.dma(r1[:, :], oT1[64:65, :])
                    P.v("reciprocal", r0[:, :], r0[:, :])
                    P.v("reciprocal", r1[:, :], r1[:, :])
                    P.v("tensor_scalar_mul", r1[:, :], r1[:, :], neglam[:, 0:1])
                    P.mm(G[0][0:64, :], sb["onesfr"][:, 0:64], r0[:, :],
                         start=True, stop=True)
                    P.cp(rb0[:, :], G[0][0:64, :])
                    P.mm(G[0][0:64, :], sb["onesfr"][:, 0:64], r1[:, :],
                         start=True, stop=True)
                    P.cp(rb1[:, :], G[0][0:64, :])
                    P.v("tensor_mul", hsb[:, :], oT0[0:64, :], rb0[:, :])
                    P.v("tensor_mul", hscr[:, :], oT1[0:64, :], rb1[:, :])
                    P.v("tensor_add", hsb[:, :], hsb[:, :], hscr[:, :])
                    P.mm(G[0][0:1, :], sb["onesfc"][0:64, :], hsb[:, :],
                         start=True, stop=True)
                    P.act(hscr[:, :], hsb[:, :], AF.Square)
                    P.mm(G[3][0:1, :], sb["onesfc"][0:64, :], hscr[:, :],
                         start=True, stop=True)
                    P.cp(rowa[:, :], G[0][0:1, :])
                    P.cp(rowb[:, :], G[3][0:1, :])
                    P.v("tensor_scalar_mul", rowa[:, :], rowa[:, :], 1.0 / 64)
                    P.v("tensor_scalar_mul", rowb[:, :], rowb[:, :], 1.0 / 64)
                    P.v("tensor_mul", r0[:, :], rowa[:, :], rowa[:, :])
                    P.v("tensor_sub", rowb[:, :], rowb[:, :], r0[:, :])
                    P.v("tensor_scalar_add", rowb[:, :], rowb[:, :], 1e-5)
                    P.act(r0[:, :], rowb[:, :], AF.Ln)
                    P.v("tensor_scalar_mul", r0[:, :], r0[:, :], -0.5)
                    P.act(rowb[:, :], r0[:, :], AF.Exp)
                    P.mm(G[0][0:64, :], sb["onesfr"][:, 0:64], rowa[:, :],
                         start=True, stop=True)
                    P.cp(mub[:, :], G[0][0:64, :])
                    P.mm(G[0][0:64, :], sb["onesfr"][:, 0:64], rowb[:, :],
                         start=True, stop=True)
                    P.cp(rb0[:, :], G[0][0:64, :])
                    P.v("tensor_sub", hsb[:, :], hsb[:, :], mub[:, :])
                    P.v("tensor_mul", hsb[:, :], hsb[:, :], rb0[:, :])
                    P.v("tensor_scalar", hsb[:, :], hsb[:, :], lnwS[:, h:h + 1],
                        lnbS[:, h:h + 1], ALU.mult, ALU.add)
                    P.v("tensor_copy", oTf[:, :], hsb[:, :])
                    P.dma(oT_h[h * 64:(h + 1) * 64, qc * 512:(qc + 1) * 512], oTf[:, :])
            for i in range(NT):
                rr = r0b + i * 128
                P.dma(oTl[:, :].rearrange("p (a c) -> p a c", c=128),
                      oT_h[:, i * 128:(i + 1) * 128].rearrange("(a p) c -> p a c", p=128))
                for kc in range(8):
                    for n in range(2):
                        P.mm(G[n][:, :], oTl[:, kc * 128:(kc + 1) * 128],
                             WoSb[:, kc * D + n * 512: kc * D + (n + 1) * 512],
                             start=(kc == 0), stop=(kc == 7))
                P.dma(xt[:, :], x[rr:rr + 128, :])
                for n in range(2):
                    P.v("tensor_add", yt[:, n * 512:(n + 1) * 512], G[n][:, :],
                        xt[:, n * 512:(n + 1) * 512])
                P.dma(y[rr:rr + 128, :], yt[:, :])
        finalize(nc, P, sem)
    return nc


def map_C(xc, inp):
    m = common_in_map()
    m["x"] = xc
    m["W"] = np.ascontiguousarray(
        np.concatenate([inp["L1_attn_wq0"], inp["L1_attn_wq1"],
                        inp["L1_attn_wk0"], inp["L1_attn_wk1"],
                        inp["L1_attn_wv"]], axis=1))
    m["Wo"] = np.ascontiguousarray(inp["L1_attn_wo"])
    m["lvec"] = np.ascontiguousarray(
        np.stack([inp["L1_attn_l0"], inp["L1_attn_l1"],
                  inp["L1_attn_l2"], inp["L1_attn_l3"]], axis=1))
    m["lnwT"] = np.ascontiguousarray(np.asarray(inp["L1_attn_ln_w"]).T)
    m["lnbT"] = np.ascontiguousarray(np.asarray(inp["L1_attn_ln_b"]).T)
    m["Kb"] = _T["C_Kb"]
    m["Qb"] = _T["C_Qb"]
    m["CT"] = _T["C_CT"]
    return m


def build_B():
    nc = bass.Bass("TRN2", target_bir_lowering=False)
    aps = {}
    decl_common_inputs(nc, aps)
    x = nc.dram_tensor("x", [NB * S, D], F32, kind="ExternalInput").ap()
    wqkv = nc.dram_tensor("wqkv", [2, D, 1536], F32, kind="ExternalInput").ap()
    qsc = nc.dram_tensor("qsc", [2, 1, 512], F32, kind="ExternalInput").ap()
    ksc = nc.dram_tensor("ksc", [2, 1, 512], F32, kind="ExternalInput").ap()
    wo = nc.dram_tensor("wo", [2, 512, D], F32, kind="ExternalInput").ap()
    lng = nc.dram_tensor("lng", [2, 1, D], F32, kind="ExternalInput").ap()
    lnbi = nc.dram_tensor("lnbi", [2, 1, D], F32, kind="ExternalInput").ap()
    w1 = nc.dram_tensor("w1", [2, D, 5460], F32, kind="ExternalInput").ap()
    b1 = nc.dram_tensor("b1", [2, 1, 5460], F32, kind="ExternalInput").ap()
    w2 = nc.dram_tensor("w2", [2, 2730, D], F32, kind="ExternalInput").ap()
    b2 = nc.dram_tensor("b2", [2, 1, D], F32, kind="ExternalInput").ap()
    xc_ = nc.dram_tensor("xc", [S, 512], F32, kind="ExternalInput").ap()
    xs_ = nc.dram_tensor("xs", [S, 512], F32, kind="ExternalInput").ap()
    xck = nc.dram_tensor("xck", [S, 512], F32, kind="ExternalInput").ap()
    xsk = nc.dram_tensor("xsk", [S, 512], F32, kind="ExternalInput").ap()
    mB0 = nc.dram_tensor("mB0", [512, 512], BF16, kind="ExternalInput").ap()
    mB = nc.dram_tensor("mB", [512, 1024], BF16, kind="ExternalInput").ap()
    mB0T = nc.dram_tensor("mB0T", [512, 512], BF16, kind="ExternalInput").ap()
    mBT = nc.dram_tensor("mBT", [1024, 512], BF16, kind="ExternalInput").ap()
    y = nc.dram_tensor("y", [NB * S, D], F32, kind="ExternalOutput").ap()
    z_h = nc.dram_tensor("z_h", [NB * S, D], F32, kind="Internal").ap()
    z2_h = nc.dram_tensor("z2_h", [NB * S, D], F32, kind="Internal").ap()
    h_h = nc.dram_tensor("h_h", [NB * S, D], BF16, kind="Internal").ap()
    qT_h = nc.dram_tensor("qT_h", [512, NB * S], BF16, kind="Internal").ap()
    kT_h = nc.dram_tensor("kT_h", [512, NB * S], BF16, kind="Internal").ap()
    v_h = nc.dram_tensor("v_h", [NB * S, 512], BF16, kind="Internal").ap()
    oT_h = nc.dram_tensor("oT_h", [512, NB * S], BF16, kind="Internal").ap()
    gT_h = nc.dram_tensor("gT_h", [2730, NB * S], BF16, kind="Internal").ap()

    P = Prog()
    with ExitStack() as stk:
        E = stk.enter_context
        sb = {}
        alloc_common(E, nc, sb)
        wA = E(nc.sbuf_tensor("wA", [128, 8 * 5460], BF16))
        Wfs = E(nc.sbuf_tensor("Wfs", [128, 1536], F32))
        b1S = E(nc.sbuf_tensor("b1S", [1, 5460], BF16))
        b2S = E(nc.sbuf_tensor("b2S", [1, D], BF16))
        rowf = E(nc.sbuf_tensor("rowf", [1, 1536], F32))
        qscB = E(nc.sbuf_tensor("qscB", [128, 512], F32))
        kscB = E(nc.sbuf_tensor("kscB", [128, 512], F32))
        lngB = E(nc.sbuf_tensor("lngB", [128, D], F32))
        lnbB = E(nc.sbuf_tensor("lnbB", [128, D], F32))
        xt = E(nc.sbuf_tensor("xt", [128, D], F32))
        ht = E(nc.sbuf_tensor("ht", [128, D], F32))
        lnt = E(nc.sbuf_tensor("lnt", [128, D], F32))
        lnb = E(nc.sbuf_tensor("lnb", [128, D], BF16))
        lnT = E(nc.sbuf_tensor("lnT", [128, 8 * 128], BF16))
        cost = E(nc.sbuf_tensor("cost", [128, 512], F32))
        sint = E(nc.sbuf_tensor("sint", [128, 512], F32))
        qs0 = E(nc.sbuf_tensor("qs0", [128, 512], F32))
        nrm = E(nc.sbuf_tensor("nrm", [128, 16], F32))
        qn = E(nc.sbuf_tensor("qn", [128, 512], F32))
        qa = E(nc.sbuf_tensor("qa", [128, 512], F32))
        shq = E(nc.sbuf_tensor("shq", [128, 512], F32))
        qf = E(nc.sbuf_tensor("qf", [128, 512], F32))
        qfb = E(nc.sbuf_tensor("qfb", [128, 512], BF16))
        tstage = E(nc.sbuf_tensor("tstage", [128, 4 * 128], BF16))
        vb = E(nc.sbuf_tensor("vb", [128, 512], BF16))
        maskP = E(nc.sbuf_tensor("maskP", [128, 4 * 1024], BF16))
        maskT = E(nc.sbuf_tensor("maskT", [128, 8 * 512], BF16))
        qsb = E(nc.sbuf_tensor("qsb", [64, 512], BF16))
        ksb = E(nc.sbuf_tensor("ksb", [64, 1024], BF16))
        qhat = E(nc.sbuf_tensor("qhat", [65, 512], BF16))
        khat = E(nc.sbuf_tensor("khat", [65, 1024], BF16))
        vg = E(nc.sbuf_tensor("vg", [128, 8 * 65], BF16))
        ssb = sb["lnscr"]
        nmx = E(nc.sbuf_tensor("nmx", [128, 1], F32))
        nmr = E(nc.sbuf_tensor("nmr", [1, 128], F32))
        nmrb = E(nc.sbuf_tensor("nmrb", [1, 128], BF16))
        expb = E(nc.sbuf_tensor("expb", [128, 512], BF16))
        oTsb = E(nc.sbuf_tensor("oTsb", [65, 512], F32))
        r0 = E(nc.sbuf_tensor("r0", [1, 512], F32))
        rb = E(nc.sbuf_tensor("rb", [64, 512], F32))
        oTf = E(nc.sbuf_tensor("oTf", [64, 512], BF16))
        oTl = E(nc.sbuf_tensor("oTl", [128, 4 * 128], BF16))
        hb = E(nc.sbuf_tensor("hb", [128, D], BF16))
        gel = E(nc.sbuf_tensor("gel", [128, 512], F32))
        prodb = E(nc.sbuf_tensor("prodb", [128, 512], BF16))
        gTl = E(nc.sbuf_tensor("gTl", [128, 21 * 128], BF16))
        gTll = E(nc.sbuf_tensor("gTll", [42, 128], BF16))
        yt = lnt
        sem = E(nc.semaphore("gsem"))

        load_common(P, sb, aps)
        G = [sb[f"G{i}"] for i in range(6)]
        vgv = vg[:, :].rearrange("p (t c) -> p t c", c=65)

        def conv_load(dst_bf, src_ap, rows, cols):
            c0 = 0
            while c0 < cols:
                cw = min(1536, cols - c0)
                P.dma(Wfs[0:rows, 0:cw], src_ap[:, c0:c0 + cw] if rows == 128
                      else src_ap[0:rows, c0:c0 + cw])
                P.v("tensor_copy", dst_bf[0:rows, c0:c0 + cw], Wfs[0:rows, 0:cw])
                c0 += cw

        def bcast_row(src_l_ap, dst, w_):
            c0 = 0
            while c0 < w_:
                cw = min(512, w_ - c0)
                P.dma(rowf[:, 0:cw], src_l_ap[:, c0:c0 + cw])
                P.mm(G[0][:, 0:cw], sb["onesfr"][:, :], rowf[:, 0:cw],
                     start=True, stop=True)
                P.cp(dst[:, c0:c0 + cw], G[0][:, 0:cw])
                c0 += cw

        def shifted_load(dst, srcbuf, rr, i):
            P.dma(dst[:, 0:512], srcbuf[rr:rr + 128, 0:512])
            if i == 0:
                P.v("memset", dst[0:1, 512:1024], 0.0)
                P.dma(dst[1:128, 512:1024], srcbuf[rr:rr + 127, 512:1024])
            else:
                P.dma(dst[:, 512:1024], srcbuf[rr - 1:rr + 127, 512:1024])

        for b in range(NB):
            for i in range(NT):
                rr = b * S + i * 128
                P.dma(xt[:, :], x[rr:rr + 128, :])
                emit_ln(P, sb, lnt[:, :], xt[:, :], D)
                P.dma(z_h[rr:rr + 128, :], lnt[:, :])

        for l in range(2):
            for kc in range(8):
                conv_load(wA[:, kc * 1536:(kc + 1) * 1536], wqkv[l, kc * 128:(kc + 1) * 128, :],
                          128, 1536)
            bcast_row(qsc[l], qscB, 512)
            bcast_row(ksc[l], kscB, 512)
            for b in range(NB):
                r0b = b * S
                for i in range(NT):
                    rr = r0b + i * 128
                    shifted_load(ht, z_h, rr, i)
                    P.v("tensor_copy", lnb[:, :], ht[:, :])
                    emit_transpose_row(P, sb, lnb, lnT, 8)
                    for kc in range(8):
                        for n in range(3):
                            P.mm(G[n][:, :], lnT[:, kc * 128:(kc + 1) * 128],
                                 wA[:, kc * 1536 + n * 512: kc * 1536 + (n + 1) * 512],
                                 start=(kc == 0), stop=(kc == 7))
                    for (gi, scB, cO, sO, Tdst) in ((0, qscB, xc_, xs_, qT_h),
                                                    (1, kscB, xck, xsk, kT_h)):
                        P.cp(qs0[:, :], G[gi][:, :])
                        for hh in range(8):
                            P.act(sb["lnscr"][:, hh * 64:(hh + 1) * 64],
                                  qs0[:, hh * 64:(hh + 1) * 64], AF.Square,
                                  accum_out=nrm[:, hh:hh + 1])
                        P.act(nrm[:, 0:8], nrm[:, 0:8], AF.Sqrt)
                        P.v("tensor_scalar_max", nrm[:, 0:8], nrm[:, 0:8], 1e-12)
                        P.v("reciprocal", nrm[:, 8:16], nrm[:, 0:8])
                        for hh in range(8):
                            P.v("tensor_scalar_mul", qn[:, hh * 64:(hh + 1) * 64],
                                qs0[:, hh * 64:(hh + 1) * 64], nrm[:, 8 + hh:9 + hh])
                        P.v("tensor_mul", qn[:, :], qn[:, :], scB[:, :])
                        P.dma(cost[:, :], cO[i * 128:(i + 1) * 128, :])
                        P.dma(sint[:, :], sO[i * 128:(i + 1) * 128, :])
                        P.v("tensor_mul", qa[:, :], qn[:, :], cost[:, :])
                        qv = qn[:, :].rearrange("p (h a c) -> p h a c", a=2, c=32)
                        sv = shq[:, :].rearrange("p (h a c) -> p h a c", a=2, c=32)
                        P.v("tensor_copy", sv[:, :, 0, :], qv[:, :, 1, :])
                        P.v("tensor_copy", sv[:, :, 1, :], qv[:, :, 0, :])
                        P.v("tensor_mul", shq[:, :], shq[:, :], sint[:, :])
                        P.v("tensor_add", qf[:, :], qa[:, :], shq[:, :])
                        P.v("tensor_copy", qfb[:, :], qf[:, :])
                        for hp in range(4):
                            P.T(sb["psT"][:, :], qfb[:, hp * 128:(hp + 1) * 128],
                                sb["identb"][:, :])
                            P.cp(tstage[:, hp * 128:(hp + 1) * 128], sb["psT"][:, :])
                        P.dma(Tdst[:, rr:rr + 128].rearrange("(a p) c -> p a c", p=128),
                              tstage[:, :].rearrange("p (a c) -> p a c", c=128))
                    P.v("tensor_copy", vb[:, :], G[2][:, :])
                    P.dma(v_h[rr:rr + 128, :], vb[:, :])
            for w in range(4):
                kw = 512 if w == 0 else 1024
                nkt = kw // 128
                if w == 0:
                    P.dma(maskP[:, 0:4 * 512].rearrange("p (a c) -> p a c", c=512),
                          mB0.rearrange("(a p) c -> p a c", p=128))
                    P.dma(maskT[:, 0:4 * 512].rearrange("p (a c) -> p a c", c=512),
                          mB0T.rearrange("(a p) c -> p a c", p=128))
                elif w == 1:
                    P.dma(maskP[:, :].rearrange("p (a c) -> p a c", c=1024),
                          mB.rearrange("(a p) c -> p a c", p=128))
                    P.dma(maskT[:, :].rearrange("p (a c) -> p a c", c=512),
                          mBT.rearrange("(a p) c -> p a c", p=128))
                for b in range(NB):
                    r0b = b * S
                    k0g = r0b + (w * 512 - (0 if w == 0 else 512))
                    q0g = r0b + w * 512
                    for h in range(8):
                        P.dma(qsb[:, :], qT_h[h * 64:(h + 1) * 64, q0g:q0g + 512])
                        P.dma(ksb[:, 0:kw], kT_h[h * 64:(h + 1) * 64, k0g:k0g + kw])
                        P.dma(vg[:, 0:nkt * 65].rearrange("p (t c) -> p t c", c=65)[:, :, 0:64],
                              v_h[k0g:k0g + kw, h * 64:(h + 1) * 64]
                              .rearrange("(t p) c -> p t c", p=128))
                        P.v("memset",
                            vg[:, 0:nkt * 65].rearrange("p (t c) -> p t c", c=65)[:, :, 64:65],
                            1.0)
                        P.v("tensor_copy", qhat[0:64, :], qsb[:, :])
                        P.v("tensor_copy", khat[0:64, 0:kw], ksb[:, 0:kw])
                        P.v("memset", khat[64:65, 0:kw], 1.0)
                        for t in range(4):
                            for n in range(kw // 512):
                                P.mm(G[n][:, :], qsb[:, t * 128:(t + 1) * 128],
                                     ksb[:, n * 512:(n + 1) * 512], start=True, stop=True)
                            for n in range(kw // 512):
                                P.v("tensor_add", ssb[:, n * 512:(n + 1) * 512],
                                    G[n][:, :],
                                    maskP[:, t * kw + n * 512: t * kw + (n + 1) * 512])
                            P.v("reduce_max", nmx[:, :], ssb[:, 0:kw], axis=AX.X,
                                negate=True)
                            P.T(sb["psTf"][0:1, :], nmx[:, :], sb["identf"][:, :])
                            P.cp(nmr[:, :], sb["psTf"][0:1, :])
                            P.v("tensor_copy", nmrb[:, :], nmr[:, :])
                            P.dma(qhat[64:65, t * 128:(t + 1) * 128], nmrb[:, :])
                        for kt in range(nkt):
                            P.mm(G[0][:, :], khat[:, kt * 128:(kt + 1) * 128],
                                 qhat[:, :], start=True, stop=True)
                            P.v("tensor_add", ssb[:, 0:512], G[0][:, :],
                                maskT[:, kt * 512:(kt + 1) * 512])
                            P.act(expb[:, :], ssb[:, 0:512], AF.Exp)
                            P.mm(G[1][0:65, :],
                                 vg[:, 0:nkt * 65].rearrange("p (t c) -> p t c", c=65)[:, kt, :],
                                 expb[:, :], start=(kt == 0), stop=(kt == nkt - 1))
                        P.cp(oTsb[:, :], G[1][0:65, :])
                        P.dma(r0[:, :], oTsb[64:65, :])
                        P.v("reciprocal", r0[:, :], r0[:, :])
                        P.mm(G[0][0:64, :], sb["onesfr"][:, 0:64], r0[:, :],
                             start=True, stop=True)
                        P.cp(rb[:, :], G[0][0:64, :])
                        P.v("tensor_mul", oTf[:, :], oTsb[0:64, :], rb[:, :])
                        P.dma(oT_h[h * 64:(h + 1) * 64, q0g:q0g + 512], oTf[:, :])
            for kc in range(4):
                conv_load(wA[:, kc * D:(kc + 1) * D], wo[l, kc * 128:(kc + 1) * 128, :],
                          128, D)
            for b in range(NB):
                r0b = b * S
                for i in range(NT):
                    rr = r0b + i * 128
                    P.dma(oTl[:, :].rearrange("p (a c) -> p a c", c=128),
                          oT_h[:, rr:rr + 128].rearrange("(a p) c -> p a c", p=128))
                    for kc in range(4):
                        for n in range(2):
                            P.mm(G[n][:, :], oTl[:, kc * 128:(kc + 1) * 128],
                                 wA[:, kc * D + n * 512: kc * D + (n + 1) * 512],
                                 start=(kc == 0), stop=(kc == 3))
                    P.dma(xt[:, :], z_h[rr:rr + 128, :])
                    for n in range(2):
                        P.v("tensor_add", yt[:, n * 512:(n + 1) * 512], G[n][:, :],
                            xt[:, n * 512:(n + 1) * 512])
                    P.dma(z2_h[rr:rr + 128, :], yt[:, :])
            bcast_row(lng[l], lngB, D)
            bcast_row(lnbi[l], lnbB, D)
            for b in range(NB):
                r0b = b * S
                for i in range(NT):
                    rr = r0b + i * 128
                    shifted_load(ht, z2_h, rr, i)
                    emit_ln(P, sb, lnt[:, :], ht[:, :], D, gain=lngB[:, :],
                            bias_t=lnbB[:, :])
                    P.v("tensor_copy", hb[:, :], lnt[:, :])
                    P.dma(h_h[rr:rr + 128, :], hb[:, :])
            for kc in range(8):
                conv_load(wA[:, kc * 5460:(kc + 1) * 5460],
                          w1[l, kc * 128:(kc + 1) * 128, :], 128, 5460)
            conv_load(b1S, b1[l], 1, 5460)
            for b in range(NB):
                r0b = b * S
                for i in range(NT):
                    rr = r0b + i * 128
                    P.dma(hb[:, :], h_h[rr:rr + 128, :])
                    emit_transpose_row(P, sb, hb, lnT, 8)
                    for c in range(6):
                        c0 = c * 512
                        cw = min(512, 2730 - c0)
                        for kc in range(8):
                            P.mm(G[0][:, 0:cw], lnT[:, kc * 128:(kc + 1) * 128],
                                 wA[:, kc * 5460 + c0: kc * 5460 + c0 + cw],
                                 start=(kc == 0), stop=False)
                            P.mm(G[1][:, 0:cw], lnT[:, kc * 128:(kc + 1) * 128],
                                 wA[:, kc * 5460 + 2730 + c0: kc * 5460 + 2730 + c0 + cw],
                                 start=(kc == 0), stop=False)
                        P.mm(G[0][:, 0:cw], sb["onesb"][:, :], b1S[:, c0:c0 + cw],
                             start=False, stop=True)
                        P.mm(G[1][:, 0:cw], sb["onesb"][:, :],
                             b1S[:, 2730 + c0:2730 + c0 + cw], start=False, stop=True)
                        P.act(gel[:, 0:cw], G[1][:, 0:cw], AF.Gelu)
                        P.v("tensor_mul", prodb[:, 0:cw], G[0][:, 0:cw], gel[:, 0:cw])
                        nsub = (cw + 127) // 128
                        for tsub in range(nsub):
                            tw = min(128, cw - tsub * 128)
                            P.T(sb["psT"][0:tw, :], prodb[:, tsub * 128:tsub * 128 + tw],
                                sb["identb"][:, :])
                            P.cp(tstage[0:tw, tsub * 128:(tsub + 1) * 128],
                                 sb["psT"][0:tw, :])
                            P.dma(gT_h[c0 + tsub * 128: c0 + tsub * 128 + tw, rr:rr + 128],
                                  tstage[0:tw, tsub * 128:(tsub + 1) * 128])
            for kc in range(22):
                r = kc * 128
                nr = min(128, 2730 - r)
                conv_load(wA[:, kc * D:(kc + 1) * D], w2[l, r:r + nr, :], nr, D)
            conv_load(b2S, b2[l], 1, D)
            for b in range(NB):
                r0b = b * S
                for i in range(NT):
                    rr = r0b + i * 128
                    P.dma(gTl[:, :].rearrange("p (a c) -> p a c", c=128),
                          gT_h[0:2688, rr:rr + 128].rearrange("(a p) c -> p a c", p=128))
                    P.dma(gTll[:, :], gT_h[2688:2730, rr:rr + 128])
                    for kc in range(22):
                        nr = min(128, 2730 - kc * 128)
                        lhs = (gTl[:, kc * 128:(kc + 1) * 128] if kc < 21
                               else gTll[:, :])
                        for n in range(2):
                            P.mm(G[2 + n][:, :], lhs,
                                 wA[0:nr, kc * D + n * 512: kc * D + (n + 1) * 512],
                                 start=(kc == 0), stop=False)
                    for n in range(2):
                        P.mm(G[2 + n][:, :], sb["onesb"][:, :],
                             b2S[:, n * 512:(n + 1) * 512], start=False, stop=True)
                    P.dma(xt[:, :], z2_h[rr:rr + 128, :])
                    for n in range(2):
                        P.v("tensor_add", yt[:, n * 512:(n + 1) * 512], G[2 + n][:, :],
                            xt[:, n * 512:(n + 1) * 512])
                    if l == 1:
                        P.dma(xt[:, :], x[rr:rr + 128, :])
                        P.v("tensor_add", yt[:, :], yt[:, :], xt[:, :])
                        P.dma(y[rr:rr + 128, :], yt[:, :])
                    else:
                        P.dma(z_h[rr:rr + 128, :], yt[:, :])
        finalize(nc, P, sem)
    return nc


def map_B(xc_in, inp):
    m = common_in_map()
    m["x"] = xc_in
    m["wqkv"] = np.ascontiguousarray(inp["L0_ffn_wqkv"])
    m["qsc"] = np.ascontiguousarray(
        np.stack([np.tile(np.asarray(inp["L0_ffn_q_scale"])[l], 8)[None, :]
                  for l in range(2)]))
    m["ksc"] = np.ascontiguousarray(
        np.stack([np.tile(np.asarray(inp["L0_ffn_k_scale"])[l], 8)[None, :]
                  for l in range(2)]))
    m["wo"] = np.ascontiguousarray(inp["L0_ffn_wo"])
    m["lng"] = np.ascontiguousarray(np.asarray(inp["L0_ffn_ln_g"])[:, None, :])
    m["lnbi"] = np.ascontiguousarray(np.asarray(inp["L0_ffn_ln_b"])[:, None, :])
    m["w1"] = np.ascontiguousarray(inp["L0_ffn_w1"])
    m["b1"] = np.ascontiguousarray(np.asarray(inp["L0_ffn_b1"])[:, None, :])
    m["w2"] = np.ascontiguousarray(inp["L0_ffn_w2"])
    m["b2"] = np.ascontiguousarray(np.asarray(inp["L0_ffn_b2"])[:, None, :])
    m["xc"] = _T["xpos_cq"]
    m["xs"] = _T["xpos_sq"]
    m["xck"] = _T["xpos_ck"]
    m["xsk"] = _T["xpos_sk"]
    m["mB0"] = _T["maskB0"]
    m["mB"] = _T["maskB"]
    m["mB0T"] = _T["maskB0T"]
    m["mBT"] = _T["maskBT"]
    return m


def _run_stage(build, mapf, xs_per_core, inp):
    nc = build()
    maps = [mapf(xs_per_core[c], inp) for c in range(NCORES)]
    res = run_bass_kernel_spmd(nc, maps, core_ids=list(range(NCORES)))
    return [np.asarray(r["y"], dtype=np.float32) for r in res.results]


STAGES = [(build_A, map_A), (build_B, map_B), (build_C, map_C), (build_D, map_D)]


def kernel(**inp):
    x = np.ascontiguousarray(np.asarray(inp["L0_attn_x"], dtype=np.float32))
    xs = [np.ascontiguousarray(x[2 * c:2 * c + 2].reshape(NB * S, D))
          for c in range(NCORES)]
    for build, mapf in STAGES:
        xs = _run_stage(build, mapf, xs, inp)
    out = np.stack([s.reshape(NB, S, D) for s in xs]).reshape(16, S, D)
    return out


# version 02
# speedup vs baseline: 1.0052x; 1.0052x over previous
import numpy as np
import ml_dtypes
import concourse.bass as bass
import concourse.mybir as mybir
from concourse.bass_utils import run_bass_kernel_spmd
from contextlib import ExitStack

F32 = mybir.dt.float32
BF16 = mybir.dt.bfloat16
AF = mybir.ActivationFunctionType
ALU = mybir.AluOpType
AX = mybir.AxisListType
BF = ml_dtypes.bfloat16

ENGINES = ("sync", "scalar", "vector", "tensor", "gpsimd")
NCORES = 8
S = 2048
NB = 2
NT = S // 128
D = 1024


class Prog:
    def __init__(self):
        self.ops = []

    def emit(self, eng, fn, delta=1):
        self.ops.append((eng, fn, delta))

    def dma(self, out, in_, eng="sync", **kw):
        self.emit(eng, lambda E, o=out, i=in_, kw=kw: E.dma_start(o, i, **kw), 16)

    def op(self, eng, name, *args, **kw):
        self.emit(eng, lambda E, n=name, a=args, kw=kw: getattr(E, n)(*a, **kw), 1)

    def mm(self, out, lhsT, rhs, start, stop):
        self.op("tensor", "matmul", out, lhsT, rhs, start=start, stop=stop)

    def T(self, out, in_, ident):
        self.op("tensor", "transpose", out, in_, ident)

    def cp(self, dst, src):
        self.op("scalar", "copy", dst, src)

    def act(self, out, in_, func, **kw):
        self.op("scalar", "activation", out, in_, func, **kw)

    def v(self, name, *args, **kw):
        self.op("vector", name, *args, **kw)


def finalize(nc, P, sem):
    pre = []
    v = 0
    for (e, fn, d) in P.ops:
        pre.append(v)
        v += d
    total = v
    ops = P.ops
    with nc.Block() as block:
        def mk(eng_name):
            def body(E):
                for i, (e, fn, d) in enumerate(ops):
                    if e != eng_name:
                        continue
                    if i > 0:
                        E.wait_ge(sem, pre[i])
                    fn(E).then_inc(sem, d)
                if eng_name == "sync":
                    E.wait_ge(sem, total)
            return body
        for en in ENGINES:
            getattr(block, en)(mk(en))


def _tables():
    t = {}
    t["identb"] = np.eye(128, dtype=np.float32).astype(BF)
    t["identf"] = np.eye(128, dtype=np.float32)
    t["onesb"] = np.ones((1, 128), dtype=np.float32).astype(BF)
    t["onesfr"] = np.ones((1, 128), dtype=np.float32)
    t["onesfc"] = np.ones((128, 1), dtype=np.float32)

    hd = 64
    inv = 1.0 / 10000.0 ** (np.arange(0, hd, 2, dtype=np.float64) / hd)
    pos = np.arange(S, dtype=np.float64)
    fr = np.outer(pos, inv)
    emb = np.concatenate([fr, fr], -1)
    cos = np.cos(emb)
    sin = np.sin(emb)
    sgn = np.where(np.arange(hd) % 2 == 0, -1.0, 1.0)
    c = 8.0 ** -0.5
    t["ropeA_cos"] = np.tile(cos * c, (1, 16)).astype(np.float32)
    t["ropeA_sin"] = np.tile(sin * sgn * c, (1, 16)).astype(np.float32)

    WIN = 512
    half = np.arange(0, hd, 2, dtype=np.float64)
    invf = 1.0 / 10000.0 ** (half / hd)
    tt = np.arange(S, dtype=np.float64)
    fr2 = tt[:, None] * invf[None, :]
    fr2 = np.concatenate([fr2, fr2], -1)
    base = (half + 0.4 * hd) / (1.4 * hd)
    power = (tt - S // 2) / float(WIN // 2)
    scale = base[None, :] ** power[:, None]
    scale = np.concatenate([scale, scale], -1)
    cos2 = np.cos(fr2)
    sin2 = np.sin(fr2)
    sgn2 = np.where(np.arange(hd) < 32, -1.0, 1.0)
    t["xpos_cq"] = np.tile(cos2 * scale * 8.0, (1, 8)).astype(np.float32)
    t["xpos_sq"] = np.tile(sin2 * scale * sgn2 * 8.0, (1, 8)).astype(np.float32)
    t["xpos_ck"] = np.tile(cos2 / scale, (1, 8)).astype(np.float32)
    t["xpos_sk"] = np.tile(sin2 * sgn2 / scale, (1, 8)).astype(np.float32)

    NEG = -1.0e30
    n = S
    w = n // WIN
    bt = np.arange(n).reshape(w, WIN)
    btp = np.pad(bt, ((1, 0), (0, 0)), constant_values=-1)
    btk = np.concatenate([btp[:w], btp[1:]], axis=1)
    m_list = []
    for wi in range(w):
        i = bt[wi][:, None]
        j = btk[wi][None, :]
        mask = (i < j) | (i > j + WIN) | (j == -1)
        m_list.append(np.where(mask, NEG, 0.0).astype(np.float32))
    assert np.array_equal(m_list[1], m_list[2])
    t["maskB0"] = m_list[0][:, 512:].astype(BF)
    t["maskB"] = m_list[1].astype(BF)
    t["maskB0T"] = np.ascontiguousarray(t["maskB0"].T)
    t["maskBT"] = np.ascontiguousarray(t["maskB"].T)

    H = 16
    slopes = 2.0 ** (-8.0 * np.arange(1, H + 1, dtype=np.float64) / H)
    j = np.arange(S, dtype=np.float64)
    t["C_Kb"] = np.stack([j, np.ones_like(j)], 0).astype(np.float32)
    Qb = np.zeros((2, H, S))
    for h in range(H):
        Qb[0, h, :] = slopes[h]
        Qb[1, h, :] = -slopes[h] * j
    t["C_Qb"] = Qb.reshape(2, H * S).astype(np.float32)
    CT = np.zeros((16, 128, 512), dtype=np.float32)
    for kt in range(16):
        qc = kt // 4
        jj = kt * 128 + np.arange(128)[:, None]
        ii = qc * 512 + np.arange(512)[None, :]
        CT[kt] = np.where(jj > ii, NEG, 0.0)
    t["C_CT"] = CT
    return t


_T = _tables()


def emit_ln(P, sb, out, in_, cols, gain=None, bias_t=None):
    st = sb["stats"]
    scr = sb["lnscr"]
    P.act(scr[:, 0:cols], in_, AF.Square, accum_out=st[:, 1:2])
    P.v("reduce_sum", st[:, 0:1], in_, axis=AX.X)
    P.v("tensor_scalar_mul", st[:, 2:3], st[:, 0:1], 1.0 / cols)
    P.v("tensor_mul", st[:, 3:4], st[:, 2:3], st[:, 2:3])
    P.v("tensor_scalar", st[:, 4:5], st[:, 1:2], 1.0 / cols, None, ALU.mult)
    P.v("tensor_sub", st[:, 4:5], st[:, 4:5], st[:, 3:4])
    P.v("tensor_scalar_add", st[:, 4:5], st[:, 4:5], 1e-5)
    P.act(st[:, 5:6], st[:, 4:5], AF.Sqrt)
    P.v("reciprocal", st[:, 6:7], st[:, 5:6])
    P.v("tensor_scalar", out, in_, st[:, 2:3], st[:, 6:7], ALU.subtract, ALU.mult)
    if gain is not None:
        P.v("tensor_mul", out, out, gain)
        P.v("tensor_add", out, out, bias_t)


def emit_transpose_row(P, sb, lnb_bf, lnT_bf, nk):
    for kc in range(nk):
        P.T(sb["psT"][:, :], lnb_bf[:, kc * 128:(kc + 1) * 128], sb["identb"][:, :])
        P.cp(lnT_bf[:, kc * 128:(kc + 1) * 128], sb["psT"][:, :])


def alloc_common(E, nc, sb):
    sb["identb"] = E(nc.sbuf_tensor("identb_s", [128, 128], BF16))
    sb["identf"] = E(nc.sbuf_tensor("identf_s", [128, 128], F32))
    sb["onesb"] = E(nc.sbuf_tensor("onesb_s", [1, 128], BF16))
    sb["onesfr"] = E(nc.sbuf_tensor("onesfr_s", [1, 128], F32))
    sb["onesfc"] = E(nc.sbuf_tensor("onesfc_s", [128, 1], F32))
    sb["stats"] = E(nc.sbuf_tensor("stats", [128, 16], F32))
    sb["lnscr"] = E(nc.sbuf_tensor("lnscr", [128, 1024], F32))
    for i in range(6):
        sb[f"G{i}"] = E(nc.psum_tensor(f"G{i}", [128, 512], F32))
    sb["psT"] = E(nc.psum_tensor("psT", [128, 128], BF16))
    sb["psTf"] = E(nc.psum_tensor("psTf", [128, 128], F32))


def load_common(P, sb, aps):
    for nm in ("identb", "identf", "onesb", "onesfr", "onesfc"):
        P.dma(sb[nm][:, :], aps[nm])


def decl_common_inputs(nc, aps):
    for nm in ("identb", "identf", "onesb", "onesfr", "onesfc"):
        arr = _T[nm]
        dt = BF16 if arr.dtype == BF else F32
        aps[nm] = nc.dram_tensor(nm, list(arr.shape), dt, kind="ExternalInput").ap()


def common_in_map():
    return {nm: _T[nm] for nm in ("identb", "identf", "onesb", "onesfr", "onesfc")}


def emit_A(P, nc, sb, x, y):
    W = nc.dram_tensor("A_W", [D, 3 * D], F32, kind="ExternalInput").ap()
    bqkv = nc.dram_tensor("A_bqkv", [1, 3 * D], F32, kind="ExternalInput").ap()
    WoT = nc.dram_tensor("A_WoT", [D, D], F32, kind="ExternalInput").ap()
    bo = nc.dram_tensor("A_bo", [1, D], F32, kind="ExternalInput").ap()
    rc = nc.dram_tensor("A_ropec", [S, D], F32, kind="ExternalInput").ap()
    rs = nc.dram_tensor("A_ropes", [S, D], F32, kind="ExternalInput").ap()
    qT_h = nc.dram_tensor("A_qT_h", [D, S], BF16, kind="Internal").ap()
    kT_h = nc.dram_tensor("A_kT_h", [D, S], BF16, kind="Internal").ap()
    v_h = nc.dram_tensor("A_v_h", [S, D], BF16, kind="Internal").ap()
    oT_h = nc.dram_tensor("A_oT_h", [D, S], BF16, kind="Internal").ap()

    with ExitStack() as stk:
        E = stk.enter_context
        Wsb = E(nc.sbuf_tensor("A_Wsb", [128, 8 * 3 * D], BF16))
        Wfs = E(nc.sbuf_tensor("Wfs", [128, 3 * D], F32))
        WoTsb = E(nc.sbuf_tensor("WoTsb", [128, 8 * D], BF16))
        bsb = E(nc.sbuf_tensor("bsb", [1, 3 * D], BF16))
        bosb = E(nc.sbuf_tensor("bosb", [1, D], BF16))
        bscr = E(nc.sbuf_tensor("bscr", [1, 3 * D], F32))
        xt = E(nc.sbuf_tensor("xt", [128, D], F32))
        lnt = E(nc.sbuf_tensor("lnt", [128, D], F32))
        lnb = E(nc.sbuf_tensor("lnb", [128, D], BF16))
        lnT = E(nc.sbuf_tensor("lnT", [128, 8 * 128], BF16))
        cost = E(nc.sbuf_tensor("cost", [128, D], F32))
        sint = E(nc.sbuf_tensor("sint", [128, D], F32))
        qs0 = E(nc.sbuf_tensor("qs0", [128, D], F32))
        qa = E(nc.sbuf_tensor("qa", [128, D], F32))
        shq = E(nc.sbuf_tensor("shq", [128, D], F32))
        qf = E(nc.sbuf_tensor("qf", [128, D], F32))
        qfb = E(nc.sbuf_tensor("qfb", [128, D], BF16))
        tstage = E(nc.sbuf_tensor("tstage", [128, 8 * 128], BF16))
        vb = E(nc.sbuf_tensor("vb", [128, D], BF16))
        qsb = E(nc.sbuf_tensor("qsb", [64, S], BF16))
        ksb = E(nc.sbuf_tensor("ksb", [64, S], BF16))
        vg = E(nc.sbuf_tensor("vg", [128, 16 * 65], BF16))
        expb = E(nc.sbuf_tensor("expb", [128, 512], BF16))
        oTsb = E(nc.sbuf_tensor("oTsb", [65, 512], F32))
        r0 = E(nc.sbuf_tensor("r0", [1, 512], F32))
        rb = E(nc.sbuf_tensor("rb", [64, 512], F32))
        oTf = E(nc.sbuf_tensor("oTf", [64, 512], BF16))
        oTl = E(nc.sbuf_tensor("oTl", [128, 8 * 128], BF16))
        yt = E(nc.sbuf_tensor("yt", [128, D], F32))

        for kc in range(8):
            P.dma(Wfs[:, :], W[kc * 128:(kc + 1) * 128, :])
            P.v("tensor_copy", Wsb[:, kc * 3 * D:(kc + 1) * 3 * D], Wfs[:, :])
        for kc in range(8):
            P.dma(Wfs[:, 0:D], WoT[kc * 128:(kc + 1) * 128, :])
            P.v("tensor_copy", WoTsb[:, kc * D:(kc + 1) * D], Wfs[:, 0:D])
        P.dma(bscr[:, :], bqkv)
        P.v("tensor_copy", bsb[:, :], bscr[:, :])
        P.dma(bscr[:, 0:D], bo)
        P.v("tensor_copy", bosb[:, :], bscr[:, 0:D])

        G = [sb[f"G{i}"] for i in range(6)]
        vgv = vg[:, :].rearrange("p (t c) -> p t c", c=65)
        for b in range(NB):
            r0b = b * S
            for i in range(NT):
                rr = r0b + i * 128
                P.dma(xt[:, :], x[rr:rr + 128, :])
                emit_ln(P, sb, lnt[:, :], xt[:, :], D)
                P.v("tensor_copy", lnb[:, :], lnt[:, :])
                emit_transpose_row(P, sb, lnb, lnT, 8)
                P.dma(cost[:, :], rc[i * 128:(i + 1) * 128, :])
                P.dma(sint[:, :], rs[i * 128:(i + 1) * 128, :])
                for kc in range(8):
                    for n in range(6):
                        P.mm(G[n][:, :], lnT[:, kc * 128:(kc + 1) * 128],
                             Wsb[:, kc * 3 * D + n * 512: kc * 3 * D + (n + 1) * 512],
                             start=(kc == 0), stop=False)
                for n in range(6):
                    P.mm(G[n][:, :], sb["onesb"][:, :], bsb[:, n * 512:(n + 1) * 512],
                         start=False, stop=True)
                for (g0, g1, Tdst) in ((0, 1, qT_h), (2, 3, kT_h)):
                    P.cp(qs0[:, 0:512], G[g0][:, :])
                    P.cp(qs0[:, 512:1024], G[g1][:, :])
                    P.v("tensor_mul", qa[:, :], qs0[:, :], cost[:, :])
                    P.v("tensor_copy", shq[:, 0:D:2], qs0[:, 1:D:2])
                    P.v("tensor_copy", shq[:, 1:D:2], qs0[:, 0:D:2])
                    P.v("tensor_mul", shq[:, :], shq[:, :], sint[:, :])
                    P.v("tensor_add", qf[:, :], qa[:, :], shq[:, :])
                    P.v("tensor_copy", qfb[:, :], qf[:, :])
                    for hp in range(8):
                        P.T(sb["psT"][:, :], qfb[:, hp * 128:(hp + 1) * 128],
                            sb["identb"][:, :])
                        P.cp(tstage[:, hp * 128:(hp + 1) * 128], sb["psT"][:, :])
                    P.dma(Tdst[:, i * 128:(i + 1) * 128]
                          .rearrange("(a p) c -> p a c", p=128),
                          tstage[:, :].rearrange("p (a c) -> p a c", c=128))
                P.cp(vb[:, 0:512], G[4][:, :])
                P.cp(vb[:, 512:1024], G[5][:, :])
                P.dma(v_h[i * 128:(i + 1) * 128, :], vb[:, :])
            for h in range(16):
                P.dma(qsb[:, :], qT_h[h * 64:(h + 1) * 64, :])
                P.dma(ksb[:, :], kT_h[h * 64:(h + 1) * 64, :])
                P.dma(vgv[:, :, 0:64],
                      v_h[:, h * 64:(h + 1) * 64].rearrange("(t p) c -> p t c", p=128))
                P.v("memset", vgv[:, :, 64:65], 1.0)
                for qc in range(4):
                    for kt in range(16):
                        P.mm(G[0][:, :], ksb[:, kt * 128:(kt + 1) * 128],
                             qsb[:, qc * 512:(qc + 1) * 512], start=True, stop=True)
                        P.act(expb[:, :], G[0][:, :], AF.Exp)
                        P.mm(G[2 + qc][0:65, :], vgv[:, kt, :],
                             expb[:, :], start=(kt == 0), stop=(kt == 15))
                for qc in range(4):
                    P.cp(oTsb[:, :], G[2 + qc][0:65, :])
                    P.dma(r0[:, :], oTsb[64:65, :])
                    P.v("reciprocal", r0[:, :], r0[:, :])
                    P.mm(G[0][0:64, :], sb["onesfr"][:, 0:64], r0[:, :],
                         start=True, stop=True)
                    P.cp(rb[:, :], G[0][0:64, :])
                    P.v("tensor_mul", oTf[:, :], oTsb[0:64, :], rb[:, :])
                    P.dma(oT_h[h * 64:(h + 1) * 64, qc * 512:(qc + 1) * 512],
                          oTf[:, :])
            for i in range(NT):
                rr = r0b + i * 128
                P.dma(oTl[:, :].rearrange("p (a c) -> p a c", c=128),
                      oT_h[:, i * 128:(i + 1) * 128].rearrange("(a p) c -> p a c", p=128))
                for kc in range(8):
                    for n in range(2):
                        P.mm(G[n][:, :], oTl[:, kc * 128:(kc + 1) * 128],
                             WoTsb[:, kc * D + n * 512: kc * D + (n + 1) * 512],
                             start=(kc == 0), stop=False)
                for n in range(2):
                    P.mm(G[n][:, :], sb["onesb"][:, :], bosb[:, n * 512:(n + 1) * 512],
                         start=False, stop=True)
                P.dma(xt[:, :], x[rr:rr + 128, :])
                for n in range(2):
                    P.v("tensor_add", yt[:, n * 512:(n + 1) * 512], G[n][:, :],
                        xt[:, n * 512:(n + 1) * 512])
                P.dma(y[rr:rr + 128, :], yt[:, :])


def map_A(inp):
    m = {}
    m["A_W"] = np.ascontiguousarray(
        np.concatenate([np.asarray(inp["L0_attn_Wq"]).T, np.asarray(inp["L0_attn_Wk"]).T,
                        np.asarray(inp["L0_attn_Wv"]).T], axis=1))
    m["A_bqkv"] = np.ascontiguousarray(
        np.concatenate([inp["L0_attn_bq"], inp["L0_attn_bk"],
                        inp["L0_attn_bv"]])[None, :])
    m["A_WoT"] = np.ascontiguousarray(np.asarray(inp["L0_attn_Wo"]).T)
    m["A_bo"] = np.ascontiguousarray(np.asarray(inp["L0_attn_bo"])[None, :])
    m["A_ropec"] = _T["ropeA_cos"]
    m["A_ropes"] = _T["ropeA_sin"]
    return m


def emit_D(P, nc, sb, x, y):
    Win = nc.dram_tensor("D_Win", [D, 64], F32, kind="ExternalInput").ap()
    bin_ = nc.dram_tensor("D_bin", [1, 64], F32, kind="ExternalInput").ap()
    W1r = nc.dram_tensor("D_W1r", [1, 1024], F32, kind="ExternalInput").ap()
    b1r = nc.dram_tensor("D_b1r", [1, 1024], F32, kind="ExternalInput").ap()
    W2r = nc.dram_tensor("D_W2r", [1, 1024], F32, kind="ExternalInput").ap()
    b2r = nc.dram_tensor("D_b2r", [1, 64], F32, kind="ExternalInput").ap()
    Wout = nc.dram_tensor("D_Wout", [64, D], F32, kind="ExternalInput").ap()
    bout = nc.dram_tensor("D_bout", [1, D], F32, kind="ExternalInput").ap()

    with ExitStack() as stk:
        E = stk.enter_context
        Wfs = E(nc.sbuf_tensor("D_Wfs", [128, 1024], F32))
        Winb = E(nc.sbuf_tensor("Winb", [128, 8 * 64], BF16))
        Woutb = E(nc.sbuf_tensor("Woutb", [64, D], BF16))
        binb = E(nc.sbuf_tensor("binb", [1, 64], BF16))
        boutb = E(nc.sbuf_tensor("boutb", [1, D], BF16))
        rowf = E(nc.sbuf_tensor("rowf", [1, 1024], F32))
        W1b = E(nc.sbuf_tensor("W1b", [128, 1024], F32))
        b1b = E(nc.sbuf_tensor("b1b", [128, 1024], F32))
        W2b = E(nc.sbuf_tensor("W2b", [128, 1024], F32))
        b2b = E(nc.sbuf_tensor("b2b", [128, 64], F32))
        xt = E(nc.sbuf_tensor("D_xt", [128, D], F32))
        lnt = E(nc.sbuf_tensor("D_lnt", [128, D], F32))
        lnb = E(nc.sbuf_tensor("D_lnb", [128, D], BF16))
        lnT = E(nc.sbuf_tensor("D_lnT", [128, 8 * 128], BF16))
        st_ = E(nc.sbuf_tensor("st_", [128, 64], F32))
        htmp = E(nc.sbuf_tensor("htmp", [128, 64], F32))
        gacc = E(nc.sbuf_tensor("gacc", [128, 64], F32))
        gb = E(nc.sbuf_tensor("gb", [128, 64], BF16))
        gT = E(nc.sbuf_tensor("gT", [64, 128], BF16))
        yt = E(nc.sbuf_tensor("D_yt", [128, D], F32))

        G = [sb[f"G{i}"] for i in range(6)]
        for kc in range(8):
            P.dma(Wfs[:, 0:64], Win[kc * 128:(kc + 1) * 128, :])
            P.v("tensor_copy", Winb[:, kc * 64:(kc + 1) * 64], Wfs[:, 0:64])
        P.dma(Wfs[0:64, 0:D], Wout)
        P.v("tensor_copy", Woutb[:, :], Wfs[0:64, 0:D])
        P.dma(rowf[:, 0:64], bin_)
        P.v("tensor_copy", binb[:, :], rowf[:, 0:64])
        P.dma(rowf[:, 0:D], bout)
        P.v("tensor_copy", boutb[:, :], rowf[:, 0:D])
        for src, dst in ((W1r, W1b), (b1r, b1b), (W2r, W2b)):
            P.dma(rowf[:, :], src)
            for n in range(2):
                P.mm(G[0][:, :], sb["onesfr"][:, :], rowf[:, n * 512:(n + 1) * 512],
                     start=True, stop=True)
                P.cp(dst[:, n * 512:(n + 1) * 512], G[0][:, :])
        P.dma(rowf[:, 0:64], b2r)
        P.mm(G[0][:, 0:64], sb["onesfr"][:, :], rowf[:, 0:64], start=True, stop=True)
        P.cp(b2b[:, :], G[0][:, 0:64])

        for b in range(NB):
            for i in range(NT):
                rr = b * S + i * 128
                P.dma(xt[:, :], x[rr:rr + 128, :])
                emit_ln(P, sb, lnt[:, :], xt[:, :], D)
                P.v("tensor_copy", lnb[:, :], lnt[:, :])
                emit_transpose_row(P, sb, lnb, lnT, 8)
                for kc in range(8):
                    P.mm(G[0][:, 0:64], lnT[:, kc * 128:(kc + 1) * 128],
                         Winb[:, kc * 64:(kc + 1) * 64], start=(kc == 0), stop=False)
                P.mm(G[0][:, 0:64], sb["onesb"][:, :], binb[:, :],
                     start=False, stop=True)
                P.cp(st_[:, :], G[0][:, 0:64])
                for ii in range(16):
                    P.v("tensor_mul", htmp[:, :], st_[:, :], W1b[:, ii * 64:(ii + 1) * 64])
                    P.v("tensor_add", htmp[:, :], htmp[:, :], b1b[:, ii * 64:(ii + 1) * 64])
                    P.v("tensor_scalar_max", htmp[:, :], htmp[:, :], 0.0)
                    P.v("tensor_mul", htmp[:, :], htmp[:, :], W2b[:, ii * 64:(ii + 1) * 64])
                    if ii == 0:
                        P.v("tensor_copy", gacc[:, :], htmp[:, :])
                    else:
                        P.v("tensor_add", gacc[:, :], gacc[:, :], htmp[:, :])
                P.v("tensor_add", gacc[:, :], gacc[:, :], b2b[:, :])
                P.v("tensor_copy", gb[:, :], gacc[:, :])
                P.T(sb["psT"][0:64, :], gb[:, :], sb["identb"][:, :])
                P.cp(gT[:, :], sb["psT"][0:64, :])
                for n in range(2):
                    P.mm(G[1 + n][:, :], gT[:, :], Woutb[:, n * 512:(n + 1) * 512],
                         start=True, stop=False)
                    P.mm(G[1 + n][:, :], sb["onesb"][:, :], boutb[:, n * 512:(n + 1) * 512],
                         start=False, stop=True)
                    P.v("tensor_add", yt[:, n * 512:(n + 1) * 512], G[1 + n][:, :],
                        xt[:, n * 512:(n + 1) * 512])
                P.dma(y[rr:rr + 128, :], yt[:, :])


def map_D(inp):
    m = {}
    m["D_Win"] = np.ascontiguousarray(inp["L1_ffn_Win"])
    m["D_bin"] = np.ascontiguousarray(np.asarray(inp["L1_ffn_bin_"])[None, :])
    m["D_W1r"] = np.ascontiguousarray(np.asarray(inp["L1_ffn_W1"]).T.reshape(1, 1024))
    m["D_b1r"] = np.ascontiguousarray(np.asarray(inp["L1_ffn_b1"]).T.reshape(1, 1024))
    m["D_W2r"] = np.ascontiguousarray(np.asarray(inp["L1_ffn_W2"]).T.reshape(1, 1024))
    m["D_b2r"] = np.ascontiguousarray(np.asarray(inp["L1_ffn_b2"])[None, :])
    m["D_Wout"] = np.ascontiguousarray(inp["L1_ffn_Wout"])
    m["D_bout"] = np.ascontiguousarray(np.asarray(inp["L1_ffn_bout"])[None, :])
    return m


def emit_C(P, nc, sb, x, y):
    W = nc.dram_tensor("C_W", [D, 5 * D], F32, kind="ExternalInput").ap()
    Wo = nc.dram_tensor("C_Wo", [D, D], F32, kind="ExternalInput").ap()
    lvec = nc.dram_tensor("C_lvec", [64, 4], F32, kind="ExternalInput").ap()
    lnwT = nc.dram_tensor("C_lnwT", [64, 16], F32, kind="ExternalInput").ap()
    lnbT = nc.dram_tensor("C_lnbT", [64, 16], F32, kind="ExternalInput").ap()
    Kb = nc.dram_tensor("C_Kb", [2, S], F32, kind="ExternalInput").ap()
    Qb = nc.dram_tensor("C_Qb", [2, 16 * S], F32, kind="ExternalInput").ap()
    CT = nc.dram_tensor("C_CT", [16, 128, 512], F32, kind="ExternalInput").ap()
    pT_h = nc.dram_tensor("C_pT_h", [4 * D, S], BF16, kind="Internal").ap()
    v_h = nc.dram_tensor("C_v_h", [S, D], BF16, kind="Internal").ap()
    oT_h = nc.dram_tensor("C_oT_h", [D, S], BF16, kind="Internal").ap()

    with ExitStack() as stk:
        E = stk.enter_context
        Wsb = E(nc.sbuf_tensor("C_Wsb", [128, 8 * 5120], BF16))
        Wfs = E(nc.sbuf_tensor("C_Wfs", [128, 5120], F32))
        WoSb = E(nc.sbuf_tensor("WoSb", [128, 8 * D], BF16))
        Kbs = E(nc.sbuf_tensor("Kbs", [2, S], F32))
        Qhs = E(nc.sbuf_tensor("Qhs", [2, S], F32))
        CTs = E(nc.sbuf_tensor("CTs", [128, 512], F32))
        lnwS = E(nc.sbuf_tensor("lnwS", [64, 16], F32))
        lnbS = E(nc.sbuf_tensor("lnbS", [64, 16], F32))
        lvs = E(nc.sbuf_tensor("lvs", [64, 4], F32))
        neglam = E(nc.sbuf_tensor("neglam", [1, 4], F32))
        xt = E(nc.sbuf_tensor("C_xt", [128, D], F32))
        lnt = E(nc.sbuf_tensor("C_lnt", [128, D], F32))
        lnb = E(nc.sbuf_tensor("C_lnb", [128, D], BF16))
        lnT = E(nc.sbuf_tensor("C_lnT", [128, 8 * 128], BF16))
        pstage = E(nc.sbuf_tensor("C_pstage", [128, 512], BF16))
        tstage = E(nc.sbuf_tensor("C_tstage", [128, 4 * 128], BF16))
        vb = E(nc.sbuf_tensor("C_vb", [128, D], BF16))
        q0sb = E(nc.sbuf_tensor("q0sb", [64, S], BF16))
        q1sb = E(nc.sbuf_tensor("q1sb", [64, S], BF16))
        k0sb = E(nc.sbuf_tensor("k0sb", [64, S], BF16))
        k1sb = E(nc.sbuf_tensor("k1sb", [64, S], BF16))
        vg = E(nc.sbuf_tensor("C_vg", [128, 16 * 65], BF16))
        ssb = E(nc.sbuf_tensor("ssb", [128, 512], F32))
        expb = E(nc.sbuf_tensor("C_expb", [128, 512], BF16))
        oT0 = E(nc.sbuf_tensor("oT0", [65, 512], F32))
        oT1 = E(nc.sbuf_tensor("oT1", [65, 512], F32))
        r0 = E(nc.sbuf_tensor("C_r0", [1, 512], F32))
        r1 = E(nc.sbuf_tensor("r1", [1, 512], F32))
        rb0 = E(nc.sbuf_tensor("rb0", [64, 512], F32))
        rb1 = E(nc.sbuf_tensor("rb1", [64, 512], F32))
        hsb = E(nc.sbuf_tensor("hsb", [64, 512], F32))
        hscr = E(nc.sbuf_tensor("hscr", [64, 512], F32))
        rowa = E(nc.sbuf_tensor("rowa", [1, 512], F32))
        rowb = E(nc.sbuf_tensor("rowb", [1, 512], F32))
        mub = E(nc.sbuf_tensor("mub", [64, 512], F32))
        oTf = E(nc.sbuf_tensor("C_oTf", [64, 512], BF16))
        oTl = E(nc.sbuf_tensor("C_oTl", [128, 8 * 128], BF16))
        yt = E(nc.sbuf_tensor("C_yt", [128, D], F32))

        for kc in range(8):
            P.dma(Wfs[:, :], W[kc * 128:(kc + 1) * 128, :])
            P.v("tensor_copy", Wsb[:, kc * 5120:(kc + 1) * 5120], Wfs[:, :])
        for kc in range(8):
            P.dma(Wfs[:, 0:D], Wo[kc * 128:(kc + 1) * 128, :])
            P.v("tensor_copy", WoSb[:, kc * D:(kc + 1) * D], Wfs[:, 0:D])
        P.dma(Kbs[:, :], Kb)
        P.dma(lnwS[:, :], lnwT)
        P.dma(lnbS[:, :], lnbT)
        P.dma(lvs[:, :], lvec)
        P.mm(sb["G0"][0:1, 0:1], lvs[:, 0:1], lvs[:, 1:2], start=True, stop=True)
        P.mm(sb["G0"][0:1, 1:2], lvs[:, 2:3], lvs[:, 3:4], start=True, stop=True)
        P.cp(neglam[:, 0:2], sb["G0"][0:1, 0:2])
        P.act(neglam[:, 2:4], neglam[:, 0:2], AF.Exp)
        P.v("tensor_add", neglam[:, 0:1], neglam[:, 2:3], neglam[:, 3:4])
        P.v("tensor_scalar_add", neglam[:, 0:1], neglam[:, 0:1], -1.6)

        G = [sb[f"G{i}"] for i in range(6)]
        vgv = vg[:, :].rearrange("p (t c) -> p t c", c=65)
        SC = 0.125 ** 0.5
        for b in range(NB):
            r0b = b * S
            for i in range(NT):
                rr = r0b + i * 128
                P.dma(xt[:, :], x[rr:rr + 128, :])
                emit_ln(P, sb, lnt[:, :], xt[:, :], D)
                P.v("tensor_copy", lnb[:, :], lnt[:, :])
                emit_transpose_row(P, sb, lnb, lnT, 8)
                for half in range(2):
                    for kc in range(8):
                        for n in range(5):
                            nn = half * 5 + n
                            P.mm(G[n][:, :], lnT[:, kc * 128:(kc + 1) * 128],
                                 Wsb[:, kc * 5120 + nn * 512: kc * 5120 + (nn + 1) * 512],
                                 start=(kc == 0), stop=(kc == 7))
                    for n in range(5):
                        nn = half * 5 + n
                        proj = nn // 2
                        if proj < 4:
                            P.v("tensor_scalar", pstage[:, :], G[n][:, :], SC, None,
                                ALU.mult)
                            for hp in range(4):
                                P.T(sb["psT"][:, :], pstage[:, hp * 128:(hp + 1) * 128],
                                    sb["identb"][:, :])
                                P.cp(tstage[:, hp * 128:(hp + 1) * 128], sb["psT"][:, :])
                            ro = proj * D + (nn % 2) * 512
                            P.dma(pT_h[ro:ro + 512, i * 128:(i + 1) * 128]
                                  .rearrange("(a p) c -> p a c", p=128),
                                  tstage[:, :].rearrange("p (a c) -> p a c", c=128))
                        else:
                            P.v("tensor_copy", vb[:, (nn % 2) * 512:(nn % 2) * 512 + 512],
                                G[n][:, :])
                P.dma(v_h[i * 128:(i + 1) * 128, :], vb[:, :])
            for h in range(16):
                P.dma(q0sb[:, :], pT_h[0 * D + h * 64: 0 * D + (h + 1) * 64, :])
                P.dma(q1sb[:, :], pT_h[1 * D + h * 64: 1 * D + (h + 1) * 64, :])
                P.dma(k0sb[:, :], pT_h[2 * D + h * 64: 2 * D + (h + 1) * 64, :])
                P.dma(k1sb[:, :], pT_h[3 * D + h * 64: 3 * D + (h + 1) * 64, :])
                P.dma(Qhs[:, :], Qb[:, h * S:(h + 1) * S])
                P.dma(vgv[:, :, 0:64],
                      v_h[:, h * 64:(h + 1) * 64].rearrange("(t p) c -> p t c", p=128))
                P.v("memset", vgv[:, :, 64:65], 1.0)
                for qc in range(4):
                    nkt = 4 * (qc + 1)
                    for kt in range(nkt):
                        diag = (kt // 4 == qc)
                        if diag:
                            P.dma(CTs[:, :], CT[kt, :, :])
                        for (qq, kk, gi) in ((q0sb, k0sb, 1), (q1sb, k1sb, 2)):
                            P.mm(G[0][:, :], kk[:, kt * 128:(kt + 1) * 128],
                                 qq[:, qc * 512:(qc + 1) * 512], start=True, stop=False)
                            P.mm(G[0][:, :], Kbs[:, kt * 128:(kt + 1) * 128],
                                 Qhs[:, qc * 512:(qc + 1) * 512], start=False, stop=True)
                            if diag:
                                P.v("tensor_add", ssb[:, :], G[0][:, :], CTs[:, :])
                                P.act(expb[:, :], ssb[:, :], AF.Exp)
                            else:
                                P.act(expb[:, :], G[0][:, :], AF.Exp)
                            P.mm(G[gi][0:65, :], vgv[:, kt, :], expb[:, :],
                                 start=(kt == 0), stop=(kt == nkt - 1))
                    P.cp(oT0[:, :], G[1][0:65, :])
                    P.cp(oT1[:, :], G[2][0:65, :])
                    P.dma(r0[:, :], oT0[64:65, :])
                    P.dma(r1[:, :], oT1[64:65, :])
                    P.v("reciprocal", r0[:, :], r0[:, :])
                    P.v("reciprocal", r1[:, :], r1[:, :])
                    P.v("tensor_scalar_mul", r1[:, :], r1[:, :], neglam[:, 0:1])
                    P.mm(G[0][0:64, :], sb["onesfr"][:, 0:64], r0[:, :],
                         start=True, stop=True)
                    P.cp(rb0[:, :], G[0][0:64, :])
                    P.mm(G[0][0:64, :], sb["onesfr"][:, 0:64], r1[:, :],
                         start=True, stop=True)
                    P.cp(rb1[:, :], G[0][0:64, :])
                    P.v("tensor_mul", hsb[:, :], oT0[0:64, :], rb0[:, :])
                    P.v("tensor_mul", hscr[:, :], oT1[0:64, :], rb1[:, :])
                    P.v("tensor_add", hsb[:, :], hsb[:, :], hscr[:, :])
                    P.mm(G[0][0:1, :], sb["onesfc"][0:64, :], hsb[:, :],
                         start=True, stop=True)
                    P.act(hscr[:, :], hsb[:, :], AF.Square)
                    P.mm(G[3][0:1, :], sb["onesfc"][0:64, :], hscr[:, :],
                         start=True, stop=True)
                    P.cp(rowa[:, :], G[0][0:1, :])
                    P.cp(rowb[:, :], G[3][0:1, :])
                    P.v("tensor_scalar_mul", rowa[:, :], rowa[:, :], 1.0 / 64)
                    P.v("tensor_scalar_mul", rowb[:, :], rowb[:, :], 1.0 / 64)
                    P.v("tensor_mul", r0[:, :], rowa[:, :], rowa[:, :])
                    P.v("tensor_sub", rowb[:, :], rowb[:, :], r0[:, :])
                    P.v("tensor_scalar_add", rowb[:, :], rowb[:, :], 1e-5)
                    P.act(r0[:, :], rowb[:, :], AF.Ln)
                    P.v("tensor_scalar_mul", r0[:, :], r0[:, :], -0.5)
                    P.act(rowb[:, :], r0[:, :], AF.Exp)
                    P.mm(G[0][0:64, :], sb["onesfr"][:, 0:64], rowa[:, :],
                         start=True, stop=True)
                    P.cp(mub[:, :], G[0][0:64, :])
                    P.mm(G[0][0:64, :], sb["onesfr"][:, 0:64], rowb[:, :],
                         start=True, stop=True)
                    P.cp(rb0[:, :], G[0][0:64, :])
                    P.v("tensor_sub", hsb[:, :], hsb[:, :], mub[:, :])
                    P.v("tensor_mul", hsb[:, :], hsb[:, :], rb0[:, :])
                    P.v("tensor_scalar", hsb[:, :], hsb[:, :], lnwS[:, h:h + 1],
                        lnbS[:, h:h + 1], ALU.mult, ALU.add)
                    P.v("tensor_copy", oTf[:, :], hsb[:, :])
                    P.dma(oT_h[h * 64:(h + 1) * 64, qc * 512:(qc + 1) * 512], oTf[:, :])
            for i in range(NT):
                rr = r0b + i * 128
                P.dma(oTl[:, :].rearrange("p (a c) -> p a c", c=128),
                      oT_h[:, i * 128:(i + 1) * 128].rearrange("(a p) c -> p a c", p=128))
                for kc in range(8):
                    for n in range(2):
                        P.mm(G[n][:, :], oTl[:, kc * 128:(kc + 1) * 128],
                             WoSb[:, kc * D + n * 512: kc * D + (n + 1) * 512],
                             start=(kc == 0), stop=(kc == 7))
                P.dma(xt[:, :], x[rr:rr + 128, :])
                for n in range(2):
                    P.v("tensor_add", yt[:, n * 512:(n + 1) * 512], G[n][:, :],
                        xt[:, n * 512:(n + 1) * 512])
                P.dma(y[rr:rr + 128, :], yt[:, :])


def map_C(inp):
    m = {}
    m["C_W"] = np.ascontiguousarray(
        np.concatenate([inp["L1_attn_wq0"], inp["L1_attn_wq1"],
                        inp["L1_attn_wk0"], inp["L1_attn_wk1"],
                        inp["L1_attn_wv"]], axis=1))
    m["C_Wo"] = np.ascontiguousarray(inp["L1_attn_wo"])
    m["C_lvec"] = np.ascontiguousarray(
        np.stack([inp["L1_attn_l0"], inp["L1_attn_l1"],
                  inp["L1_attn_l2"], inp["L1_attn_l3"]], axis=1))
    m["C_lnwT"] = np.ascontiguousarray(np.asarray(inp["L1_attn_ln_w"]).T)
    m["C_lnbT"] = np.ascontiguousarray(np.asarray(inp["L1_attn_ln_b"]).T)
    m["C_Kb"] = _T["C_Kb"]
    m["C_Qb"] = _T["C_Qb"]
    m["C_CT"] = _T["C_CT"]
    return m


def emit_B(P, nc, sb, x, y):
    wqkv = nc.dram_tensor("B_wqkv", [2, D, 1536], F32, kind="ExternalInput").ap()
    qsc = nc.dram_tensor("B_qsc", [2, 1, 512], F32, kind="ExternalInput").ap()
    ksc = nc.dram_tensor("B_ksc", [2, 1, 512], F32, kind="ExternalInput").ap()
    wo = nc.dram_tensor("B_wo", [2, 512, D], F32, kind="ExternalInput").ap()
    lng = nc.dram_tensor("B_lng", [2, 1, D], F32, kind="ExternalInput").ap()
    lnbi = nc.dram_tensor("B_lnbi", [2, 1, D], F32, kind="ExternalInput").ap()
    w1 = nc.dram_tensor("B_w1", [2, D, 5460], F32, kind="ExternalInput").ap()
    b1 = nc.dram_tensor("B_b1", [2, 1, 5460], F32, kind="ExternalInput").ap()
    w2 = nc.dram_tensor("B_w2", [2, 2730, D], F32, kind="ExternalInput").ap()
    b2 = nc.dram_tensor("B_b2", [2, 1, D], F32, kind="ExternalInput").ap()
    xc_ = nc.dram_tensor("B_xc", [S, 512], F32, kind="ExternalInput").ap()
    xs_ = nc.dram_tensor("B_xs", [S, 512], F32, kind="ExternalInput").ap()
    xck = nc.dram_tensor("B_xck", [S, 512], F32, kind="ExternalInput").ap()
    xsk = nc.dram_tensor("B_xsk", [S, 512], F32, kind="ExternalInput").ap()
    mB0 = nc.dram_tensor("B_mB0", [512, 512], BF16, kind="ExternalInput").ap()
    mB = nc.dram_tensor("B_mB", [512, 1024], BF16, kind="ExternalInput").ap()
    mB0T = nc.dram_tensor("B_mB0T", [512, 512], BF16, kind="ExternalInput").ap()
    mBT = nc.dram_tensor("B_mBT", [1024, 512], BF16, kind="ExternalInput").ap()
    z_h = nc.dram_tensor("B_z_h", [NB * S, D], F32, kind="Internal").ap()
    z2_h = nc.dram_tensor("B_z2_h", [NB * S, D], F32, kind="Internal").ap()
    h_h = nc.dram_tensor("B_h_h", [NB * S, D], BF16, kind="Internal").ap()
    qT_h = nc.dram_tensor("B_qT_h", [512, NB * S], BF16, kind="Internal").ap()
    kT_h = nc.dram_tensor("B_kT_h", [512, NB * S], BF16, kind="Internal").ap()
    v_h = nc.dram_tensor("B_v_h", [NB * S, 512], BF16, kind="Internal").ap()
    oT_h = nc.dram_tensor("B_oT_h", [512, NB * S], BF16, kind="Internal").ap()
    gT_h = nc.dram_tensor("B_gT_h", [2730, NB * S], BF16, kind="Internal").ap()

    with ExitStack() as stk:
        E = stk.enter_context
        wA = E(nc.sbuf_tensor("wA", [128, 8 * 5460], BF16))
        Wfs = E(nc.sbuf_tensor("B_Wfs", [128, 1536], F32))
        b1S = E(nc.sbuf_tensor("b1S", [1, 5460], BF16))
        b2S = E(nc.sbuf_tensor("b2S", [1, D], BF16))
        rowf = E(nc.sbuf_tensor("B_rowf", [1, 1536], F32))
        qscB = E(nc.sbuf_tensor("qscB", [128, 512], F32))
        kscB = E(nc.sbuf_tensor("kscB", [128, 512], F32))
        lngB = E(nc.sbuf_tensor("lngB", [128, D], F32))
        lnbB = E(nc.sbuf_tensor("lnbB", [128, D], F32))
        xt = E(nc.sbuf_tensor("B_xt", [128, D], F32))
        ht = E(nc.sbuf_tensor("B_ht", [128, D], F32))
        lnt = E(nc.sbuf_tensor("B_lnt", [128, D], F32))
        lnb = E(nc.sbuf_tensor("B_lnb", [128, D], BF16))
        lnT = E(nc.sbuf_tensor("B_lnT", [128, 8 * 128], BF16))
        cost = E(nc.sbuf_tensor("B_cost", [128, 512], F32))
        sint = E(nc.sbuf_tensor("B_sint", [128, 512], F32))
        qs0 = E(nc.sbuf_tensor("B_qs0", [128, 512], F32))
        nrm = E(nc.sbuf_tensor("B_nrm", [128, 16], F32))
        qn = E(nc.sbuf_tensor("B_qn", [128, 512], F32))
        qa = E(nc.sbuf_tensor("B_qa", [128, 512], F32))
        shq = E(nc.sbuf_tensor("B_shq", [128, 512], F32))
        qf = E(nc.sbuf_tensor("B_qf", [128, 512], F32))
        qfb = E(nc.sbuf_tensor("B_qfb", [128, 512], BF16))
        tstage = E(nc.sbuf_tensor("B_tstage", [128, 4 * 128], BF16))
        vb = E(nc.sbuf_tensor("B_vb", [128, 512], BF16))
        maskP = E(nc.sbuf_tensor("maskP", [128, 4 * 1024], BF16))
        maskT = E(nc.sbuf_tensor("maskT", [128, 8 * 512], BF16))
        qsb = E(nc.sbuf_tensor("B_qsb", [64, 512], BF16))
        ksb = E(nc.sbuf_tensor("B_ksb", [64, 1024], BF16))
        qhat = E(nc.sbuf_tensor("qhat", [65, 512], BF16))
        khat = E(nc.sbuf_tensor("khat", [65, 1024], BF16))
        vg = E(nc.sbuf_tensor("B_vg", [128, 8 * 65], BF16))
        ssb = sb["lnscr"]
        nmx = E(nc.sbuf_tensor("B_nmx", [128, 1], F32))
        nmr = E(nc.sbuf_tensor("B_nmr", [1, 128], F32))
        nmrb = E(nc.sbuf_tensor("B_nmrb", [1, 128], BF16))
        expb = E(nc.sbuf_tensor("B_expb", [128, 512], BF16))
        oTsb = E(nc.sbuf_tensor("B_oTsb", [65, 512], F32))
        r0 = E(nc.sbuf_tensor("B_r0", [1, 512], F32))
        rb = E(nc.sbuf_tensor("B_rb", [64, 512], F32))
        oTf = E(nc.sbuf_tensor("B_oTf", [64, 512], BF16))
        oTl = E(nc.sbuf_tensor("B_oTl", [128, 4 * 128], BF16))
        hb = E(nc.sbuf_tensor("B_hb", [128, D], BF16))
        gel = E(nc.sbuf_tensor("B_gel", [128, 512], F32))
        prodb = E(nc.sbuf_tensor("B_prodb", [128, 512], BF16))
        gTl = E(nc.sbuf_tensor("gTl", [128, 21 * 128], BF16))
        gTll = E(nc.sbuf_tensor("gTll", [42, 128], BF16))
        yt = lnt

        G = [sb[f"G{i}"] for i in range(6)]
        vgv = vg[:, :].rearrange("p (t c) -> p t c", c=65)

        def conv_load(dst_bf, src_ap, rows, cols):
            c0 = 0
            while c0 < cols:
                cw = min(1536, cols - c0)
                P.dma(Wfs[0:rows, 0:cw], src_ap[:, c0:c0 + cw] if rows == 128
                      else src_ap[0:rows, c0:c0 + cw])
                P.v("tensor_copy", dst_bf[0:rows, c0:c0 + cw], Wfs[0:rows, 0:cw])
                c0 += cw

        def bcast_row(src_l_ap, dst, w_):
            c0 = 0
            while c0 < w_:
                cw = min(512, w_ - c0)
                P.dma(rowf[:, 0:cw], src_l_ap[:, c0:c0 + cw])
                P.mm(G[0][:, 0:cw], sb["onesfr"][:, :], rowf[:, 0:cw],
                     start=True, stop=True)
                P.cp(dst[:, c0:c0 + cw], G[0][:, 0:cw])
                c0 += cw

        def shifted_load(dst, srcbuf, rr, i):
            P.dma(dst[:, 0:512], srcbuf[rr:rr + 128, 0:512])
            if i == 0:
                P.v("memset", dst[0:1, 512:1024], 0.0)
                P.dma(dst[1:128, 512:1024], srcbuf[rr:rr + 127, 512:1024])
            else:
                P.dma(dst[:, 512:1024], srcbuf[rr - 1:rr + 127, 512:1024])

        for b in range(NB):
            for i in range(NT):
                rr = b * S + i * 128
                P.dma(xt[:, :], x[rr:rr + 128, :])
                emit_ln(P, sb, lnt[:, :], xt[:, :], D)
                P.dma(z_h[rr:rr + 128, :], lnt[:, :])

        for l in range(2):
            for kc in range(8):
                conv_load(wA[:, kc * 1536:(kc + 1) * 1536], wqkv[l, kc * 128:(kc + 1) * 128, :],
                          128, 1536)
            bcast_row(qsc[l], qscB, 512)
            bcast_row(ksc[l], kscB, 512)
            for b in range(NB):
                r0b = b * S
                for i in range(NT):
                    rr = r0b + i * 128
                    shifted_load(ht, z_h, rr, i)
                    P.v("tensor_copy", lnb[:, :], ht[:, :])
                    emit_transpose_row(P, sb, lnb, lnT, 8)
                    for kc in range(8):
                        for n in range(3):
                            P.mm(G[n][:, :], lnT[:, kc * 128:(kc + 1) * 128],
                                 wA[:, kc * 1536 + n * 512: kc * 1536 + (n + 1) * 512],
                                 start=(kc == 0), stop=(kc == 7))
                    for (gi, scB, cO, sO, Tdst) in ((0, qscB, xc_, xs_, qT_h),
                                                    (1, kscB, xck, xsk, kT_h)):
                        P.cp(qs0[:, :], G[gi][:, :])
                        for hh in range(8):
                            P.act(sb["lnscr"][:, hh * 64:(hh + 1) * 64],
                                  qs0[:, hh * 64:(hh + 1) * 64], AF.Square,
                                  accum_out=nrm[:, hh:hh + 1])
                        P.act(nrm[:, 0:8], nrm[:, 0:8], AF.Sqrt)
                        P.v("tensor_scalar_max", nrm[:, 0:8], nrm[:, 0:8], 1e-12)
                        P.v("reciprocal", nrm[:, 8:16], nrm[:, 0:8])
                        for hh in range(8):
                            P.v("tensor_scalar_mul", qn[:, hh * 64:(hh + 1) * 64],
                                qs0[:, hh * 64:(hh + 1) * 64], nrm[:, 8 + hh:9 + hh])
                        P.v("tensor_mul", qn[:, :], qn[:, :], scB[:, :])
                        P.dma(cost[:, :], cO[i * 128:(i + 1) * 128, :])
                        P.dma(sint[:, :], sO[i * 128:(i + 1) * 128, :])
                        P.v("tensor_mul", qa[:, :], qn[:, :], cost[:, :])
                        qv = qn[:, :].rearrange("p (h a c) -> p h a c", a=2, c=32)
                        sv = shq[:, :].rearrange("p (h a c) -> p h a c", a=2, c=32)
                        P.v("tensor_copy", sv[:, :, 0, :], qv[:, :, 1, :])
                        P.v("tensor_copy", sv[:, :, 1, :], qv[:, :, 0, :])
                        P.v("tensor_mul", shq[:, :], shq[:, :], sint[:, :])
                        P.v("tensor_add", qf[:, :], qa[:, :], shq[:, :])
                        P.v("tensor_copy", qfb[:, :], qf[:, :])
                        for hp in range(4):
                            P.T(sb["psT"][:, :], qfb[:, hp * 128:(hp + 1) * 128],
                                sb["identb"][:, :])
                            P.cp(tstage[:, hp * 128:(hp + 1) * 128], sb["psT"][:, :])
                        P.dma(Tdst[:, rr:rr + 128].rearrange("(a p) c -> p a c", p=128),
                              tstage[:, :].rearrange("p (a c) -> p a c", c=128))
                    P.v("tensor_copy", vb[:, :], G[2][:, :])
                    P.dma(v_h[rr:rr + 128, :], vb[:, :])
            for w in range(4):
                kw = 512 if w == 0 else 1024
                nkt = kw // 128
                if w == 0:
                    P.dma(maskP[:, 0:4 * 512].rearrange("p (a c) -> p a c", c=512),
                          mB0.rearrange("(a p) c -> p a c", p=128))
                    P.dma(maskT[:, 0:4 * 512].rearrange("p (a c) -> p a c", c=512),
                          mB0T.rearrange("(a p) c -> p a c", p=128))
                elif w == 1:
                    P.dma(maskP[:, :].rearrange("p (a c) -> p a c", c=1024),
                          mB.rearrange("(a p) c -> p a c", p=128))
                    P.dma(maskT[:, :].rearrange("p (a c) -> p a c", c=512),
                          mBT.rearrange("(a p) c -> p a c", p=128))
                for b in range(NB):
                    r0b = b * S
                    k0g = r0b + (w * 512 - (0 if w == 0 else 512))
                    q0g = r0b + w * 512
                    for h in range(8):
                        P.dma(qsb[:, :], qT_h[h * 64:(h + 1) * 64, q0g:q0g + 512])
                        P.dma(ksb[:, 0:kw], kT_h[h * 64:(h + 1) * 64, k0g:k0g + kw])
                        P.dma(vg[:, 0:nkt * 65].rearrange("p (t c) -> p t c", c=65)[:, :, 0:64],
                              v_h[k0g:k0g + kw, h * 64:(h + 1) * 64]
                              .rearrange("(t p) c -> p t c", p=128))
                        P.v("memset",
                            vg[:, 0:nkt * 65].rearrange("p (t c) -> p t c", c=65)[:, :, 64:65],
                            1.0)
                        P.v("tensor_copy", qhat[0:64, :], qsb[:, :])
                        P.v("tensor_copy", khat[0:64, 0:kw], ksb[:, 0:kw])
                        P.v("memset", khat[64:65, 0:kw], 1.0)
                        for t in range(4):
                            for n in range(kw // 512):
                                P.mm(G[n][:, :], qsb[:, t * 128:(t + 1) * 128],
                                     ksb[:, n * 512:(n + 1) * 512], start=True, stop=True)
                            for n in range(kw // 512):
                                P.v("tensor_add", ssb[:, n * 512:(n + 1) * 512],
                                    G[n][:, :],
                                    maskP[:, t * kw + n * 512: t * kw + (n + 1) * 512])
                            P.v("reduce_max", nmx[:, :], ssb[:, 0:kw], axis=AX.X,
                                negate=True)
                            P.T(sb["psTf"][0:1, :], nmx[:, :], sb["identf"][:, :])
                            P.cp(nmr[:, :], sb["psTf"][0:1, :])
                            P.v("tensor_copy", nmrb[:, :], nmr[:, :])
                            P.dma(qhat[64:65, t * 128:(t + 1) * 128], nmrb[:, :])
                        for kt in range(nkt):
                            P.mm(G[0][:, :], khat[:, kt * 128:(kt + 1) * 128],
                                 qhat[:, :], start=True, stop=True)
                            P.v("tensor_add", ssb[:, 0:512], G[0][:, :],
                                maskT[:, kt * 512:(kt + 1) * 512])
                            P.act(expb[:, :], ssb[:, 0:512], AF.Exp)
                            P.mm(G[1][0:65, :],
                                 vg[:, 0:nkt * 65].rearrange("p (t c) -> p t c", c=65)[:, kt, :],
                                 expb[:, :], start=(kt == 0), stop=(kt == nkt - 1))
                        P.cp(oTsb[:, :], G[1][0:65, :])
                        P.dma(r0[:, :], oTsb[64:65, :])
                        P.v("reciprocal", r0[:, :], r0[:, :])
                        P.mm(G[0][0:64, :], sb["onesfr"][:, 0:64], r0[:, :],
                             start=True, stop=True)
                        P.cp(rb[:, :], G[0][0:64, :])
                        P.v("tensor_mul", oTf[:, :], oTsb[0:64, :], rb[:, :])
                        P.dma(oT_h[h * 64:(h + 1) * 64, q0g:q0g + 512], oTf[:, :])
            for kc in range(4):
                conv_load(wA[:, kc * D:(kc + 1) * D], wo[l, kc * 128:(kc + 1) * 128, :],
                          128, D)
            for b in range(NB):
                r0b = b * S
                for i in range(NT):
                    rr = r0b + i * 128
                    P.dma(oTl[:, :].rearrange("p (a c) -> p a c", c=128),
                          oT_h[:, rr:rr + 128].rearrange("(a p) c -> p a c", p=128))
                    for kc in range(4):
                        for n in range(2):
                            P.mm(G[n][:, :], oTl[:, kc * 128:(kc + 1) * 128],
                                 wA[:, kc * D + n * 512: kc * D + (n + 1) * 512],
                                 start=(kc == 0), stop=(kc == 3))
                    P.dma(xt[:, :], z_h[rr:rr + 128, :])
                    for n in range(2):
                        P.v("tensor_add", yt[:, n * 512:(n + 1) * 512], G[n][:, :],
                            xt[:, n * 512:(n + 1) * 512])
                    P.dma(z2_h[rr:rr + 128, :], yt[:, :])
            bcast_row(lng[l], lngB, D)
            bcast_row(lnbi[l], lnbB, D)
            for b in range(NB):
                r0b = b * S
                for i in range(NT):
                    rr = r0b + i * 128
                    shifted_load(ht, z2_h, rr, i)
                    emit_ln(P, sb, lnt[:, :], ht[:, :], D, gain=lngB[:, :],
                            bias_t=lnbB[:, :])
                    P.v("tensor_copy", hb[:, :], lnt[:, :])
                    P.dma(h_h[rr:rr + 128, :], hb[:, :])
            for kc in range(8):
                conv_load(wA[:, kc * 5460:(kc + 1) * 5460],
                          w1[l, kc * 128:(kc + 1) * 128, :], 128, 5460)
            conv_load(b1S, b1[l], 1, 5460)
            for b in range(NB):
                r0b = b * S
                for i in range(NT):
                    rr = r0b + i * 128
                    P.dma(hb[:, :], h_h[rr:rr + 128, :])
                    emit_transpose_row(P, sb, hb, lnT, 8)
                    for c in range(6):
                        c0 = c * 512
                        cw = min(512, 2730 - c0)
                        for kc in range(8):
                            P.mm(G[0][:, 0:cw], lnT[:, kc * 128:(kc + 1) * 128],
                                 wA[:, kc * 5460 + c0: kc * 5460 + c0 + cw],
                                 start=(kc == 0), stop=False)
                            P.mm(G[1][:, 0:cw], lnT[:, kc * 128:(kc + 1) * 128],
                                 wA[:, kc * 5460 + 2730 + c0: kc * 5460 + 2730 + c0 + cw],
                                 start=(kc == 0), stop=False)
                        P.mm(G[0][:, 0:cw], sb["onesb"][:, :], b1S[:, c0:c0 + cw],
                             start=False, stop=True)
                        P.mm(G[1][:, 0:cw], sb["onesb"][:, :],
                             b1S[:, 2730 + c0:2730 + c0 + cw], start=False, stop=True)
                        P.act(gel[:, 0:cw], G[1][:, 0:cw], AF.Gelu)
                        P.v("tensor_mul", prodb[:, 0:cw], G[0][:, 0:cw], gel[:, 0:cw])
                        nsub = (cw + 127) // 128
                        for tsub in range(nsub):
                            tw = min(128, cw - tsub * 128)
                            P.T(sb["psT"][0:tw, :], prodb[:, tsub * 128:tsub * 128 + tw],
                                sb["identb"][:, :])
                            P.cp(tstage[0:tw, tsub * 128:(tsub + 1) * 128],
                                 sb["psT"][0:tw, :])
                            P.dma(gT_h[c0 + tsub * 128: c0 + tsub * 128 + tw, rr:rr + 128],
                                  tstage[0:tw, tsub * 128:(tsub + 1) * 128])
            for kc in range(22):
                r = kc * 128
                nr = min(128, 2730 - r)
                conv_load(wA[:, kc * D:(kc + 1) * D], w2[l, r:r + nr, :], nr, D)
            conv_load(b2S, b2[l], 1, D)
            for b in range(NB):
                r0b = b * S
                for i in range(NT):
                    rr = r0b + i * 128
                    P.dma(gTl[:, :].rearrange("p (a c) -> p a c", c=128),
                          gT_h[0:2688, rr:rr + 128].rearrange("(a p) c -> p a c", p=128))
                    P.dma(gTll[:, :], gT_h[2688:2730, rr:rr + 128])
                    for kc in range(22):
                        nr = min(128, 2730 - kc * 128)
                        lhs = (gTl[:, kc * 128:(kc + 1) * 128] if kc < 21
                               else gTll[:, :])
                        for n in range(2):
                            P.mm(G[2 + n][:, :], lhs,
                                 wA[0:nr, kc * D + n * 512: kc * D + (n + 1) * 512],
                                 start=(kc == 0), stop=False)
                    for n in range(2):
                        P.mm(G[2 + n][:, :], sb["onesb"][:, :],
                             b2S[:, n * 512:(n + 1) * 512], start=False, stop=True)
                    P.dma(xt[:, :], z2_h[rr:rr + 128, :])
                    for n in range(2):
                        P.v("tensor_add", yt[:, n * 512:(n + 1) * 512], G[2 + n][:, :],
                            xt[:, n * 512:(n + 1) * 512])
                    if l == 1:
                        P.dma(xt[:, :], x[rr:rr + 128, :])
                        P.v("tensor_add", yt[:, :], yt[:, :], xt[:, :])
                        P.dma(y[rr:rr + 128, :], yt[:, :])
                    else:
                        P.dma(z_h[rr:rr + 128, :], yt[:, :])


def map_B(inp):
    m = {}
    m["B_wqkv"] = np.ascontiguousarray(inp["L0_ffn_wqkv"])
    m["B_qsc"] = np.ascontiguousarray(
        np.stack([np.tile(np.asarray(inp["L0_ffn_q_scale"])[l], 8)[None, :]
                  for l in range(2)]))
    m["B_ksc"] = np.ascontiguousarray(
        np.stack([np.tile(np.asarray(inp["L0_ffn_k_scale"])[l], 8)[None, :]
                  for l in range(2)]))
    m["B_wo"] = np.ascontiguousarray(inp["L0_ffn_wo"])
    m["B_lng"] = np.ascontiguousarray(np.asarray(inp["L0_ffn_ln_g"])[:, None, :])
    m["B_lnbi"] = np.ascontiguousarray(np.asarray(inp["L0_ffn_ln_b"])[:, None, :])
    m["B_w1"] = np.ascontiguousarray(inp["L0_ffn_w1"])
    m["B_b1"] = np.ascontiguousarray(np.asarray(inp["L0_ffn_b1"])[:, None, :])
    m["B_w2"] = np.ascontiguousarray(inp["L0_ffn_w2"])
    m["B_b2"] = np.ascontiguousarray(np.asarray(inp["L0_ffn_b2"])[:, None, :])
    m["B_xc"] = _T["xpos_cq"]
    m["B_xs"] = _T["xpos_sq"]
    m["B_xck"] = _T["xpos_ck"]
    m["B_xsk"] = _T["xpos_sk"]
    m["B_mB0"] = _T["maskB0"]
    m["B_mB"] = _T["maskB"]
    m["B_mB0T"] = _T["maskB0T"]
    m["B_mBT"] = _T["maskBT"]
    return m


EMITS = [emit_A, emit_B, emit_C, emit_D]
MAPS = [map_A, map_B, map_C, map_D]


def build_fused(stages=(0, 1, 2, 3)):
    nc = bass.Bass("TRN2", target_bir_lowering=False)
    aps = {}
    decl_common_inputs(nc, aps)
    x = nc.dram_tensor("x", [NB * S, D], F32, kind="ExternalInput").ap()
    y = nc.dram_tensor("y", [NB * S, D], F32, kind="ExternalOutput").ap()
    mids = [nc.dram_tensor(f"mid{i}", [NB * S, D], F32, kind="Internal").ap()
            for i in range(len(stages) - 1)]
    chain = [x] + mids + [y]
    P = Prog()
    with ExitStack() as stk:
        sb = {}
        alloc_common(stk.enter_context, nc, sb)
        sem = stk.enter_context(nc.semaphore("gsem"))
        load_common(P, sb, aps)
        for k, si in enumerate(stages):
            EMITS[si](P, nc, sb, chain[k], chain[k + 1])
        finalize(nc, P, sem)
    return nc


def map_fused(xc, inp, stages=(0, 1, 2, 3)):
    m = common_in_map()
    m["x"] = xc
    for si in stages:
        m.update(MAPS[si](inp))
    return m


def kernel(**inp):
    x = np.ascontiguousarray(np.asarray(inp["L0_attn_x"], dtype=np.float32))
    xs = [np.ascontiguousarray(x[2 * c:2 * c + 2].reshape(NB * S, D))
          for c in range(NCORES)]
    nc = build_fused()
    maps = [map_fused(xs[c], inp) for c in range(NCORES)]
    res = run_bass_kernel_spmd(nc, maps, core_ids=list(range(NCORES)))
    out = np.stack([np.asarray(r["y"], np.float32).reshape(NB, S, D)
                    for r in res.results]).reshape(16, S, D)
    return out


# version 03
# speedup vs baseline: 1.3211x; 1.3144x over previous
import numpy as np
import ml_dtypes
import concourse.bass as bass
import concourse.mybir as mybir
from concourse.bass_utils import run_bass_kernel_spmd
from contextlib import ExitStack

F32 = mybir.dt.float32
BF16 = mybir.dt.bfloat16
AF = mybir.ActivationFunctionType
ALU = mybir.AluOpType
AX = mybir.AxisListType
BF = ml_dtypes.bfloat16

ENGINES = ("sync", "scalar", "vector", "tensor", "gpsimd")
NCORES = 8
S = 2048
NB = 2
NT = S // 128
D = 1024


import bass_rust as _br

SELF_SYNC = ("scalar", "vector", "gpsimd")
CHAIN_SERIAL = ("sync",)


class Prog:
    def __init__(self):
        self.ops = []

    def dma(self, out, in_, eng="sync", **kw):
        self.ops.append((eng, "dma_start", (out, in_), kw, 16))

    def op(self, eng, name, *args, **kw):
        self.ops.append((eng, name, args, kw, 1))

    def barrier(self):
        self.ops.append(("BARRIER", None, None, None, 0))

    def mm(self, out, lhsT, rhs, start, stop):
        self.op("tensor", "matmul", out, lhsT, rhs, start=start, stop=stop)

    def T(self, out, in_, ident):
        self.op("tensor", "transpose", out, in_, ident)

    def cp(self, dst, src):
        self.op("scalar", "copy", dst, src)

    def act(self, out, in_, func, **kw):
        self.op("scalar", "activation", out, in_, func, **kw)

    def v(self, name, *args, **kw):
        self.op("vector", name, *args, **kw)


def _rw_sets(opname, args, kw):
    aps = [a for a in args if isinstance(a, _br.AP)]
    writes = [aps[0].tensor.name] if aps else []
    reads = [a.tensor.name for a in aps[1:]]
    for k, v in kw.items():
        if isinstance(v, _br.AP):
            if k == "accum_out":
                writes.append(v.tensor.name)
            else:
                reads.append(v.tensor.name)
    if opname == "matmul" and not kw.get("start", True):
        reads.append(writes[0])
    return reads, writes


def finalize(nc, P, sems):
    ops = P.ops
    n = len(ops)
    cum = {e: 0 for e in ENGINES}
    opval = [0] * n
    openg = [None] * n
    waits = [None] * n
    last_w = {}
    last_r = {}
    prev_op = {}
    bar = {e: 0 for e in ENGINES}
    for i, (e, opname, args, kw, d) in enumerate(ops):
        if e == "BARRIER":
            bar = dict(cum)
            last_w.clear()
            last_r.clear()
            continue
        cum[e] += d
        opval[i] = cum[e]
        openg[i] = e
        dep = {}
        if e in CHAIN_SERIAL and prev_op.get(e) is not None:
            dep[e] = opval[prev_op[e]]
        prev_op[e] = i
        def add(j):
            f = openg[j]
            if f != e or e in SELF_SYNC:
                dep[f] = max(dep.get(f, 0), opval[j])
        reads, writes = _rw_sets(opname, args, kw)
        for b in reads:
            for f, j in last_w.get(b, {}).items():
                add(j)
        for b in writes:
            for f, j in last_w.get(b, {}).items():
                add(j)
            for f, j in last_r.get(b, {}).items():
                add(j)
        for f, v in bar.items():
            if f != e and v > 0:
                dep[f] = max(dep.get(f, 0), v)
        waits[i] = dep
        for b in reads:
            last_r.setdefault(b, {})[e] = i
        for b in writes:
            last_w.setdefault(b, {})[e] = i
            lr = last_r.get(b)
            if lr is not None:
                lr.clear()
            last_r.setdefault(b, {})[e] = i
    total = dict(cum)
    with nc.Block() as block:
        def mk(eng_name):
            def body(E):
                water = {f: 0 for f in ENGINES}
                for i, (e, opname, args, kw, d) in enumerate(ops):
                    if e != eng_name:
                        continue
                    for f, v in waits[i].items():
                        if v > water[f]:
                            E.wait_ge(sems[f], v)
                            water[f] = v
                    getattr(E, opname)(*args, **kw).then_inc(sems[e], d)
                for f in ENGINES:
                    if f != eng_name and total[f] > water[f]:
                        E.wait_ge(sems[f], total[f])
                        water[f] = total[f]
            return body
        for en in ENGINES:
            getattr(block, en)(mk(en))


def _tables():
    t = {}
    t["identb"] = np.eye(128, dtype=np.float32).astype(BF)
    t["identf"] = np.eye(128, dtype=np.float32)
    t["onesb"] = np.ones((1, 128), dtype=np.float32).astype(BF)
    t["onesfr"] = np.ones((1, 128), dtype=np.float32)
    t["onesfc"] = np.ones((128, 1), dtype=np.float32)

    hd = 64
    inv = 1.0 / 10000.0 ** (np.arange(0, hd, 2, dtype=np.float64) / hd)
    pos = np.arange(S, dtype=np.float64)
    fr = np.outer(pos, inv)
    emb = np.concatenate([fr, fr], -1)
    cos = np.cos(emb)
    sin = np.sin(emb)
    sgn = np.where(np.arange(hd) % 2 == 0, -1.0, 1.0)
    c = 8.0 ** -0.5
    t["ropeA_cos"] = np.tile(cos * c, (1, 16)).astype(np.float32)
    t["ropeA_sin"] = np.tile(sin * sgn * c, (1, 16)).astype(np.float32)

    WIN = 512
    half = np.arange(0, hd, 2, dtype=np.float64)
    invf = 1.0 / 10000.0 ** (half / hd)
    tt = np.arange(S, dtype=np.float64)
    fr2 = tt[:, None] * invf[None, :]
    fr2 = np.concatenate([fr2, fr2], -1)
    base = (half + 0.4 * hd) / (1.4 * hd)
    power = (tt - S // 2) / float(WIN // 2)
    scale = base[None, :] ** power[:, None]
    scale = np.concatenate([scale, scale], -1)
    cos2 = np.cos(fr2)
    sin2 = np.sin(fr2)
    sgn2 = np.where(np.arange(hd) < 32, -1.0, 1.0)
    t["xpos_cq"] = np.tile(cos2 * scale * 8.0, (1, 8)).astype(np.float32)
    t["xpos_sq"] = np.tile(sin2 * scale * sgn2 * 8.0, (1, 8)).astype(np.float32)
    t["xpos_ck"] = np.tile(cos2 / scale, (1, 8)).astype(np.float32)
    t["xpos_sk"] = np.tile(sin2 * sgn2 / scale, (1, 8)).astype(np.float32)

    NEG = -1.0e30
    n = S
    w = n // WIN
    bt = np.arange(n).reshape(w, WIN)
    btp = np.pad(bt, ((1, 0), (0, 0)), constant_values=-1)
    btk = np.concatenate([btp[:w], btp[1:]], axis=1)
    m_list = []
    for wi in range(w):
        i = bt[wi][:, None]
        j = btk[wi][None, :]
        mask = (i < j) | (i > j + WIN) | (j == -1)
        m_list.append(np.where(mask, NEG, 0.0).astype(np.float32))
    assert np.array_equal(m_list[1], m_list[2])
    t["maskB0"] = m_list[0][:, 512:].astype(BF)
    t["maskB"] = m_list[1].astype(BF)
    t["maskB0T"] = np.ascontiguousarray(t["maskB0"].T)
    t["maskBT"] = np.ascontiguousarray(t["maskB"].T)

    H = 16
    slopes = 2.0 ** (-8.0 * np.arange(1, H + 1, dtype=np.float64) / H)
    j = np.arange(S, dtype=np.float64)
    t["C_Kb"] = np.stack([j, np.ones_like(j)], 0).astype(np.float32)
    Qb = np.zeros((2, H, S))
    for h in range(H):
        Qb[0, h, :] = slopes[h]
        Qb[1, h, :] = -slopes[h] * j
    t["C_Qb"] = Qb.reshape(2, H * S).astype(np.float32)
    CT = np.zeros((16, 128, 512), dtype=np.float32)
    for kt in range(16):
        qc = kt // 4
        jj = kt * 128 + np.arange(128)[:, None]
        ii = qc * 512 + np.arange(512)[None, :]
        CT[kt] = np.where(jj > ii, NEG, 0.0)
    t["C_CT"] = CT
    return t


_T = _tables()


def emit_ln(P, sb, out, in_, cols, gain=None, bias_t=None):
    st = sb["stats"]
    scr = sb["lnscr"]
    P.act(scr[:, 0:cols], in_, AF.Square, accum_out=st[:, 1:2])
    P.v("reduce_sum", st[:, 0:1], in_, axis=AX.X)
    P.v("tensor_scalar_mul", st[:, 2:3], st[:, 0:1], 1.0 / cols)
    P.v("tensor_mul", st[:, 3:4], st[:, 2:3], st[:, 2:3])
    P.v("tensor_scalar", st[:, 4:5], st[:, 1:2], 1.0 / cols, None, ALU.mult)
    P.v("tensor_sub", st[:, 4:5], st[:, 4:5], st[:, 3:4])
    P.v("tensor_scalar_add", st[:, 4:5], st[:, 4:5], 1e-5)
    P.act(st[:, 5:6], st[:, 4:5], AF.Sqrt)
    P.v("reciprocal", st[:, 6:7], st[:, 5:6])
    P.v("tensor_scalar", out, in_, st[:, 2:3], st[:, 6:7], ALU.subtract, ALU.mult)
    if gain is not None:
        P.v("tensor_mul", out, out, gain)
        P.v("tensor_add", out, out, bias_t)


def emit_transpose_row(P, sb, lnb_bf, lnT_bf, nk):
    for kc in range(nk):
        P.T(sb["psT"][:, :], lnb_bf[:, kc * 128:(kc + 1) * 128], sb["identb"][:, :])
        P.cp(lnT_bf[:, kc * 128:(kc + 1) * 128], sb["psT"][:, :])


def alloc_common(E, nc, sb):
    sb["identb"] = E(nc.sbuf_tensor("identb_s", [128, 128], BF16))
    sb["identf"] = E(nc.sbuf_tensor("identf_s", [128, 128], F32))
    sb["onesb"] = E(nc.sbuf_tensor("onesb_s", [1, 128], BF16))
    sb["onesfr"] = E(nc.sbuf_tensor("onesfr_s", [1, 128], F32))
    sb["onesfc"] = E(nc.sbuf_tensor("onesfc_s", [128, 1], F32))
    sb["stats"] = E(nc.sbuf_tensor("stats", [128, 16], F32))
    sb["lnscr"] = E(nc.sbuf_tensor("lnscr", [128, 1024], F32))
    for i in range(6):
        sb[f"G{i}"] = E(nc.psum_tensor(f"G{i}", [128, 512], F32))
    sb["psT"] = E(nc.psum_tensor("psT", [128, 128], BF16))
    sb["psTf"] = E(nc.psum_tensor("psTf", [128, 128], F32))


def load_common(P, sb, aps):
    for nm in ("identb", "identf", "onesb", "onesfr", "onesfc"):
        P.dma(sb[nm][:, :], aps[nm])


def decl_common_inputs(nc, aps):
    for nm in ("identb", "identf", "onesb", "onesfr", "onesfc"):
        arr = _T[nm]
        dt = BF16 if arr.dtype == BF else F32
        aps[nm] = nc.dram_tensor(nm, list(arr.shape), dt, kind="ExternalInput").ap()


def common_in_map():
    return {nm: _T[nm] for nm in ("identb", "identf", "onesb", "onesfr", "onesfc")}


def emit_A(P, nc, sb, x, y):
    W = nc.dram_tensor("A_W", [D, 3 * D], F32, kind="ExternalInput").ap()
    bqkv = nc.dram_tensor("A_bqkv", [1, 3 * D], F32, kind="ExternalInput").ap()
    WoT = nc.dram_tensor("A_WoT", [D, D], F32, kind="ExternalInput").ap()
    bo = nc.dram_tensor("A_bo", [1, D], F32, kind="ExternalInput").ap()
    rc = nc.dram_tensor("A_ropec", [S, D], F32, kind="ExternalInput").ap()
    rs = nc.dram_tensor("A_ropes", [S, D], F32, kind="ExternalInput").ap()
    qT_h = nc.dram_tensor("A_qT_h", [D, S], BF16, kind="Internal").ap()
    kT_h = nc.dram_tensor("A_kT_h", [D, S], BF16, kind="Internal").ap()
    v_h = nc.dram_tensor("A_v_h", [S, D], BF16, kind="Internal").ap()
    oT_h = nc.dram_tensor("A_oT_h", [D, S], BF16, kind="Internal").ap()

    with ExitStack() as stk:
        E = stk.enter_context
        Wsb = E(nc.sbuf_tensor("A_Wsb", [128, 8 * 3 * D], BF16))
        Wfs = E(nc.sbuf_tensor("Wfs", [128, 3 * D], F32))
        WoTsb = E(nc.sbuf_tensor("WoTsb", [128, 8 * D], BF16))
        bsb = E(nc.sbuf_tensor("bsb", [1, 3 * D], BF16))
        bosb = E(nc.sbuf_tensor("bosb", [1, D], BF16))
        bscr = E(nc.sbuf_tensor("bscr", [1, 3 * D], F32))
        xt = E(nc.sbuf_tensor("xt", [128, D], F32))
        lnt = E(nc.sbuf_tensor("lnt", [128, D], F32))
        lnb = E(nc.sbuf_tensor("lnb", [128, D], BF16))
        lnT = E(nc.sbuf_tensor("lnT", [128, 8 * 128], BF16))
        cost = E(nc.sbuf_tensor("cost", [128, D], F32))
        sint = E(nc.sbuf_tensor("sint", [128, D], F32))
        qs0 = E(nc.sbuf_tensor("qs0", [128, D], F32))
        qa = E(nc.sbuf_tensor("qa", [128, D], F32))
        shq = E(nc.sbuf_tensor("shq", [128, D], F32))
        qf = E(nc.sbuf_tensor("qf", [128, D], F32))
        qfb = E(nc.sbuf_tensor("qfb", [128, D], BF16))
        tstage = E(nc.sbuf_tensor("tstage", [128, 8 * 128], BF16))
        vb = E(nc.sbuf_tensor("vb", [128, D], BF16))
        qsb = E(nc.sbuf_tensor("qsb", [64, S], BF16))
        ksb = E(nc.sbuf_tensor("ksb", [64, S], BF16))
        vg = E(nc.sbuf_tensor("vg", [128, 16 * 65], BF16))
        expb = E(nc.sbuf_tensor("expb", [128, 512], BF16))
        oTsb = E(nc.sbuf_tensor("oTsb", [65, 512], F32))
        r0 = E(nc.sbuf_tensor("r0", [1, 512], F32))
        rb = E(nc.sbuf_tensor("rb", [64, 512], F32))
        oTf = E(nc.sbuf_tensor("oTf", [64, 512], BF16))
        oTl = E(nc.sbuf_tensor("oTl", [128, 8 * 128], BF16))
        yt = E(nc.sbuf_tensor("yt", [128, D], F32))

        for kc in range(8):
            P.dma(Wfs[:, :], W[kc * 128:(kc + 1) * 128, :])
            P.v("tensor_copy", Wsb[:, kc * 3 * D:(kc + 1) * 3 * D], Wfs[:, :])
        for kc in range(8):
            P.dma(Wfs[:, 0:D], WoT[kc * 128:(kc + 1) * 128, :])
            P.v("tensor_copy", WoTsb[:, kc * D:(kc + 1) * D], Wfs[:, 0:D])
        P.dma(bscr[:, :], bqkv)
        P.v("tensor_copy", bsb[:, :], bscr[:, :])
        P.dma(bscr[:, 0:D], bo)
        P.v("tensor_copy", bosb[:, :], bscr[:, 0:D])

        G = [sb[f"G{i}"] for i in range(6)]
        vgv = vg[:, :].rearrange("p (t c) -> p t c", c=65)
        for b in range(NB):
            r0b = b * S
            for i in range(NT):
                rr = r0b + i * 128
                P.dma(xt[:, :], x[rr:rr + 128, :])
                emit_ln(P, sb, lnt[:, :], xt[:, :], D)
                P.v("tensor_copy", lnb[:, :], lnt[:, :])
                emit_transpose_row(P, sb, lnb, lnT, 8)
                P.dma(cost[:, :], rc[i * 128:(i + 1) * 128, :])
                P.dma(sint[:, :], rs[i * 128:(i + 1) * 128, :])
                for kc in range(8):
                    for n in range(6):
                        P.mm(G[n][:, :], lnT[:, kc * 128:(kc + 1) * 128],
                             Wsb[:, kc * 3 * D + n * 512: kc * 3 * D + (n + 1) * 512],
                             start=(kc == 0), stop=False)
                for n in range(6):
                    P.mm(G[n][:, :], sb["onesb"][:, :], bsb[:, n * 512:(n + 1) * 512],
                         start=False, stop=True)
                for (g0, g1, Tdst) in ((0, 1, qT_h), (2, 3, kT_h)):
                    P.cp(qs0[:, 0:512], G[g0][:, :])
                    P.cp(qs0[:, 512:1024], G[g1][:, :])
                    P.v("tensor_mul", qa[:, :], qs0[:, :], cost[:, :])
                    P.v("tensor_copy", shq[:, 0:D:2], qs0[:, 1:D:2])
                    P.v("tensor_copy", shq[:, 1:D:2], qs0[:, 0:D:2])
                    P.v("tensor_mul", shq[:, :], shq[:, :], sint[:, :])
                    P.v("tensor_add", qf[:, :], qa[:, :], shq[:, :])
                    P.v("tensor_copy", qfb[:, :], qf[:, :])
                    for hp in range(8):
                        P.T(sb["psT"][:, :], qfb[:, hp * 128:(hp + 1) * 128],
                            sb["identb"][:, :])
                        P.cp(tstage[:, hp * 128:(hp + 1) * 128], sb["psT"][:, :])
                    P.dma(Tdst[:, i * 128:(i + 1) * 128]
                          .rearrange("(a p) c -> p a c", p=128),
                          tstage[:, :].rearrange("p (a c) -> p a c", c=128))
                P.cp(vb[:, 0:512], G[4][:, :])
                P.cp(vb[:, 512:1024], G[5][:, :])
                P.dma(v_h[i * 128:(i + 1) * 128, :], vb[:, :])
            for h in range(16):
                P.dma(qsb[:, :], qT_h[h * 64:(h + 1) * 64, :])
                P.dma(ksb[:, :], kT_h[h * 64:(h + 1) * 64, :])
                P.dma(vgv[:, :, 0:64],
                      v_h[:, h * 64:(h + 1) * 64].rearrange("(t p) c -> p t c", p=128))
                P.v("memset", vgv[:, :, 64:65], 1.0)
                for qc in range(4):
                    for kt in range(16):
                        P.mm(G[0][:, :], ksb[:, kt * 128:(kt + 1) * 128],
                             qsb[:, qc * 512:(qc + 1) * 512], start=True, stop=True)
                        P.act(expb[:, :], G[0][:, :], AF.Exp)
                        P.mm(G[2 + qc][0:65, :], vgv[:, kt, :],
                             expb[:, :], start=(kt == 0), stop=(kt == 15))
                for qc in range(4):
                    P.cp(oTsb[:, :], G[2 + qc][0:65, :])
                    P.dma(r0[:, :], oTsb[64:65, :])
                    P.v("reciprocal", r0[:, :], r0[:, :])
                    P.mm(G[0][0:64, :], sb["onesfr"][:, 0:64], r0[:, :],
                         start=True, stop=True)
                    P.cp(rb[:, :], G[0][0:64, :])
                    P.v("tensor_mul", oTf[:, :], oTsb[0:64, :], rb[:, :])
                    P.dma(oT_h[h * 64:(h + 1) * 64, qc * 512:(qc + 1) * 512],
                          oTf[:, :])
            for i in range(NT):
                rr = r0b + i * 128
                P.dma(oTl[:, :].rearrange("p (a c) -> p a c", c=128),
                      oT_h[:, i * 128:(i + 1) * 128].rearrange("(a p) c -> p a c", p=128))
                for kc in range(8):
                    for n in range(2):
                        P.mm(G[n][:, :], oTl[:, kc * 128:(kc + 1) * 128],
                             WoTsb[:, kc * D + n * 512: kc * D + (n + 1) * 512],
                             start=(kc == 0), stop=False)
                for n in range(2):
                    P.mm(G[n][:, :], sb["onesb"][:, :], bosb[:, n * 512:(n + 1) * 512],
                         start=False, stop=True)
                P.dma(xt[:, :], x[rr:rr + 128, :])
                for n in range(2):
                    P.v("tensor_add", yt[:, n * 512:(n + 1) * 512], G[n][:, :],
                        xt[:, n * 512:(n + 1) * 512])
                P.dma(y[rr:rr + 128, :], yt[:, :])


def map_A(inp):
    m = {}
    m["A_W"] = np.ascontiguousarray(
        np.concatenate([np.asarray(inp["L0_attn_Wq"]).T, np.asarray(inp["L0_attn_Wk"]).T,
                        np.asarray(inp["L0_attn_Wv"]).T], axis=1))
    m["A_bqkv"] = np.ascontiguousarray(
        np.concatenate([inp["L0_attn_bq"], inp["L0_attn_bk"],
                        inp["L0_attn_bv"]])[None, :])
    m["A_WoT"] = np.ascontiguousarray(np.asarray(inp["L0_attn_Wo"]).T)
    m["A_bo"] = np.ascontiguousarray(np.asarray(inp["L0_attn_bo"])[None, :])
    m["A_ropec"] = _T["ropeA_cos"]
    m["A_ropes"] = _T["ropeA_sin"]
    return m


def emit_D(P, nc, sb, x, y):
    Win = nc.dram_tensor("D_Win", [D, 64], F32, kind="ExternalInput").ap()
    bin_ = nc.dram_tensor("D_bin", [1, 64], F32, kind="ExternalInput").ap()
    W1r = nc.dram_tensor("D_W1r", [1, 1024], F32, kind="ExternalInput").ap()
    b1r = nc.dram_tensor("D_b1r", [1, 1024], F32, kind="ExternalInput").ap()
    W2r = nc.dram_tensor("D_W2r", [1, 1024], F32, kind="ExternalInput").ap()
    b2r = nc.dram_tensor("D_b2r", [1, 64], F32, kind="ExternalInput").ap()
    Wout = nc.dram_tensor("D_Wout", [64, D], F32, kind="ExternalInput").ap()
    bout = nc.dram_tensor("D_bout", [1, D], F32, kind="ExternalInput").ap()

    with ExitStack() as stk:
        E = stk.enter_context
        Wfs = E(nc.sbuf_tensor("D_Wfs", [128, 1024], F32))
        Winb = E(nc.sbuf_tensor("Winb", [128, 8 * 64], BF16))
        Woutb = E(nc.sbuf_tensor("Woutb", [64, D], BF16))
        binb = E(nc.sbuf_tensor("binb", [1, 64], BF16))
        boutb = E(nc.sbuf_tensor("boutb", [1, D], BF16))
        rowf = E(nc.sbuf_tensor("rowf", [1, 1024], F32))
        W1b = E(nc.sbuf_tensor("W1b", [128, 1024], F32))
        b1b = E(nc.sbuf_tensor("b1b", [128, 1024], F32))
        W2b = E(nc.sbuf_tensor("W2b", [128, 1024], F32))
        b2b = E(nc.sbuf_tensor("b2b", [128, 64], F32))
        xt = E(nc.sbuf_tensor("D_xt", [128, D], F32))
        lnt = E(nc.sbuf_tensor("D_lnt", [128, D], F32))
        lnb = E(nc.sbuf_tensor("D_lnb", [128, D], BF16))
        lnT = E(nc.sbuf_tensor("D_lnT", [128, 8 * 128], BF16))
        st_ = E(nc.sbuf_tensor("st_", [128, 64], F32))
        htmp = E(nc.sbuf_tensor("htmp", [128, 64], F32))
        gacc = E(nc.sbuf_tensor("gacc", [128, 64], F32))
        gb = E(nc.sbuf_tensor("gb", [128, 64], BF16))
        gT = E(nc.sbuf_tensor("gT", [64, 128], BF16))
        yt = E(nc.sbuf_tensor("D_yt", [128, D], F32))

        G = [sb[f"G{i}"] for i in range(6)]
        for kc in range(8):
            P.dma(Wfs[:, 0:64], Win[kc * 128:(kc + 1) * 128, :])
            P.v("tensor_copy", Winb[:, kc * 64:(kc + 1) * 64], Wfs[:, 0:64])
        P.dma(Wfs[0:64, 0:D], Wout)
        P.v("tensor_copy", Woutb[:, :], Wfs[0:64, 0:D])
        P.dma(rowf[:, 0:64], bin_)
        P.v("tensor_copy", binb[:, :], rowf[:, 0:64])
        P.dma(rowf[:, 0:D], bout)
        P.v("tensor_copy", boutb[:, :], rowf[:, 0:D])
        for src, dst in ((W1r, W1b), (b1r, b1b), (W2r, W2b)):
            P.dma(rowf[:, :], src)
            for n in range(2):
                P.mm(G[0][:, :], sb["onesfr"][:, :], rowf[:, n * 512:(n + 1) * 512],
                     start=True, stop=True)
                P.cp(dst[:, n * 512:(n + 1) * 512], G[0][:, :])
        P.dma(rowf[:, 0:64], b2r)
        P.mm(G[0][:, 0:64], sb["onesfr"][:, :], rowf[:, 0:64], start=True, stop=True)
        P.cp(b2b[:, :], G[0][:, 0:64])

        for b in range(NB):
            for i in range(NT):
                rr = b * S + i * 128
                P.dma(xt[:, :], x[rr:rr + 128, :])
                emit_ln(P, sb, lnt[:, :], xt[:, :], D)
                P.v("tensor_copy", lnb[:, :], lnt[:, :])
                emit_transpose_row(P, sb, lnb, lnT, 8)
                for kc in range(8):
                    P.mm(G[0][:, 0:64], lnT[:, kc * 128:(kc + 1) * 128],
                         Winb[:, kc * 64:(kc + 1) * 64], start=(kc == 0), stop=False)
                P.mm(G[0][:, 0:64], sb["onesb"][:, :], binb[:, :],
                     start=False, stop=True)
                P.cp(st_[:, :], G[0][:, 0:64])
                for ii in range(16):
                    P.v("tensor_mul", htmp[:, :], st_[:, :], W1b[:, ii * 64:(ii + 1) * 64])
                    P.v("tensor_add", htmp[:, :], htmp[:, :], b1b[:, ii * 64:(ii + 1) * 64])
                    P.v("tensor_scalar_max", htmp[:, :], htmp[:, :], 0.0)
                    P.v("tensor_mul", htmp[:, :], htmp[:, :], W2b[:, ii * 64:(ii + 1) * 64])
                    if ii == 0:
                        P.v("tensor_copy", gacc[:, :], htmp[:, :])
                    else:
                        P.v("tensor_add", gacc[:, :], gacc[:, :], htmp[:, :])
                P.v("tensor_add", gacc[:, :], gacc[:, :], b2b[:, :])
                P.v("tensor_copy", gb[:, :], gacc[:, :])
                P.T(sb["psT"][0:64, :], gb[:, :], sb["identb"][:, :])
                P.cp(gT[:, :], sb["psT"][0:64, :])
                for n in range(2):
                    P.mm(G[1 + n][:, :], gT[:, :], Woutb[:, n * 512:(n + 1) * 512],
                         start=True, stop=False)
                    P.mm(G[1 + n][:, :], sb["onesb"][:, :], boutb[:, n * 512:(n + 1) * 512],
                         start=False, stop=True)
                    P.v("tensor_add", yt[:, n * 512:(n + 1) * 512], G[1 + n][:, :],
                        xt[:, n * 512:(n + 1) * 512])
                P.dma(y[rr:rr + 128, :], yt[:, :])


def map_D(inp):
    m = {}
    m["D_Win"] = np.ascontiguousarray(inp["L1_ffn_Win"])
    m["D_bin"] = np.ascontiguousarray(np.asarray(inp["L1_ffn_bin_"])[None, :])
    m["D_W1r"] = np.ascontiguousarray(np.asarray(inp["L1_ffn_W1"]).T.reshape(1, 1024))
    m["D_b1r"] = np.ascontiguousarray(np.asarray(inp["L1_ffn_b1"]).T.reshape(1, 1024))
    m["D_W2r"] = np.ascontiguousarray(np.asarray(inp["L1_ffn_W2"]).T.reshape(1, 1024))
    m["D_b2r"] = np.ascontiguousarray(np.asarray(inp["L1_ffn_b2"])[None, :])
    m["D_Wout"] = np.ascontiguousarray(inp["L1_ffn_Wout"])
    m["D_bout"] = np.ascontiguousarray(np.asarray(inp["L1_ffn_bout"])[None, :])
    return m


def emit_C(P, nc, sb, x, y):
    W = nc.dram_tensor("C_W", [D, 5 * D], F32, kind="ExternalInput").ap()
    Wo = nc.dram_tensor("C_Wo", [D, D], F32, kind="ExternalInput").ap()
    lvec = nc.dram_tensor("C_lvec", [64, 4], F32, kind="ExternalInput").ap()
    lnwT = nc.dram_tensor("C_lnwT", [64, 16], F32, kind="ExternalInput").ap()
    lnbT = nc.dram_tensor("C_lnbT", [64, 16], F32, kind="ExternalInput").ap()
    Kb = nc.dram_tensor("C_Kb", [2, S], F32, kind="ExternalInput").ap()
    Qb = nc.dram_tensor("C_Qb", [2, 16 * S], F32, kind="ExternalInput").ap()
    CT = nc.dram_tensor("C_CT", [16, 128, 512], F32, kind="ExternalInput").ap()
    pT_h = nc.dram_tensor("C_pT_h", [4 * D, S], BF16, kind="Internal").ap()
    v_h = nc.dram_tensor("C_v_h", [S, D], BF16, kind="Internal").ap()
    oT_h = nc.dram_tensor("C_oT_h", [D, S], BF16, kind="Internal").ap()

    with ExitStack() as stk:
        E = stk.enter_context
        Wsb = E(nc.sbuf_tensor("C_Wsb", [128, 8 * 5120], BF16))
        Wfs = E(nc.sbuf_tensor("C_Wfs", [128, 5120], F32))
        WoSb = E(nc.sbuf_tensor("WoSb", [128, 8 * D], BF16))
        Kbs = E(nc.sbuf_tensor("Kbs", [2, S], F32))
        Qhs = E(nc.sbuf_tensor("Qhs", [2, S], F32))
        CTs = E(nc.sbuf_tensor("CTs", [128, 512], F32))
        lnwS = E(nc.sbuf_tensor("lnwS", [64, 16], F32))
        lnbS = E(nc.sbuf_tensor("lnbS", [64, 16], F32))
        lvs = E(nc.sbuf_tensor("lvs", [64, 4], F32))
        neglam = E(nc.sbuf_tensor("neglam", [1, 4], F32))
        xt = E(nc.sbuf_tensor("C_xt", [128, D], F32))
        lnt = E(nc.sbuf_tensor("C_lnt", [128, D], F32))
        lnb = E(nc.sbuf_tensor("C_lnb", [128, D], BF16))
        lnT = E(nc.sbuf_tensor("C_lnT", [128, 8 * 128], BF16))
        pstage = E(nc.sbuf_tensor("C_pstage", [128, 512], BF16))
        tstage = E(nc.sbuf_tensor("C_tstage", [128, 4 * 128], BF16))
        vb = E(nc.sbuf_tensor("C_vb", [128, D], BF16))
        q0sb = E(nc.sbuf_tensor("q0sb", [64, S], BF16))
        q1sb = E(nc.sbuf_tensor("q1sb", [64, S], BF16))
        k0sb = E(nc.sbuf_tensor("k0sb", [64, S], BF16))
        k1sb = E(nc.sbuf_tensor("k1sb", [64, S], BF16))
        vg = E(nc.sbuf_tensor("C_vg", [128, 16 * 65], BF16))
        ssb = E(nc.sbuf_tensor("ssb", [128, 512], F32))
        expb = E(nc.sbuf_tensor("C_expb", [128, 512], BF16))
        oT0 = E(nc.sbuf_tensor("oT0", [65, 512], F32))
        oT1 = E(nc.sbuf_tensor("oT1", [65, 512], F32))
        r0 = E(nc.sbuf_tensor("C_r0", [1, 512], F32))
        r1 = E(nc.sbuf_tensor("r1", [1, 512], F32))
        rb0 = E(nc.sbuf_tensor("rb0", [64, 512], F32))
        rb1 = E(nc.sbuf_tensor("rb1", [64, 512], F32))
        hsb = E(nc.sbuf_tensor("hsb", [64, 512], F32))
        hscr = E(nc.sbuf_tensor("hscr", [64, 512], F32))
        rowa = E(nc.sbuf_tensor("rowa", [1, 512], F32))
        rowb = E(nc.sbuf_tensor("rowb", [1, 512], F32))
        mub = E(nc.sbuf_tensor("mub", [64, 512], F32))
        oTf = E(nc.sbuf_tensor("C_oTf", [64, 512], BF16))
        oTl = E(nc.sbuf_tensor("C_oTl", [128, 8 * 128], BF16))
        yt = E(nc.sbuf_tensor("C_yt", [128, D], F32))

        for kc in range(8):
            P.dma(Wfs[:, :], W[kc * 128:(kc + 1) * 128, :])
            P.v("tensor_copy", Wsb[:, kc * 5120:(kc + 1) * 5120], Wfs[:, :])
        for kc in range(8):
            P.dma(Wfs[:, 0:D], Wo[kc * 128:(kc + 1) * 128, :])
            P.v("tensor_copy", WoSb[:, kc * D:(kc + 1) * D], Wfs[:, 0:D])
        P.dma(Kbs[:, :], Kb)
        P.dma(lnwS[:, :], lnwT)
        P.dma(lnbS[:, :], lnbT)
        P.dma(lvs[:, :], lvec)
        P.mm(sb["G0"][0:1, 0:1], lvs[:, 0:1], lvs[:, 1:2], start=True, stop=True)
        P.mm(sb["G0"][0:1, 1:2], lvs[:, 2:3], lvs[:, 3:4], start=True, stop=True)
        P.cp(neglam[:, 0:2], sb["G0"][0:1, 0:2])
        P.act(neglam[:, 2:4], neglam[:, 0:2], AF.Exp)
        P.v("tensor_add", neglam[:, 0:1], neglam[:, 2:3], neglam[:, 3:4])
        P.v("tensor_scalar_add", neglam[:, 0:1], neglam[:, 0:1], -1.6)

        G = [sb[f"G{i}"] for i in range(6)]
        vgv = vg[:, :].rearrange("p (t c) -> p t c", c=65)
        SC = 0.125 ** 0.5
        for b in range(NB):
            r0b = b * S
            for i in range(NT):
                rr = r0b + i * 128
                P.dma(xt[:, :], x[rr:rr + 128, :])
                emit_ln(P, sb, lnt[:, :], xt[:, :], D)
                P.v("tensor_copy", lnb[:, :], lnt[:, :])
                emit_transpose_row(P, sb, lnb, lnT, 8)
                for half in range(2):
                    for kc in range(8):
                        for n in range(5):
                            nn = half * 5 + n
                            P.mm(G[n][:, :], lnT[:, kc * 128:(kc + 1) * 128],
                                 Wsb[:, kc * 5120 + nn * 512: kc * 5120 + (nn + 1) * 512],
                                 start=(kc == 0), stop=(kc == 7))
                    for n in range(5):
                        nn = half * 5 + n
                        proj = nn // 2
                        if proj < 4:
                            P.v("tensor_scalar", pstage[:, :], G[n][:, :], SC, None,
                                ALU.mult)
                            for hp in range(4):
                                P.T(sb["psT"][:, :], pstage[:, hp * 128:(hp + 1) * 128],
                                    sb["identb"][:, :])
                                P.cp(tstage[:, hp * 128:(hp + 1) * 128], sb["psT"][:, :])
                            ro = proj * D + (nn % 2) * 512
                            P.dma(pT_h[ro:ro + 512, i * 128:(i + 1) * 128]
                                  .rearrange("(a p) c -> p a c", p=128),
                                  tstage[:, :].rearrange("p (a c) -> p a c", c=128))
                        else:
                            P.v("tensor_copy", vb[:, (nn % 2) * 512:(nn % 2) * 512 + 512],
                                G[n][:, :])
                P.dma(v_h[i * 128:(i + 1) * 128, :], vb[:, :])
            for h in range(16):
                P.dma(q0sb[:, :], pT_h[0 * D + h * 64: 0 * D + (h + 1) * 64, :])
                P.dma(q1sb[:, :], pT_h[1 * D + h * 64: 1 * D + (h + 1) * 64, :])
                P.dma(k0sb[:, :], pT_h[2 * D + h * 64: 2 * D + (h + 1) * 64, :])
                P.dma(k1sb[:, :], pT_h[3 * D + h * 64: 3 * D + (h + 1) * 64, :])
                P.dma(Qhs[:, :], Qb[:, h * S:(h + 1) * S])
                P.dma(vgv[:, :, 0:64],
                      v_h[:, h * 64:(h + 1) * 64].rearrange("(t p) c -> p t c", p=128))
                P.v("memset", vgv[:, :, 64:65], 1.0)
                for qc in range(4):
                    nkt = 4 * (qc + 1)
                    for kt in range(nkt):
                        diag = (kt // 4 == qc)
                        if diag:
                            P.dma(CTs[:, :], CT[kt, :, :])
                        for (qq, kk, gi) in ((q0sb, k0sb, 1), (q1sb, k1sb, 2)):
                            P.mm(G[0][:, :], kk[:, kt * 128:(kt + 1) * 128],
                                 qq[:, qc * 512:(qc + 1) * 512], start=True, stop=False)
                            P.mm(G[0][:, :], Kbs[:, kt * 128:(kt + 1) * 128],
                                 Qhs[:, qc * 512:(qc + 1) * 512], start=False, stop=True)
                            if diag:
                                P.v("tensor_add", ssb[:, :], G[0][:, :], CTs[:, :])
                                P.act(expb[:, :], ssb[:, :], AF.Exp)
                            else:
                                P.act(expb[:, :], G[0][:, :], AF.Exp)
                            P.mm(G[gi][0:65, :], vgv[:, kt, :], expb[:, :],
                                 start=(kt == 0), stop=(kt == nkt - 1))
                    P.cp(oT0[:, :], G[1][0:65, :])
                    P.cp(oT1[:, :], G[2][0:65, :])
                    P.dma(r0[:, :], oT0[64:65, :])
                    P.dma(r1[:, :], oT1[64:65, :])
                    P.v("reciprocal", r0[:, :], r0[:, :])
                    P.v("reciprocal", r1[:, :], r1[:, :])
                    P.v("tensor_scalar_mul", r1[:, :], r1[:, :], neglam[:, 0:1])
                    P.mm(G[0][0:64, :], sb["onesfr"][:, 0:64], r0[:, :],
                         start=True, stop=True)
                    P.cp(rb0[:, :], G[0][0:64, :])
                    P.mm(G[0][0:64, :], sb["onesfr"][:, 0:64], r1[:, :],
                         start=True, stop=True)
                    P.cp(rb1[:, :], G[0][0:64, :])
                    P.v("tensor_mul", hsb[:, :], oT0[0:64, :], rb0[:, :])
                    P.v("tensor_mul", hscr[:, :], oT1[0:64, :], rb1[:, :])
                    P.v("tensor_add", hsb[:, :], hsb[:, :], hscr[:, :])
                    P.mm(G[0][0:1, :], sb["onesfc"][0:64, :], hsb[:, :],
                         start=True, stop=True)
                    P.act(hscr[:, :], hsb[:, :], AF.Square)
                    P.mm(G[3][0:1, :], sb["onesfc"][0:64, :], hscr[:, :],
                         start=True, stop=True)
                    P.cp(rowa[:, :], G[0][0:1, :])
                    P.cp(rowb[:, :], G[3][0:1, :])
                    P.v("tensor_scalar_mul", rowa[:, :], rowa[:, :], 1.0 / 64)
                    P.v("tensor_scalar_mul", rowb[:, :], rowb[:, :], 1.0 / 64)
                    P.v("tensor_mul", r0[:, :], rowa[:, :], rowa[:, :])
                    P.v("tensor_sub", rowb[:, :], rowb[:, :], r0[:, :])
                    P.v("tensor_scalar_add", rowb[:, :], rowb[:, :], 1e-5)
                    P.act(r0[:, :], rowb[:, :], AF.Ln)
                    P.v("tensor_scalar_mul", r0[:, :], r0[:, :], -0.5)
                    P.act(rowb[:, :], r0[:, :], AF.Exp)
                    P.mm(G[0][0:64, :], sb["onesfr"][:, 0:64], rowa[:, :],
                         start=True, stop=True)
                    P.cp(mub[:, :], G[0][0:64, :])
                    P.mm(G[0][0:64, :], sb["onesfr"][:, 0:64], rowb[:, :],
                         start=True, stop=True)
                    P.cp(rb0[:, :], G[0][0:64, :])
                    P.v("tensor_sub", hsb[:, :], hsb[:, :], mub[:, :])
                    P.v("tensor_mul", hsb[:, :], hsb[:, :], rb0[:, :])
                    P.v("tensor_scalar", hsb[:, :], hsb[:, :], lnwS[:, h:h + 1],
                        lnbS[:, h:h + 1], ALU.mult, ALU.add)
                    P.v("tensor_copy", oTf[:, :], hsb[:, :])
                    P.dma(oT_h[h * 64:(h + 1) * 64, qc * 512:(qc + 1) * 512], oTf[:, :])
            for i in range(NT):
                rr = r0b + i * 128
                P.dma(oTl[:, :].rearrange("p (a c) -> p a c", c=128),
                      oT_h[:, i * 128:(i + 1) * 128].rearrange("(a p) c -> p a c", p=128))
                for kc in range(8):
                    for n in range(2):
                        P.mm(G[n][:, :], oTl[:, kc * 128:(kc + 1) * 128],
                             WoSb[:, kc * D + n * 512: kc * D + (n + 1) * 512],
                             start=(kc == 0), stop=(kc == 7))
                P.dma(xt[:, :], x[rr:rr + 128, :])
                for n in range(2):
                    P.v("tensor_add", yt[:, n * 512:(n + 1) * 512], G[n][:, :],
                        xt[:, n * 512:(n + 1) * 512])
                P.dma(y[rr:rr + 128, :], yt[:, :])


def map_C(inp):
    m = {}
    m["C_W"] = np.ascontiguousarray(
        np.concatenate([inp["L1_attn_wq0"], inp["L1_attn_wq1"],
                        inp["L1_attn_wk0"], inp["L1_attn_wk1"],
                        inp["L1_attn_wv"]], axis=1))
    m["C_Wo"] = np.ascontiguousarray(inp["L1_attn_wo"])
    m["C_lvec"] = np.ascontiguousarray(
        np.stack([inp["L1_attn_l0"], inp["L1_attn_l1"],
                  inp["L1_attn_l2"], inp["L1_attn_l3"]], axis=1))
    m["C_lnwT"] = np.ascontiguousarray(np.asarray(inp["L1_attn_ln_w"]).T)
    m["C_lnbT"] = np.ascontiguousarray(np.asarray(inp["L1_attn_ln_b"]).T)
    m["C_Kb"] = _T["C_Kb"]
    m["C_Qb"] = _T["C_Qb"]
    m["C_CT"] = _T["C_CT"]
    return m


def emit_B(P, nc, sb, x, y):
    wqkv = nc.dram_tensor("B_wqkv", [2, D, 1536], F32, kind="ExternalInput").ap()
    qsc = nc.dram_tensor("B_qsc", [2, 1, 512], F32, kind="ExternalInput").ap()
    ksc = nc.dram_tensor("B_ksc", [2, 1, 512], F32, kind="ExternalInput").ap()
    wo = nc.dram_tensor("B_wo", [2, 512, D], F32, kind="ExternalInput").ap()
    lng = nc.dram_tensor("B_lng", [2, 1, D], F32, kind="ExternalInput").ap()
    lnbi = nc.dram_tensor("B_lnbi", [2, 1, D], F32, kind="ExternalInput").ap()
    w1 = nc.dram_tensor("B_w1", [2, D, 5460], F32, kind="ExternalInput").ap()
    b1 = nc.dram_tensor("B_b1", [2, 1, 5460], F32, kind="ExternalInput").ap()
    w2 = nc.dram_tensor("B_w2", [2, 2730, D], F32, kind="ExternalInput").ap()
    b2 = nc.dram_tensor("B_b2", [2, 1, D], F32, kind="ExternalInput").ap()
    xc_ = nc.dram_tensor("B_xc", [S, 512], F32, kind="ExternalInput").ap()
    xs_ = nc.dram_tensor("B_xs", [S, 512], F32, kind="ExternalInput").ap()
    xck = nc.dram_tensor("B_xck", [S, 512], F32, kind="ExternalInput").ap()
    xsk = nc.dram_tensor("B_xsk", [S, 512], F32, kind="ExternalInput").ap()
    mB0 = nc.dram_tensor("B_mB0", [512, 512], BF16, kind="ExternalInput").ap()
    mB = nc.dram_tensor("B_mB", [512, 1024], BF16, kind="ExternalInput").ap()
    mB0T = nc.dram_tensor("B_mB0T", [512, 512], BF16, kind="ExternalInput").ap()
    mBT = nc.dram_tensor("B_mBT", [1024, 512], BF16, kind="ExternalInput").ap()
    z_h = nc.dram_tensor("B_z_h", [NB * S, D], F32, kind="Internal").ap()
    z2_h = nc.dram_tensor("B_z2_h", [NB * S, D], F32, kind="Internal").ap()
    h_h = nc.dram_tensor("B_h_h", [NB * S, D], BF16, kind="Internal").ap()
    qT_h = nc.dram_tensor("B_qT_h", [512, NB * S], BF16, kind="Internal").ap()
    kT_h = nc.dram_tensor("B_kT_h", [512, NB * S], BF16, kind="Internal").ap()
    v_h = nc.dram_tensor("B_v_h", [NB * S, 512], BF16, kind="Internal").ap()
    oT_h = nc.dram_tensor("B_oT_h", [512, NB * S], BF16, kind="Internal").ap()
    gT_h = nc.dram_tensor("B_gT_h", [2730, NB * S], BF16, kind="Internal").ap()

    with ExitStack() as stk:
        E = stk.enter_context
        wA = E(nc.sbuf_tensor("wA", [128, 8 * 5460], BF16))
        Wfs = E(nc.sbuf_tensor("B_Wfs", [128, 1536], F32))
        b1S = E(nc.sbuf_tensor("b1S", [1, 5460], BF16))
        b2S = E(nc.sbuf_tensor("b2S", [1, D], BF16))
        rowf = E(nc.sbuf_tensor("B_rowf", [1, 1536], F32))
        qscB = E(nc.sbuf_tensor("qscB", [128, 512], F32))
        kscB = E(nc.sbuf_tensor("kscB", [128, 512], F32))
        lngB = E(nc.sbuf_tensor("lngB", [128, D], F32))
        lnbB = E(nc.sbuf_tensor("lnbB", [128, D], F32))
        xt = E(nc.sbuf_tensor("B_xt", [128, D], F32))
        ht = E(nc.sbuf_tensor("B_ht", [128, D], F32))
        lnt = E(nc.sbuf_tensor("B_lnt", [128, D], F32))
        lnb = E(nc.sbuf_tensor("B_lnb", [128, D], BF16))
        lnT = E(nc.sbuf_tensor("B_lnT", [128, 8 * 128], BF16))
        cost = E(nc.sbuf_tensor("B_cost", [128, 512], F32))
        sint = E(nc.sbuf_tensor("B_sint", [128, 512], F32))
        qs0 = E(nc.sbuf_tensor("B_qs0", [128, 512], F32))
        nrm = E(nc.sbuf_tensor("B_nrm", [128, 16], F32))
        qn = E(nc.sbuf_tensor("B_qn", [128, 512], F32))
        qa = E(nc.sbuf_tensor("B_qa", [128, 512], F32))
        shq = E(nc.sbuf_tensor("B_shq", [128, 512], F32))
        qf = E(nc.sbuf_tensor("B_qf", [128, 512], F32))
        qfb = E(nc.sbuf_tensor("B_qfb", [128, 512], BF16))
        tstage = E(nc.sbuf_tensor("B_tstage", [128, 4 * 128], BF16))
        vb = E(nc.sbuf_tensor("B_vb", [128, 512], BF16))
        maskP = E(nc.sbuf_tensor("maskP", [128, 4 * 1024], BF16))
        maskT = E(nc.sbuf_tensor("maskT", [128, 8 * 512], BF16))
        qsb = E(nc.sbuf_tensor("B_qsb", [64, 512], BF16))
        ksb = E(nc.sbuf_tensor("B_ksb", [64, 1024], BF16))
        qhat = E(nc.sbuf_tensor("qhat", [65, 512], BF16))
        khat = E(nc.sbuf_tensor("khat", [65, 1024], BF16))
        vg = E(nc.sbuf_tensor("B_vg", [128, 8 * 65], BF16))
        ssb = sb["lnscr"]
        nmx = E(nc.sbuf_tensor("B_nmx", [128, 1], F32))
        nmr = E(nc.sbuf_tensor("B_nmr", [1, 128], F32))
        nmrb = E(nc.sbuf_tensor("B_nmrb", [1, 128], BF16))
        expb = E(nc.sbuf_tensor("B_expb", [128, 512], BF16))
        oTsb = E(nc.sbuf_tensor("B_oTsb", [65, 512], F32))
        r0 = E(nc.sbuf_tensor("B_r0", [1, 512], F32))
        rb = E(nc.sbuf_tensor("B_rb", [64, 512], F32))
        oTf = E(nc.sbuf_tensor("B_oTf", [64, 512], BF16))
        oTl = E(nc.sbuf_tensor("B_oTl", [128, 4 * 128], BF16))
        hb = E(nc.sbuf_tensor("B_hb", [128, D], BF16))
        gel = E(nc.sbuf_tensor("B_gel", [128, 512], F32))
        prodb = E(nc.sbuf_tensor("B_prodb", [128, 512], BF16))
        gTl = E(nc.sbuf_tensor("gTl", [128, 21 * 128], BF16))
        gTll = E(nc.sbuf_tensor("gTll", [42, 128], BF16))
        yt = lnt

        G = [sb[f"G{i}"] for i in range(6)]
        vgv = vg[:, :].rearrange("p (t c) -> p t c", c=65)

        def conv_load(dst_bf, src_ap, rows, cols):
            c0 = 0
            while c0 < cols:
                cw = min(1536, cols - c0)
                P.dma(Wfs[0:rows, 0:cw], src_ap[:, c0:c0 + cw] if rows == 128
                      else src_ap[0:rows, c0:c0 + cw])
                P.v("tensor_copy", dst_bf[0:rows, c0:c0 + cw], Wfs[0:rows, 0:cw])
                c0 += cw

        def bcast_row(src_l_ap, dst, w_):
            c0 = 0
            while c0 < w_:
                cw = min(512, w_ - c0)
                P.dma(rowf[:, 0:cw], src_l_ap[:, c0:c0 + cw])
                P.mm(G[0][:, 0:cw], sb["onesfr"][:, :], rowf[:, 0:cw],
                     start=True, stop=True)
                P.cp(dst[:, c0:c0 + cw], G[0][:, 0:cw])
                c0 += cw

        def shifted_load(dst, srcbuf, rr, i):
            P.dma(dst[:, 0:512], srcbuf[rr:rr + 128, 0:512])
            if i == 0:
                P.v("memset", dst[0:1, 512:1024], 0.0)
                P.dma(dst[1:128, 512:1024], srcbuf[rr:rr + 127, 512:1024])
            else:
                P.dma(dst[:, 512:1024], srcbuf[rr - 1:rr + 127, 512:1024])

        for b in range(NB):
            for i in range(NT):
                rr = b * S + i * 128
                P.dma(xt[:, :], x[rr:rr + 128, :])
                emit_ln(P, sb, lnt[:, :], xt[:, :], D)
                P.dma(z_h[rr:rr + 128, :], lnt[:, :])

        for l in range(2):
            for kc in range(8):
                conv_load(wA[:, kc * 1536:(kc + 1) * 1536], wqkv[l, kc * 128:(kc + 1) * 128, :],
                          128, 1536)
            bcast_row(qsc[l], qscB, 512)
            bcast_row(ksc[l], kscB, 512)
            for b in range(NB):
                r0b = b * S
                for i in range(NT):
                    rr = r0b + i * 128
                    shifted_load(ht, z_h, rr, i)
                    P.v("tensor_copy", lnb[:, :], ht[:, :])
                    emit_transpose_row(P, sb, lnb, lnT, 8)
                    for kc in range(8):
                        for n in range(3):
                            P.mm(G[n][:, :], lnT[:, kc * 128:(kc + 1) * 128],
                                 wA[:, kc * 1536 + n * 512: kc * 1536 + (n + 1) * 512],
                                 start=(kc == 0), stop=(kc == 7))
                    for (gi, scB, cO, sO, Tdst) in ((0, qscB, xc_, xs_, qT_h),
                                                    (1, kscB, xck, xsk, kT_h)):
                        P.cp(qs0[:, :], G[gi][:, :])
                        for hh in range(8):
                            P.act(sb["lnscr"][:, hh * 64:(hh + 1) * 64],
                                  qs0[:, hh * 64:(hh + 1) * 64], AF.Square,
                                  accum_out=nrm[:, hh:hh + 1])
                        P.act(nrm[:, 0:8], nrm[:, 0:8], AF.Sqrt)
                        P.v("tensor_scalar_max", nrm[:, 0:8], nrm[:, 0:8], 1e-12)
                        P.v("reciprocal", nrm[:, 8:16], nrm[:, 0:8])
                        for hh in range(8):
                            P.v("tensor_scalar_mul", qn[:, hh * 64:(hh + 1) * 64],
                                qs0[:, hh * 64:(hh + 1) * 64], nrm[:, 8 + hh:9 + hh])
                        P.v("tensor_mul", qn[:, :], qn[:, :], scB[:, :])
                        P.dma(cost[:, :], cO[i * 128:(i + 1) * 128, :])
                        P.dma(sint[:, :], sO[i * 128:(i + 1) * 128, :])
                        P.v("tensor_mul", qa[:, :], qn[:, :], cost[:, :])
                        qv = qn[:, :].rearrange("p (h a c) -> p h a c", a=2, c=32)
                        sv = shq[:, :].rearrange("p (h a c) -> p h a c", a=2, c=32)
                        P.v("tensor_copy", sv[:, :, 0, :], qv[:, :, 1, :])
                        P.v("tensor_copy", sv[:, :, 1, :], qv[:, :, 0, :])
                        P.v("tensor_mul", shq[:, :], shq[:, :], sint[:, :])
                        P.v("tensor_add", qf[:, :], qa[:, :], shq[:, :])
                        P.v("tensor_copy", qfb[:, :], qf[:, :])
                        for hp in range(4):
                            P.T(sb["psT"][:, :], qfb[:, hp * 128:(hp + 1) * 128],
                                sb["identb"][:, :])
                            P.cp(tstage[:, hp * 128:(hp + 1) * 128], sb["psT"][:, :])
                        P.dma(Tdst[:, rr:rr + 128].rearrange("(a p) c -> p a c", p=128),
                              tstage[:, :].rearrange("p (a c) -> p a c", c=128))
                    P.v("tensor_copy", vb[:, :], G[2][:, :])
                    P.dma(v_h[rr:rr + 128, :], vb[:, :])
            for w in range(4):
                kw = 512 if w == 0 else 1024
                nkt = kw // 128
                if w == 0:
                    P.dma(maskP[:, 0:4 * 512].rearrange("p (a c) -> p a c", c=512),
                          mB0.rearrange("(a p) c -> p a c", p=128))
                    P.dma(maskT[:, 0:4 * 512].rearrange("p (a c) -> p a c", c=512),
                          mB0T.rearrange("(a p) c -> p a c", p=128))
                elif w == 1:
                    P.dma(maskP[:, :].rearrange("p (a c) -> p a c", c=1024),
                          mB.rearrange("(a p) c -> p a c", p=128))
                    P.dma(maskT[:, :].rearrange("p (a c) -> p a c", c=512),
                          mBT.rearrange("(a p) c -> p a c", p=128))
                for b in range(NB):
                    r0b = b * S
                    k0g = r0b + (w * 512 - (0 if w == 0 else 512))
                    q0g = r0b + w * 512
                    for h in range(8):
                        P.dma(qsb[:, :], qT_h[h * 64:(h + 1) * 64, q0g:q0g + 512])
                        P.dma(ksb[:, 0:kw], kT_h[h * 64:(h + 1) * 64, k0g:k0g + kw])
                        P.dma(vg[:, 0:nkt * 65].rearrange("p (t c) -> p t c", c=65)[:, :, 0:64],
                              v_h[k0g:k0g + kw, h * 64:(h + 1) * 64]
                              .rearrange("(t p) c -> p t c", p=128))
                        P.v("memset",
                            vg[:, 0:nkt * 65].rearrange("p (t c) -> p t c", c=65)[:, :, 64:65],
                            1.0)
                        P.v("tensor_copy", qhat[0:64, :], qsb[:, :])
                        P.v("tensor_copy", khat[0:64, 0:kw], ksb[:, 0:kw])
                        P.v("memset", khat[64:65, 0:kw], 1.0)
                        for t in range(4):
                            for n in range(kw // 512):
                                P.mm(G[n][:, :], qsb[:, t * 128:(t + 1) * 128],
                                     ksb[:, n * 512:(n + 1) * 512], start=True, stop=True)
                            for n in range(kw // 512):
                                P.v("tensor_add", ssb[:, n * 512:(n + 1) * 512],
                                    G[n][:, :],
                                    maskP[:, t * kw + n * 512: t * kw + (n + 1) * 512])
                            P.v("reduce_max", nmx[:, :], ssb[:, 0:kw], axis=AX.X,
                                negate=True)
                            P.T(sb["psTf"][0:1, :], nmx[:, :], sb["identf"][:, :])
                            P.cp(nmr[:, :], sb["psTf"][0:1, :])
                            P.v("tensor_copy", nmrb[:, :], nmr[:, :])
                            P.dma(qhat[64:65, t * 128:(t + 1) * 128], nmrb[:, :])
                        for kt in range(nkt):
                            P.mm(G[0][:, :], khat[:, kt * 128:(kt + 1) * 128],
                                 qhat[:, :], start=True, stop=True)
                            P.v("tensor_add", ssb[:, 0:512], G[0][:, :],
                                maskT[:, kt * 512:(kt + 1) * 512])
                            P.act(expb[:, :], ssb[:, 0:512], AF.Exp)
                            P.mm(G[1][0:65, :],
                                 vg[:, 0:nkt * 65].rearrange("p (t c) -> p t c", c=65)[:, kt, :],
                                 expb[:, :], start=(kt == 0), stop=(kt == nkt - 1))
                        P.cp(oTsb[:, :], G[1][0:65, :])
                        P.dma(r0[:, :], oTsb[64:65, :])
                        P.v("reciprocal", r0[:, :], r0[:, :])
                        P.mm(G[0][0:64, :], sb["onesfr"][:, 0:64], r0[:, :],
                             start=True, stop=True)
                        P.cp(rb[:, :], G[0][0:64, :])
                        P.v("tensor_mul", oTf[:, :], oTsb[0:64, :], rb[:, :])
                        P.dma(oT_h[h * 64:(h + 1) * 64, q0g:q0g + 512], oTf[:, :])
            for kc in range(4):
                conv_load(wA[:, kc * D:(kc + 1) * D], wo[l, kc * 128:(kc + 1) * 128, :],
                          128, D)
            for b in range(NB):
                r0b = b * S
                for i in range(NT):
                    rr = r0b + i * 128
                    P.dma(oTl[:, :].rearrange("p (a c) -> p a c", c=128),
                          oT_h[:, rr:rr + 128].rearrange("(a p) c -> p a c", p=128))
                    for kc in range(4):
                        for n in range(2):
                            P.mm(G[n][:, :], oTl[:, kc * 128:(kc + 1) * 128],
                                 wA[:, kc * D + n * 512: kc * D + (n + 1) * 512],
                                 start=(kc == 0), stop=(kc == 3))
                    P.dma(xt[:, :], z_h[rr:rr + 128, :])
                    for n in range(2):
                        P.v("tensor_add", yt[:, n * 512:(n + 1) * 512], G[n][:, :],
                            xt[:, n * 512:(n + 1) * 512])
                    P.dma(z2_h[rr:rr + 128, :], yt[:, :])
            bcast_row(lng[l], lngB, D)
            bcast_row(lnbi[l], lnbB, D)
            for b in range(NB):
                r0b = b * S
                for i in range(NT):
                    rr = r0b + i * 128
                    shifted_load(ht, z2_h, rr, i)
                    emit_ln(P, sb, lnt[:, :], ht[:, :], D, gain=lngB[:, :],
                            bias_t=lnbB[:, :])
                    P.v("tensor_copy", hb[:, :], lnt[:, :])
                    P.dma(h_h[rr:rr + 128, :], hb[:, :])
            for kc in range(8):
                conv_load(wA[:, kc * 5460:(kc + 1) * 5460],
                          w1[l, kc * 128:(kc + 1) * 128, :], 128, 5460)
            conv_load(b1S, b1[l], 1, 5460)
            for b in range(NB):
                r0b = b * S
                for i in range(NT):
                    rr = r0b + i * 128
                    P.dma(hb[:, :], h_h[rr:rr + 128, :])
                    emit_transpose_row(P, sb, hb, lnT, 8)
                    for c in range(6):
                        c0 = c * 512
                        cw = min(512, 2730 - c0)
                        for kc in range(8):
                            P.mm(G[0][:, 0:cw], lnT[:, kc * 128:(kc + 1) * 128],
                                 wA[:, kc * 5460 + c0: kc * 5460 + c0 + cw],
                                 start=(kc == 0), stop=False)
                            P.mm(G[1][:, 0:cw], lnT[:, kc * 128:(kc + 1) * 128],
                                 wA[:, kc * 5460 + 2730 + c0: kc * 5460 + 2730 + c0 + cw],
                                 start=(kc == 0), stop=False)
                        P.mm(G[0][:, 0:cw], sb["onesb"][:, :], b1S[:, c0:c0 + cw],
                             start=False, stop=True)
                        P.mm(G[1][:, 0:cw], sb["onesb"][:, :],
                             b1S[:, 2730 + c0:2730 + c0 + cw], start=False, stop=True)
                        P.act(gel[:, 0:cw], G[1][:, 0:cw], AF.Gelu)
                        P.v("tensor_mul", prodb[:, 0:cw], G[0][:, 0:cw], gel[:, 0:cw])
                        nsub = (cw + 127) // 128
                        for tsub in range(nsub):
                            tw = min(128, cw - tsub * 128)
                            P.T(sb["psT"][0:tw, :], prodb[:, tsub * 128:tsub * 128 + tw],
                                sb["identb"][:, :])
                            P.cp(tstage[0:tw, tsub * 128:(tsub + 1) * 128],
                                 sb["psT"][0:tw, :])
                            P.dma(gT_h[c0 + tsub * 128: c0 + tsub * 128 + tw, rr:rr + 128],
                                  tstage[0:tw, tsub * 128:(tsub + 1) * 128])
            for kc in range(22):
                r = kc * 128
                nr = min(128, 2730 - r)
                conv_load(wA[:, kc * D:(kc + 1) * D], w2[l, r:r + nr, :], nr, D)
            conv_load(b2S, b2[l], 1, D)
            for b in range(NB):
                r0b = b * S
                for i in range(NT):
                    rr = r0b + i * 128
                    P.dma(gTl[:, :].rearrange("p (a c) -> p a c", c=128),
                          gT_h[0:2688, rr:rr + 128].rearrange("(a p) c -> p a c", p=128))
                    P.dma(gTll[:, :], gT_h[2688:2730, rr:rr + 128])
                    for kc in range(22):
                        nr = min(128, 2730 - kc * 128)
                        lhs = (gTl[:, kc * 128:(kc + 1) * 128] if kc < 21
                               else gTll[:, :])
                        for n in range(2):
                            P.mm(G[2 + n][:, :], lhs,
                                 wA[0:nr, kc * D + n * 512: kc * D + (n + 1) * 512],
                                 start=(kc == 0), stop=False)
                    for n in range(2):
                        P.mm(G[2 + n][:, :], sb["onesb"][:, :],
                             b2S[:, n * 512:(n + 1) * 512], start=False, stop=True)
                    P.dma(xt[:, :], z2_h[rr:rr + 128, :])
                    for n in range(2):
                        P.v("tensor_add", yt[:, n * 512:(n + 1) * 512], G[2 + n][:, :],
                            xt[:, n * 512:(n + 1) * 512])
                    if l == 1:
                        P.dma(xt[:, :], x[rr:rr + 128, :])
                        P.v("tensor_add", yt[:, :], yt[:, :], xt[:, :])
                        P.dma(y[rr:rr + 128, :], yt[:, :])
                    else:
                        P.dma(z_h[rr:rr + 128, :], yt[:, :])


def map_B(inp):
    m = {}
    m["B_wqkv"] = np.ascontiguousarray(inp["L0_ffn_wqkv"])
    m["B_qsc"] = np.ascontiguousarray(
        np.stack([np.tile(np.asarray(inp["L0_ffn_q_scale"])[l], 8)[None, :]
                  for l in range(2)]))
    m["B_ksc"] = np.ascontiguousarray(
        np.stack([np.tile(np.asarray(inp["L0_ffn_k_scale"])[l], 8)[None, :]
                  for l in range(2)]))
    m["B_wo"] = np.ascontiguousarray(inp["L0_ffn_wo"])
    m["B_lng"] = np.ascontiguousarray(np.asarray(inp["L0_ffn_ln_g"])[:, None, :])
    m["B_lnbi"] = np.ascontiguousarray(np.asarray(inp["L0_ffn_ln_b"])[:, None, :])
    m["B_w1"] = np.ascontiguousarray(inp["L0_ffn_w1"])
    m["B_b1"] = np.ascontiguousarray(np.asarray(inp["L0_ffn_b1"])[:, None, :])
    m["B_w2"] = np.ascontiguousarray(inp["L0_ffn_w2"])
    m["B_b2"] = np.ascontiguousarray(np.asarray(inp["L0_ffn_b2"])[:, None, :])
    m["B_xc"] = _T["xpos_cq"]
    m["B_xs"] = _T["xpos_sq"]
    m["B_xck"] = _T["xpos_ck"]
    m["B_xsk"] = _T["xpos_sk"]
    m["B_mB0"] = _T["maskB0"]
    m["B_mB"] = _T["maskB"]
    m["B_mB0T"] = _T["maskB0T"]
    m["B_mBT"] = _T["maskBT"]
    return m


EMITS = [emit_A, emit_B, emit_C, emit_D]
MAPS = [map_A, map_B, map_C, map_D]


def build_fused(stages=(0, 1, 2, 3)):
    nc = bass.Bass("TRN2", target_bir_lowering=False)
    aps = {}
    decl_common_inputs(nc, aps)
    x = nc.dram_tensor("x", [NB * S, D], F32, kind="ExternalInput").ap()
    y = nc.dram_tensor("y", [NB * S, D], F32, kind="ExternalOutput").ap()
    mids = [nc.dram_tensor(f"mid{i}", [NB * S, D], F32, kind="Internal").ap()
            for i in range(len(stages) - 1)]
    chain = [x] + mids + [y]
    P = Prog()
    with ExitStack() as stk:
        sb = {}
        alloc_common(stk.enter_context, nc, sb)
        sems = {e: stk.enter_context(nc.semaphore(f"sem_{e}")) for e in ENGINES}
        load_common(P, sb, aps)
        for k, si in enumerate(stages):
            P.barrier()
            EMITS[si](P, nc, sb, chain[k], chain[k + 1])
        finalize(nc, P, sems)
    return nc


def map_fused(xc, inp, stages=(0, 1, 2, 3)):
    m = common_in_map()
    m["x"] = xc
    for si in stages:
        m.update(MAPS[si](inp))
    return m


def kernel(**inp):
    x = np.ascontiguousarray(np.asarray(inp["L0_attn_x"], dtype=np.float32))
    xs = [np.ascontiguousarray(x[2 * c:2 * c + 2].reshape(NB * S, D))
          for c in range(NCORES)]
    nc = build_fused()
    maps = [map_fused(xs[c], inp) for c in range(NCORES)]
    res = run_bass_kernel_spmd(nc, maps, core_ids=list(range(NCORES)))
    out = np.stack([np.asarray(r["y"], np.float32).reshape(NB, S, D)
                    for r in res.results]).reshape(16, S, D)
    return out


# version 04 (final)
# speedup vs baseline: 1.5214x; 1.1516x over previous
import numpy as np
import ml_dtypes
import concourse.bass as bass
import concourse.mybir as mybir
from concourse.bass_utils import run_bass_kernel_spmd
from contextlib import ExitStack

F32 = mybir.dt.float32
BF16 = mybir.dt.bfloat16
AF = mybir.ActivationFunctionType
ALU = mybir.AluOpType
AX = mybir.AxisListType
BF = ml_dtypes.bfloat16

ENGINES = ("sync", "scalar", "vector", "tensor", "gpsimd")
NCORES = 8
S = 2048
NB = 2
NT = S // 128
D = 1024


import bass_rust as _br

SELF_SYNC = ("scalar", "vector", "gpsimd")
CHAIN_SERIAL = ("sync",)


class Prog:
    def __init__(self):
        self.ops = []

    def dma(self, out, in_, eng="sync", **kw):
        self.ops.append((eng, "dma_start", (out, in_), kw, 16))

    def op(self, eng, name, *args, **kw):
        self.ops.append((eng, name, args, kw, 1))

    def barrier(self):
        self.ops.append(("BARRIER", None, None, None, 0))

    def mm(self, out, lhsT, rhs, start, stop):
        self.op("tensor", "matmul", out, lhsT, rhs, start=start, stop=stop)

    def T(self, out, in_, ident):
        self.op("tensor", "transpose", out, in_, ident)

    def cp(self, dst, src):
        self.op("scalar", "copy", dst, src)

    def act(self, out, in_, func, **kw):
        self.op("scalar", "activation", out, in_, func, **kw)

    def v(self, name, *args, **kw):
        self.op("vector", name, *args, **kw)


def _rw_sets(opname, args, kw):
    aps = [a for a in args if isinstance(a, _br.AP)]
    writes = [aps[0].tensor.name] if aps else []
    reads = [a.tensor.name for a in aps[1:]]
    for k, v in kw.items():
        if isinstance(v, _br.AP):
            if k == "accum_out":
                writes.append(v.tensor.name)
            else:
                reads.append(v.tensor.name)
    if opname == "matmul" and not kw.get("start", True):
        reads.append(writes[0])
    return reads, writes


def finalize(nc, P, sems):
    ops = P.ops
    n = len(ops)
    cum = {e: 0 for e in ENGINES}
    opval = [0] * n
    openg = [None] * n
    waits = [None] * n
    last_w = {}
    last_r = {}
    prev_op = {}
    bar = {e: 0 for e in ENGINES}
    for i, (e, opname, args, kw, d) in enumerate(ops):
        if e == "BARRIER":
            bar = dict(cum)
            last_w.clear()
            last_r.clear()
            continue
        cum[e] += d
        opval[i] = cum[e]
        openg[i] = e
        dep = {}
        if e in CHAIN_SERIAL and prev_op.get(e) is not None:
            dep[e] = opval[prev_op[e]]
        prev_op[e] = i
        def add(j):
            f = openg[j]
            if f != e or e in SELF_SYNC:
                dep[f] = max(dep.get(f, 0), opval[j])
        reads, writes = _rw_sets(opname, args, kw)
        for b in reads:
            for f, j in last_w.get(b, {}).items():
                add(j)
        for b in writes:
            for f, j in last_w.get(b, {}).items():
                add(j)
            for f, j in last_r.get(b, {}).items():
                add(j)
        for f, v in bar.items():
            if f != e and v > 0:
                dep[f] = max(dep.get(f, 0), v)
        waits[i] = dep
        for b in reads:
            last_r.setdefault(b, {})[e] = i
        for b in writes:
            last_w.setdefault(b, {})[e] = i
            lr = last_r.get(b)
            if lr is not None:
                lr.clear()
            last_r.setdefault(b, {})[e] = i
    total = dict(cum)
    with nc.Block() as block:
        def mk(eng_name):
            def body(E):
                water = {f: 0 for f in ENGINES}
                for i, (e, opname, args, kw, d) in enumerate(ops):
                    if e != eng_name:
                        continue
                    for f, v in waits[i].items():
                        if v > water[f]:
                            E.wait_ge(sems[f], v)
                            water[f] = v
                    getattr(E, opname)(*args, **kw).then_inc(sems[e], d)
                for f in ENGINES:
                    if f != eng_name and total[f] > water[f]:
                        E.wait_ge(sems[f], total[f])
                        water[f] = total[f]
            return body
        for en in ENGINES:
            getattr(block, en)(mk(en))


def _tables():
    t = {}
    t["identb"] = np.eye(128, dtype=np.float32).astype(BF)
    t["identf"] = np.eye(128, dtype=np.float32)
    t["onesb"] = np.ones((1, 128), dtype=np.float32).astype(BF)
    t["onesfr"] = np.ones((1, 128), dtype=np.float32)
    t["onesfc"] = np.ones((128, 1), dtype=np.float32)

    hd = 64
    inv = 1.0 / 10000.0 ** (np.arange(0, hd, 2, dtype=np.float64) / hd)
    pos = np.arange(S, dtype=np.float64)
    fr = np.outer(pos, inv)
    emb = np.concatenate([fr, fr], -1)
    cos = np.cos(emb)
    sin = np.sin(emb)
    sgn = np.where(np.arange(hd) % 2 == 0, -1.0, 1.0)
    c = 8.0 ** -0.5
    t["ropeA_cos"] = np.tile(cos * c, (1, 16)).astype(np.float32)
    t["ropeA_sin"] = np.tile(sin * sgn * c, (1, 16)).astype(np.float32)

    WIN = 512
    half = np.arange(0, hd, 2, dtype=np.float64)
    invf = 1.0 / 10000.0 ** (half / hd)
    tt = np.arange(S, dtype=np.float64)
    fr2 = tt[:, None] * invf[None, :]
    fr2 = np.concatenate([fr2, fr2], -1)
    base = (half + 0.4 * hd) / (1.4 * hd)
    power = (tt - S // 2) / float(WIN // 2)
    scale = base[None, :] ** power[:, None]
    scale = np.concatenate([scale, scale], -1)
    cos2 = np.cos(fr2)
    sin2 = np.sin(fr2)
    sgn2 = np.where(np.arange(hd) < 32, -1.0, 1.0)
    t["xpos_cq"] = np.tile(cos2 * scale * 8.0, (1, 8)).astype(np.float32)
    t["xpos_sq"] = np.tile(sin2 * scale * sgn2 * 8.0, (1, 8)).astype(np.float32)
    t["xpos_ck"] = np.tile(cos2 / scale, (1, 8)).astype(np.float32)
    t["xpos_sk"] = np.tile(sin2 * sgn2 / scale, (1, 8)).astype(np.float32)

    NEG = -1.0e30
    n = S
    w = n // WIN
    bt = np.arange(n).reshape(w, WIN)
    btp = np.pad(bt, ((1, 0), (0, 0)), constant_values=-1)
    btk = np.concatenate([btp[:w], btp[1:]], axis=1)
    m_list = []
    for wi in range(w):
        i = bt[wi][:, None]
        j = btk[wi][None, :]
        mask = (i < j) | (i > j + WIN) | (j == -1)
        m_list.append(np.where(mask, NEG, 0.0).astype(np.float32))
    assert np.array_equal(m_list[1], m_list[2])
    t["maskB0"] = m_list[0][:, 512:].astype(BF)
    t["maskB"] = m_list[1].astype(BF)
    t["maskB0T"] = np.ascontiguousarray(t["maskB0"].T)
    t["maskBT"] = np.ascontiguousarray(t["maskB"].T)

    H = 16
    slopes = 2.0 ** (-8.0 * np.arange(1, H + 1, dtype=np.float64) / H)
    j = np.arange(S, dtype=np.float64)
    t["C_Kb"] = np.stack([j, np.ones_like(j)], 0).astype(np.float32)
    Qb = np.zeros((2, H, S))
    for h in range(H):
        Qb[0, h, :] = slopes[h]
        Qb[1, h, :] = -slopes[h] * j
    t["C_Qb"] = Qb.reshape(2, H * S).astype(np.float32)
    CT = np.zeros((16, 128, 512), dtype=np.float32)
    for kt in range(16):
        qc = kt // 4
        jj = kt * 128 + np.arange(128)[:, None]
        ii = qc * 512 + np.arange(512)[None, :]
        CT[kt] = np.where(jj > ii, NEG, 0.0)
    t["C_CT"] = CT
    return t


_T = _tables()


def emit_ln(P, sb, out, in_, cols, gain=None, bias_t=None):
    st = sb["stats"]
    scr = sb["lnscr"]
    P.act(scr[:, 0:cols], in_, AF.Square, accum_out=st[:, 1:2])
    P.v("reduce_sum", st[:, 0:1], in_, axis=AX.X)
    P.v("tensor_scalar_mul", st[:, 2:3], st[:, 0:1], 1.0 / cols)
    P.v("tensor_mul", st[:, 3:4], st[:, 2:3], st[:, 2:3])
    P.v("tensor_scalar", st[:, 4:5], st[:, 1:2], 1.0 / cols, None, ALU.mult)
    P.v("tensor_sub", st[:, 4:5], st[:, 4:5], st[:, 3:4])
    P.v("tensor_scalar_add", st[:, 4:5], st[:, 4:5], 1e-5)
    P.act(st[:, 5:6], st[:, 4:5], AF.Sqrt)
    P.v("reciprocal", st[:, 6:7], st[:, 5:6])
    P.v("tensor_scalar", out, in_, st[:, 2:3], st[:, 6:7], ALU.subtract, ALU.mult)
    if gain is not None:
        P.v("tensor_mul", out, out, gain)
        P.v("tensor_add", out, out, bias_t)


def emit_transpose_row(P, sb, lnb_bf, lnT_bf, nk):
    for kc in range(nk):
        ps = sb["psT" if kc % 2 == 0 else "psT2"]
        P.T(ps[:, :], lnb_bf[:, kc * 128:(kc + 1) * 128], sb["identb"][:, :])
        P.cp(lnT_bf[:, kc * 128:(kc + 1) * 128], ps[:, :])


def alloc_common(E, nc, sb):
    sb["identb"] = E(nc.sbuf_tensor("identb_s", [128, 128], BF16))
    sb["identf"] = E(nc.sbuf_tensor("identf_s", [128, 128], F32))
    sb["onesb"] = E(nc.sbuf_tensor("onesb_s", [1, 128], BF16))
    sb["onesfr"] = E(nc.sbuf_tensor("onesfr_s", [1, 128], F32))
    sb["onesfc"] = E(nc.sbuf_tensor("onesfc_s", [128, 1], F32))
    sb["stats"] = E(nc.sbuf_tensor("stats", [128, 16], F32))
    sb["lnscr"] = E(nc.sbuf_tensor("lnscr", [128, 1024], F32))
    for i in range(6):
        sb[f"G{i}"] = E(nc.psum_tensor(f"G{i}", [128, 512], F32))
    sb["psT"] = E(nc.psum_tensor("psT", [128, 128], BF16))
    sb["psTf"] = E(nc.psum_tensor("psTf", [128, 128], F32))
    sb["psT2"] = sb["psT"]


def load_common(P, sb, aps):
    for nm in ("identb", "identf", "onesb", "onesfr", "onesfc"):
        P.dma(sb[nm][:, :], aps[nm])


def decl_common_inputs(nc, aps):
    for nm in ("identb", "identf", "onesb", "onesfr", "onesfc"):
        arr = _T[nm]
        dt = BF16 if arr.dtype == BF else F32
        aps[nm] = nc.dram_tensor(nm, list(arr.shape), dt, kind="ExternalInput").ap()


def common_in_map():
    return {nm: _T[nm] for nm in ("identb", "identf", "onesb", "onesfr", "onesfc")}


def emit_A(P, nc, sb, x, y):
    W = nc.dram_tensor("A_W", [D, 3 * D], F32, kind="ExternalInput").ap()
    bqkv = nc.dram_tensor("A_bqkv", [1, 3 * D], F32, kind="ExternalInput").ap()
    WoT = nc.dram_tensor("A_WoT", [D, D], F32, kind="ExternalInput").ap()
    bo = nc.dram_tensor("A_bo", [1, D], F32, kind="ExternalInput").ap()
    rc = nc.dram_tensor("A_ropec", [S, D], F32, kind="ExternalInput").ap()
    rs = nc.dram_tensor("A_ropes", [S, D], F32, kind="ExternalInput").ap()
    qT_h = nc.dram_tensor("A_qT_h", [D, S], BF16, kind="Internal").ap()
    kT_h = nc.dram_tensor("A_kT_h", [D, S], BF16, kind="Internal").ap()
    v_h = nc.dram_tensor("A_v_h", [S, D], BF16, kind="Internal").ap()
    oT_h = nc.dram_tensor("A_oT_h", [D, S], BF16, kind="Internal").ap()

    with ExitStack() as stk:
        E = stk.enter_context
        Wsb = E(nc.sbuf_tensor("A_Wsb", [128, 8 * 3 * D], BF16))
        Wfs = E(nc.sbuf_tensor("Wfs", [128, 3 * D], F32))
        WoTsb = E(nc.sbuf_tensor("WoTsb", [128, 8 * D], BF16))
        bsb = E(nc.sbuf_tensor("bsb", [1, 3 * D], BF16))
        bosb = E(nc.sbuf_tensor("bosb", [1, D], BF16))
        bscr = E(nc.sbuf_tensor("bscr", [1, 3 * D], F32))
        xt = E(nc.sbuf_tensor("xt", [128, D], F32))
        lnt = E(nc.sbuf_tensor("lnt", [128, D], F32))
        lnb = E(nc.sbuf_tensor("lnb", [128, D], BF16))
        lnT = E(nc.sbuf_tensor("lnT", [128, 8 * 128], BF16))
        cost = E(nc.sbuf_tensor("cost", [128, D], F32))
        sint = E(nc.sbuf_tensor("sint", [128, D], F32))
        qs0 = E(nc.sbuf_tensor("qs0", [128, D], F32))
        qa = E(nc.sbuf_tensor("qa", [128, D], F32))
        shq = E(nc.sbuf_tensor("shq", [128, D], F32))
        qf = E(nc.sbuf_tensor("qf", [128, D], F32))
        qfb = E(nc.sbuf_tensor("qfb", [128, D], BF16))
        tstage = E(nc.sbuf_tensor("tstage", [128, 8 * 128], BF16))
        vb = E(nc.sbuf_tensor("vb", [128, D], BF16))
        qsb = E(nc.sbuf_tensor("qsb", [64, S], BF16))
        ksb = E(nc.sbuf_tensor("ksb", [64, S], BF16))
        vg = E(nc.sbuf_tensor("vg", [128, 16 * 65], BF16))
        expb = E(nc.sbuf_tensor("expb", [128, 512], BF16))
        expb2 = E(nc.sbuf_tensor("expb2", [128, 512], BF16))
        expbs = [expb, expb2]
        oTsb = E(nc.sbuf_tensor("oTsb", [65, 512], F32))
        r0 = E(nc.sbuf_tensor("r0", [1, 512], F32))
        rb = E(nc.sbuf_tensor("rb", [64, 512], F32))
        oTf = E(nc.sbuf_tensor("oTf", [64, 512], BF16))
        oTl = E(nc.sbuf_tensor("oTl", [128, 8 * 128], BF16))
        yt = E(nc.sbuf_tensor("yt", [128, D], F32))

        for kc in range(8):
            P.dma(Wfs[:, :], W[kc * 128:(kc + 1) * 128, :])
            P.v("tensor_copy", Wsb[:, kc * 3 * D:(kc + 1) * 3 * D], Wfs[:, :])
        for kc in range(8):
            P.dma(Wfs[:, 0:D], WoT[kc * 128:(kc + 1) * 128, :])
            P.v("tensor_copy", WoTsb[:, kc * D:(kc + 1) * D], Wfs[:, 0:D])
        P.dma(bscr[:, :], bqkv)
        P.v("tensor_copy", bsb[:, :], bscr[:, :])
        P.dma(bscr[:, 0:D], bo)
        P.v("tensor_copy", bosb[:, :], bscr[:, 0:D])

        G = [sb[f"G{i}"] for i in range(6)]
        vgv = vg[:, :].rearrange("p (t c) -> p t c", c=65)
        for b in range(NB):
            r0b = b * S
            for i in range(NT):
                rr = r0b + i * 128
                P.dma(xt[:, :], x[rr:rr + 128, :])
                emit_ln(P, sb, lnt[:, :], xt[:, :], D)
                P.v("tensor_copy", lnb[:, :], lnt[:, :])
                emit_transpose_row(P, sb, lnb, lnT, 8)
                P.dma(cost[:, :], rc[i * 128:(i + 1) * 128, :])
                P.dma(sint[:, :], rs[i * 128:(i + 1) * 128, :])
                for kc in range(8):
                    for n in range(6):
                        P.mm(G[n][:, :], lnT[:, kc * 128:(kc + 1) * 128],
                             Wsb[:, kc * 3 * D + n * 512: kc * 3 * D + (n + 1) * 512],
                             start=(kc == 0), stop=False)
                for n in range(6):
                    P.mm(G[n][:, :], sb["onesb"][:, :], bsb[:, n * 512:(n + 1) * 512],
                         start=False, stop=True)
                for (g0, g1, Tdst) in ((0, 1, qT_h), (2, 3, kT_h)):
                    P.cp(qs0[:, 0:512], G[g0][:, :])
                    P.cp(qs0[:, 512:1024], G[g1][:, :])
                    P.v("tensor_mul", qa[:, :], qs0[:, :], cost[:, :])
                    P.v("tensor_copy", shq[:, 0:D:2], qs0[:, 1:D:2])
                    P.v("tensor_copy", shq[:, 1:D:2], qs0[:, 0:D:2])
                    P.v("tensor_mul", shq[:, :], shq[:, :], sint[:, :])
                    P.v("tensor_add", qf[:, :], qa[:, :], shq[:, :])
                    P.v("tensor_copy", qfb[:, :], qf[:, :])
                    for hp in range(8):
                        ps = sb["psT" if hp % 2 == 0 else "psT2"]
                        P.T(ps[:, :], qfb[:, hp * 128:(hp + 1) * 128],
                            sb["identb"][:, :])
                        P.cp(tstage[:, hp * 128:(hp + 1) * 128], ps[:, :])
                    P.dma(Tdst[:, i * 128:(i + 1) * 128]
                          .rearrange("(a p) c -> p a c", p=128),
                          tstage[:, :].rearrange("p (a c) -> p a c", c=128))
                P.cp(vb[:, 0:512], G[4][:, :])
                P.cp(vb[:, 512:1024], G[5][:, :])
                P.dma(v_h[i * 128:(i + 1) * 128, :], vb[:, :])
            for h in range(16):
                P.dma(qsb[:, :], qT_h[h * 64:(h + 1) * 64, :])
                P.dma(ksb[:, :], kT_h[h * 64:(h + 1) * 64, :])
                P.dma(vgv[:, :, 0:64],
                      v_h[:, h * 64:(h + 1) * 64].rearrange("(t p) c -> p t c", p=128))
                P.v("memset", vgv[:, :, 64:65], 1.0)
                for qc in range(4):
                    for kt in range(16):
                        gs = G[kt % 2]
                        P.mm(gs[:, :], ksb[:, kt * 128:(kt + 1) * 128],
                             qsb[:, qc * 512:(qc + 1) * 512], start=True, stop=True)
                        if kt > 0:
                            P.mm(G[2 + qc][0:65, :], vgv[:, kt - 1, :],
                                 expbs[(kt - 1) % 2][:, :],
                                 start=(kt == 1), stop=False)
                        P.act(expbs[kt % 2][:, :], gs[:, :], AF.Exp)
                    P.mm(G[2 + qc][0:65, :], vgv[:, 15, :], expbs[1][:, :],
                         start=False, stop=True)
                for qc in range(4):
                    P.cp(oTsb[:, :], G[2 + qc][0:65, :])
                    P.dma(r0[:, :], oTsb[64:65, :])
                    P.v("reciprocal", r0[:, :], r0[:, :])
                    P.mm(G[0][0:64, :], sb["onesfr"][:, 0:64], r0[:, :],
                         start=True, stop=True)
                    P.cp(rb[:, :], G[0][0:64, :])
                    P.v("tensor_mul", oTf[:, :], oTsb[0:64, :], rb[:, :])
                    P.dma(oT_h[h * 64:(h + 1) * 64, qc * 512:(qc + 1) * 512],
                          oTf[:, :])
            for i in range(NT):
                rr = r0b + i * 128
                g0 = (i % 2) * 2
                P.dma(oTl[:, :].rearrange("p (a c) -> p a c", c=128),
                      oT_h[:, i * 128:(i + 1) * 128].rearrange("(a p) c -> p a c", p=128))
                for kc in range(8):
                    for n in range(2):
                        P.mm(G[g0 + n][:, :], oTl[:, kc * 128:(kc + 1) * 128],
                             WoTsb[:, kc * D + n * 512: kc * D + (n + 1) * 512],
                             start=(kc == 0), stop=False)
                for n in range(2):
                    P.mm(G[g0 + n][:, :], sb["onesb"][:, :], bosb[:, n * 512:(n + 1) * 512],
                         start=False, stop=True)
                P.dma(xt[:, :], x[rr:rr + 128, :])
                for n in range(2):
                    P.v("tensor_add", yt[:, n * 512:(n + 1) * 512], G[g0 + n][:, :],
                        xt[:, n * 512:(n + 1) * 512])
                P.dma(y[rr:rr + 128, :], yt[:, :])


def map_A(inp):
    m = {}
    m["A_W"] = np.ascontiguousarray(
        np.concatenate([np.asarray(inp["L0_attn_Wq"]).T, np.asarray(inp["L0_attn_Wk"]).T,
                        np.asarray(inp["L0_attn_Wv"]).T], axis=1))
    m["A_bqkv"] = np.ascontiguousarray(
        np.concatenate([inp["L0_attn_bq"], inp["L0_attn_bk"],
                        inp["L0_attn_bv"]])[None, :])
    m["A_WoT"] = np.ascontiguousarray(np.asarray(inp["L0_attn_Wo"]).T)
    m["A_bo"] = np.ascontiguousarray(np.asarray(inp["L0_attn_bo"])[None, :])
    m["A_ropec"] = _T["ropeA_cos"]
    m["A_ropes"] = _T["ropeA_sin"]
    return m


def emit_D(P, nc, sb, x, y):
    Win = nc.dram_tensor("D_Win", [D, 64], F32, kind="ExternalInput").ap()
    bin_ = nc.dram_tensor("D_bin", [1, 64], F32, kind="ExternalInput").ap()
    W1r = nc.dram_tensor("D_W1r", [1, 1024], F32, kind="ExternalInput").ap()
    b1r = nc.dram_tensor("D_b1r", [1, 1024], F32, kind="ExternalInput").ap()
    W2r = nc.dram_tensor("D_W2r", [1, 1024], F32, kind="ExternalInput").ap()
    b2r = nc.dram_tensor("D_b2r", [1, 64], F32, kind="ExternalInput").ap()
    Wout = nc.dram_tensor("D_Wout", [64, D], F32, kind="ExternalInput").ap()
    bout = nc.dram_tensor("D_bout", [1, D], F32, kind="ExternalInput").ap()

    with ExitStack() as stk:
        E = stk.enter_context
        Wfs = E(nc.sbuf_tensor("D_Wfs", [128, 1024], F32))
        Winb = E(nc.sbuf_tensor("Winb", [128, 8 * 64], BF16))
        Woutb = E(nc.sbuf_tensor("Woutb", [64, D], BF16))
        binb = E(nc.sbuf_tensor("binb", [1, 64], BF16))
        boutb = E(nc.sbuf_tensor("boutb", [1, D], BF16))
        rowf = E(nc.sbuf_tensor("rowf", [1, 1024], F32))
        W1b = E(nc.sbuf_tensor("W1b", [128, 1024], F32))
        b1b = E(nc.sbuf_tensor("b1b", [128, 1024], F32))
        W2b = E(nc.sbuf_tensor("W2b", [128, 1024], F32))
        b2b = E(nc.sbuf_tensor("b2b", [128, 64], F32))
        xt = E(nc.sbuf_tensor("D_xt", [128, D], F32))
        lnt = E(nc.sbuf_tensor("D_lnt", [128, D], F32))
        lnb = E(nc.sbuf_tensor("D_lnb", [128, D], BF16))
        lnT = E(nc.sbuf_tensor("D_lnT", [128, 8 * 128], BF16))
        st_ = E(nc.sbuf_tensor("st_", [128, 64], F32))
        htmp = E(nc.sbuf_tensor("htmp", [128, 64], F32))
        gacc = E(nc.sbuf_tensor("gacc", [128, 64], F32))
        gb = E(nc.sbuf_tensor("gb", [128, 64], BF16))
        gT = E(nc.sbuf_tensor("gT", [64, 128], BF16))
        yt = E(nc.sbuf_tensor("D_yt", [128, D], F32))

        G = [sb[f"G{i}"] for i in range(6)]
        for kc in range(8):
            P.dma(Wfs[:, 0:64], Win[kc * 128:(kc + 1) * 128, :])
            P.v("tensor_copy", Winb[:, kc * 64:(kc + 1) * 64], Wfs[:, 0:64])
        P.dma(Wfs[0:64, 0:D], Wout)
        P.v("tensor_copy", Woutb[:, :], Wfs[0:64, 0:D])
        P.dma(rowf[:, 0:64], bin_)
        P.v("tensor_copy", binb[:, :], rowf[:, 0:64])
        P.dma(rowf[:, 0:D], bout)
        P.v("tensor_copy", boutb[:, :], rowf[:, 0:D])
        for src, dst in ((W1r, W1b), (b1r, b1b), (W2r, W2b)):
            P.dma(rowf[:, :], src)
            for n in range(2):
                P.mm(G[0][:, :], sb["onesfr"][:, :], rowf[:, n * 512:(n + 1) * 512],
                     start=True, stop=True)
                P.cp(dst[:, n * 512:(n + 1) * 512], G[0][:, :])
        P.dma(rowf[:, 0:64], b2r)
        P.mm(G[0][:, 0:64], sb["onesfr"][:, :], rowf[:, 0:64], start=True, stop=True)
        P.cp(b2b[:, :], G[0][:, 0:64])

        for b in range(NB):
            for i in range(NT):
                rr = b * S + i * 128
                P.dma(xt[:, :], x[rr:rr + 128, :])
                emit_ln(P, sb, lnt[:, :], xt[:, :], D)
                P.v("tensor_copy", lnb[:, :], lnt[:, :])
                emit_transpose_row(P, sb, lnb, lnT, 8)
                for kc in range(8):
                    P.mm(G[0][:, 0:64], lnT[:, kc * 128:(kc + 1) * 128],
                         Winb[:, kc * 64:(kc + 1) * 64], start=(kc == 0), stop=False)
                P.mm(G[0][:, 0:64], sb["onesb"][:, :], binb[:, :],
                     start=False, stop=True)
                P.cp(st_[:, :], G[0][:, 0:64])
                for ii in range(16):
                    P.v("tensor_mul", htmp[:, :], st_[:, :], W1b[:, ii * 64:(ii + 1) * 64])
                    P.v("tensor_add", htmp[:, :], htmp[:, :], b1b[:, ii * 64:(ii + 1) * 64])
                    P.v("tensor_scalar_max", htmp[:, :], htmp[:, :], 0.0)
                    P.v("tensor_mul", htmp[:, :], htmp[:, :], W2b[:, ii * 64:(ii + 1) * 64])
                    if ii == 0:
                        P.v("tensor_copy", gacc[:, :], htmp[:, :])
                    else:
                        P.v("tensor_add", gacc[:, :], gacc[:, :], htmp[:, :])
                P.v("tensor_add", gacc[:, :], gacc[:, :], b2b[:, :])
                P.v("tensor_copy", gb[:, :], gacc[:, :])
                P.T(sb["psT"][0:64, :], gb[:, :], sb["identb"][:, :])
                P.cp(gT[:, :], sb["psT"][0:64, :])
                for n in range(2):
                    P.mm(G[1 + n][:, :], gT[:, :], Woutb[:, n * 512:(n + 1) * 512],
                         start=True, stop=False)
                    P.mm(G[1 + n][:, :], sb["onesb"][:, :], boutb[:, n * 512:(n + 1) * 512],
                         start=False, stop=True)
                    P.v("tensor_add", yt[:, n * 512:(n + 1) * 512], G[1 + n][:, :],
                        xt[:, n * 512:(n + 1) * 512])
                P.dma(y[rr:rr + 128, :], yt[:, :])


def map_D(inp):
    m = {}
    m["D_Win"] = np.ascontiguousarray(inp["L1_ffn_Win"])
    m["D_bin"] = np.ascontiguousarray(np.asarray(inp["L1_ffn_bin_"])[None, :])
    m["D_W1r"] = np.ascontiguousarray(np.asarray(inp["L1_ffn_W1"]).T.reshape(1, 1024))
    m["D_b1r"] = np.ascontiguousarray(np.asarray(inp["L1_ffn_b1"]).T.reshape(1, 1024))
    m["D_W2r"] = np.ascontiguousarray(np.asarray(inp["L1_ffn_W2"]).T.reshape(1, 1024))
    m["D_b2r"] = np.ascontiguousarray(np.asarray(inp["L1_ffn_b2"])[None, :])
    m["D_Wout"] = np.ascontiguousarray(inp["L1_ffn_Wout"])
    m["D_bout"] = np.ascontiguousarray(np.asarray(inp["L1_ffn_bout"])[None, :])
    return m


def emit_C(P, nc, sb, x, y):
    W = nc.dram_tensor("C_W", [D, 5 * D], F32, kind="ExternalInput").ap()
    Wo = nc.dram_tensor("C_Wo", [D, D], F32, kind="ExternalInput").ap()
    lvec = nc.dram_tensor("C_lvec", [64, 4], F32, kind="ExternalInput").ap()
    lnwT = nc.dram_tensor("C_lnwT", [64, 16], F32, kind="ExternalInput").ap()
    lnbT = nc.dram_tensor("C_lnbT", [64, 16], F32, kind="ExternalInput").ap()
    Kb = nc.dram_tensor("C_Kb", [2, S], F32, kind="ExternalInput").ap()
    Qb = nc.dram_tensor("C_Qb", [2, 16 * S], F32, kind="ExternalInput").ap()
    CT = nc.dram_tensor("C_CT", [16, 128, 512], F32, kind="ExternalInput").ap()
    pT_h = nc.dram_tensor("C_pT_h", [4 * D, S], BF16, kind="Internal").ap()
    v_h = nc.dram_tensor("C_v_h", [S, D], BF16, kind="Internal").ap()
    oT_h = nc.dram_tensor("C_oT_h", [D, S], BF16, kind="Internal").ap()

    with ExitStack() as stk:
        E = stk.enter_context
        Wsb = E(nc.sbuf_tensor("C_Wsb", [128, 8 * 5120], BF16))
        Wfs = E(nc.sbuf_tensor("C_Wfs", [128, 5120], F32))
        WoSb = E(nc.sbuf_tensor("WoSb", [128, 8 * D], BF16))
        Kbs = E(nc.sbuf_tensor("Kbs", [2, S], F32))
        Qhs = E(nc.sbuf_tensor("Qhs", [2, S], F32))
        CTs = E(nc.sbuf_tensor("CTs", [128, 512], F32))
        lnwS = E(nc.sbuf_tensor("lnwS", [64, 16], F32))
        lnbS = E(nc.sbuf_tensor("lnbS", [64, 16], F32))
        lvs = E(nc.sbuf_tensor("lvs", [64, 4], F32))
        neglam = E(nc.sbuf_tensor("neglam", [1, 4], F32))
        xt = E(nc.sbuf_tensor("C_xt", [128, D], F32))
        lnt = E(nc.sbuf_tensor("C_lnt", [128, D], F32))
        lnb = E(nc.sbuf_tensor("C_lnb", [128, D], BF16))
        lnT = E(nc.sbuf_tensor("C_lnT", [128, 8 * 128], BF16))
        pstage = E(nc.sbuf_tensor("C_pstage", [128, 512], BF16))
        tstage = E(nc.sbuf_tensor("C_tstage", [128, 4 * 128], BF16))
        vb = E(nc.sbuf_tensor("C_vb", [128, D], BF16))
        q0sb = E(nc.sbuf_tensor("q0sb", [64, S], BF16))
        q1sb = E(nc.sbuf_tensor("q1sb", [64, S], BF16))
        k0sb = E(nc.sbuf_tensor("k0sb", [64, S], BF16))
        k1sb = E(nc.sbuf_tensor("k1sb", [64, S], BF16))
        vg = E(nc.sbuf_tensor("C_vg", [128, 16 * 65], BF16))
        ssb = E(nc.sbuf_tensor("ssb", [128, 512], F32))
        expb = E(nc.sbuf_tensor("C_expb", [128, 512], BF16))
        oT0 = E(nc.sbuf_tensor("oT0", [65, 512], F32))
        oT1 = E(nc.sbuf_tensor("oT1", [65, 512], F32))
        r0 = E(nc.sbuf_tensor("C_r0", [1, 512], F32))
        r1 = E(nc.sbuf_tensor("r1", [1, 512], F32))
        rb0 = E(nc.sbuf_tensor("rb0", [64, 512], F32))
        rb1 = E(nc.sbuf_tensor("rb1", [64, 512], F32))
        hsb = E(nc.sbuf_tensor("hsb", [64, 512], F32))
        hscr = E(nc.sbuf_tensor("hscr", [64, 512], F32))
        rowa = E(nc.sbuf_tensor("rowa", [1, 512], F32))
        rowb = E(nc.sbuf_tensor("rowb", [1, 512], F32))
        mub = E(nc.sbuf_tensor("mub", [64, 512], F32))
        oTf = E(nc.sbuf_tensor("C_oTf", [64, 512], BF16))
        oTl = E(nc.sbuf_tensor("C_oTl", [128, 8 * 128], BF16))
        yt = E(nc.sbuf_tensor("C_yt", [128, D], F32))

        for kc in range(8):
            P.dma(Wfs[:, :], W[kc * 128:(kc + 1) * 128, :])
            P.v("tensor_copy", Wsb[:, kc * 5120:(kc + 1) * 5120], Wfs[:, :])
        for kc in range(8):
            P.dma(Wfs[:, 0:D], Wo[kc * 128:(kc + 1) * 128, :])
            P.v("tensor_copy", WoSb[:, kc * D:(kc + 1) * D], Wfs[:, 0:D])
        P.dma(Kbs[:, :], Kb)
        P.dma(lnwS[:, :], lnwT)
        P.dma(lnbS[:, :], lnbT)
        P.dma(lvs[:, :], lvec)
        P.mm(sb["G0"][0:1, 0:1], lvs[:, 0:1], lvs[:, 1:2], start=True, stop=True)
        P.mm(sb["G0"][0:1, 1:2], lvs[:, 2:3], lvs[:, 3:4], start=True, stop=True)
        P.cp(neglam[:, 0:2], sb["G0"][0:1, 0:2])
        P.act(neglam[:, 2:4], neglam[:, 0:2], AF.Exp)
        P.v("tensor_add", neglam[:, 0:1], neglam[:, 2:3], neglam[:, 3:4])
        P.v("tensor_scalar_add", neglam[:, 0:1], neglam[:, 0:1], -1.6)

        G = [sb[f"G{i}"] for i in range(6)]
        vgv = vg[:, :].rearrange("p (t c) -> p t c", c=65)
        CTs2 = xt[:, 0:512]
        ssbs = [ssb[:, 0:512], lnt[:, 0:512]]
        expbs = [expb, pstage]
        SC = 0.125 ** 0.5
        for b in range(NB):
            r0b = b * S
            for i in range(NT):
                rr = r0b + i * 128
                P.dma(xt[:, :], x[rr:rr + 128, :])
                emit_ln(P, sb, lnt[:, :], xt[:, :], D)
                P.v("tensor_copy", lnb[:, :], lnt[:, :])
                emit_transpose_row(P, sb, lnb, lnT, 8)
                for half in range(2):
                    for kc in range(8):
                        for n in range(5):
                            nn = half * 5 + n
                            P.mm(G[n][:, :], lnT[:, kc * 128:(kc + 1) * 128],
                                 Wsb[:, kc * 5120 + nn * 512: kc * 5120 + (nn + 1) * 512],
                                 start=(kc == 0), stop=(kc == 7))
                    for n in range(5):
                        nn = half * 5 + n
                        proj = nn // 2
                        if proj < 4:
                            P.v("tensor_scalar", pstage[:, :], G[n][:, :], SC, None,
                                ALU.mult)
                            for hp in range(4):
                                ps = sb["psT" if hp % 2 == 0 else "psT2"]
                                P.T(ps[:, :], pstage[:, hp * 128:(hp + 1) * 128],
                                    sb["identb"][:, :])
                                P.cp(tstage[:, hp * 128:(hp + 1) * 128], ps[:, :])
                            ro = proj * D + (nn % 2) * 512
                            P.dma(pT_h[ro:ro + 512, i * 128:(i + 1) * 128]
                                  .rearrange("(a p) c -> p a c", p=128),
                                  tstage[:, :].rearrange("p (a c) -> p a c", c=128))
                        else:
                            P.v("tensor_copy", vb[:, (nn % 2) * 512:(nn % 2) * 512 + 512],
                                G[n][:, :])
                P.dma(v_h[i * 128:(i + 1) * 128, :], vb[:, :])
            for h in range(16):
                P.dma(q0sb[:, :], pT_h[0 * D + h * 64: 0 * D + (h + 1) * 64, :])
                P.dma(q1sb[:, :], pT_h[1 * D + h * 64: 1 * D + (h + 1) * 64, :])
                P.dma(k0sb[:, :], pT_h[2 * D + h * 64: 2 * D + (h + 1) * 64, :])
                P.dma(k1sb[:, :], pT_h[3 * D + h * 64: 3 * D + (h + 1) * 64, :])
                P.dma(Qhs[:, :], Qb[:, h * S:(h + 1) * S])
                P.dma(vgv[:, :, 0:64],
                      v_h[:, h * 64:(h + 1) * 64].rearrange("(t p) c -> p t c", p=128))
                P.v("memset", vgv[:, :, 64:65], 1.0)
                for qc in range(4):
                    nkt = 4 * (qc + 1)
                    sc_banks = [G[0], G[3], G[4], G[5]]
                    pend = None
                    j = 0
                    for kt in range(nkt):
                        diag = (kt // 4 == qc)
                        cts = CTs[:, :] if kt % 2 == 0 else CTs2
                        if diag:
                            P.dma(cts, CT[kt, :, :])
                        for (qq, kk, gi) in ((q0sb, k0sb, 1), (q1sb, k1sb, 2)):
                            bank = sc_banks[j % 4]
                            P.mm(bank[:, :], kk[:, kt * 128:(kt + 1) * 128],
                                 qq[:, qc * 512:(qc + 1) * 512], start=True, stop=False)
                            P.mm(bank[:, :], Kbs[:, kt * 128:(kt + 1) * 128],
                                 Qhs[:, qc * 512:(qc + 1) * 512], start=False, stop=True)
                            if pend is not None:
                                pgi, pkt, pj = pend
                                P.mm(G[pgi][0:65, :], vgv[:, pkt, :],
                                     expbs[pj % 2][:, :],
                                     start=(pkt == 0),
                                     stop=(pkt == nkt - 1))
                            if diag:
                                P.v("tensor_add", ssbs[j % 2], bank[:, :], cts)
                                P.act(expbs[j % 2][:, :], ssbs[j % 2], AF.Exp)
                            else:
                                P.act(expbs[j % 2][:, :], bank[:, :], AF.Exp)
                            pend = (gi, kt, j)
                            j += 1
                    pgi, pkt, pj = pend
                    P.mm(G[pgi][0:65, :], vgv[:, pkt, :], expbs[pj % 2][:, :],
                         start=(pkt == 0), stop=True)
                    P.cp(oT0[:, :], G[1][0:65, :])
                    P.cp(oT1[:, :], G[2][0:65, :])
                    P.dma(r0[:, :], oT0[64:65, :])
                    P.dma(r1[:, :], oT1[64:65, :])
                    P.v("reciprocal", r0[:, :], r0[:, :])
                    P.v("reciprocal", r1[:, :], r1[:, :])
                    P.v("tensor_scalar_mul", r1[:, :], r1[:, :], neglam[:, 0:1])
                    P.mm(G[0][0:64, :], sb["onesfr"][:, 0:64], r0[:, :],
                         start=True, stop=True)
                    P.cp(rb0[:, :], G[0][0:64, :])
                    P.mm(G[0][0:64, :], sb["onesfr"][:, 0:64], r1[:, :],
                         start=True, stop=True)
                    P.cp(rb1[:, :], G[0][0:64, :])
                    P.v("tensor_mul", hsb[:, :], oT0[0:64, :], rb0[:, :])
                    P.v("tensor_mul", hscr[:, :], oT1[0:64, :], rb1[:, :])
                    P.v("tensor_add", hsb[:, :], hsb[:, :], hscr[:, :])
                    P.mm(G[0][0:1, :], sb["onesfc"][0:64, :], hsb[:, :],
                         start=True, stop=True)
                    P.act(hscr[:, :], hsb[:, :], AF.Square)
                    P.mm(G[3][0:1, :], sb["onesfc"][0:64, :], hscr[:, :],
                         start=True, stop=True)
                    P.cp(rowa[:, :], G[0][0:1, :])
                    P.cp(rowb[:, :], G[3][0:1, :])
                    P.v("tensor_scalar_mul", rowa[:, :], rowa[:, :], 1.0 / 64)
                    P.v("tensor_scalar_mul", rowb[:, :], rowb[:, :], 1.0 / 64)
                    P.v("tensor_mul", r0[:, :], rowa[:, :], rowa[:, :])
                    P.v("tensor_sub", rowb[:, :], rowb[:, :], r0[:, :])
                    P.v("tensor_scalar_add", rowb[:, :], rowb[:, :], 1e-5)
                    P.act(r0[:, :], rowb[:, :], AF.Ln)
                    P.v("tensor_scalar_mul", r0[:, :], r0[:, :], -0.5)
                    P.act(rowb[:, :], r0[:, :], AF.Exp)
                    P.mm(G[0][0:64, :], sb["onesfr"][:, 0:64], rowa[:, :],
                         start=True, stop=True)
                    P.cp(mub[:, :], G[0][0:64, :])
                    P.mm(G[0][0:64, :], sb["onesfr"][:, 0:64], rowb[:, :],
                         start=True, stop=True)
                    P.cp(rb0[:, :], G[0][0:64, :])
                    P.v("tensor_sub", hsb[:, :], hsb[:, :], mub[:, :])
                    P.v("tensor_mul", hsb[:, :], hsb[:, :], rb0[:, :])
                    P.v("tensor_scalar", hsb[:, :], hsb[:, :], lnwS[:, h:h + 1],
                        lnbS[:, h:h + 1], ALU.mult, ALU.add)
                    P.v("tensor_copy", oTf[:, :], hsb[:, :])
                    P.dma(oT_h[h * 64:(h + 1) * 64, qc * 512:(qc + 1) * 512], oTf[:, :])
            for i in range(NT):
                rr = r0b + i * 128
                g0 = (i % 2) * 2
                P.dma(oTl[:, :].rearrange("p (a c) -> p a c", c=128),
                      oT_h[:, i * 128:(i + 1) * 128].rearrange("(a p) c -> p a c", p=128))
                for kc in range(8):
                    for n in range(2):
                        P.mm(G[g0 + n][:, :], oTl[:, kc * 128:(kc + 1) * 128],
                             WoSb[:, kc * D + n * 512: kc * D + (n + 1) * 512],
                             start=(kc == 0), stop=(kc == 7))
                P.dma(xt[:, :], x[rr:rr + 128, :])
                for n in range(2):
                    P.v("tensor_add", yt[:, n * 512:(n + 1) * 512], G[g0 + n][:, :],
                        xt[:, n * 512:(n + 1) * 512])
                P.dma(y[rr:rr + 128, :], yt[:, :])


def map_C(inp):
    m = {}
    m["C_W"] = np.ascontiguousarray(
        np.concatenate([inp["L1_attn_wq0"], inp["L1_attn_wq1"],
                        inp["L1_attn_wk0"], inp["L1_attn_wk1"],
                        inp["L1_attn_wv"]], axis=1))
    m["C_Wo"] = np.ascontiguousarray(inp["L1_attn_wo"])
    m["C_lvec"] = np.ascontiguousarray(
        np.stack([inp["L1_attn_l0"], inp["L1_attn_l1"],
                  inp["L1_attn_l2"], inp["L1_attn_l3"]], axis=1))
    m["C_lnwT"] = np.ascontiguousarray(np.asarray(inp["L1_attn_ln_w"]).T)
    m["C_lnbT"] = np.ascontiguousarray(np.asarray(inp["L1_attn_ln_b"]).T)
    m["C_Kb"] = _T["C_Kb"]
    m["C_Qb"] = _T["C_Qb"]
    m["C_CT"] = _T["C_CT"]
    return m


def emit_B(P, nc, sb, x, y):
    wqkv = nc.dram_tensor("B_wqkv", [2, D, 1536], F32, kind="ExternalInput").ap()
    qsc = nc.dram_tensor("B_qsc", [2, 1, 512], F32, kind="ExternalInput").ap()
    ksc = nc.dram_tensor("B_ksc", [2, 1, 512], F32, kind="ExternalInput").ap()
    wo = nc.dram_tensor("B_wo", [2, 512, D], F32, kind="ExternalInput").ap()
    lng = nc.dram_tensor("B_lng", [2, 1, D], F32, kind="ExternalInput").ap()
    lnbi = nc.dram_tensor("B_lnbi", [2, 1, D], F32, kind="ExternalInput").ap()
    w1 = nc.dram_tensor("B_w1", [2, D, 5460], F32, kind="ExternalInput").ap()
    b1 = nc.dram_tensor("B_b1", [2, 1, 5460], F32, kind="ExternalInput").ap()
    w2 = nc.dram_tensor("B_w2", [2, 2730, D], F32, kind="ExternalInput").ap()
    b2 = nc.dram_tensor("B_b2", [2, 1, D], F32, kind="ExternalInput").ap()
    xc_ = nc.dram_tensor("B_xc", [S, 512], F32, kind="ExternalInput").ap()
    xs_ = nc.dram_tensor("B_xs", [S, 512], F32, kind="ExternalInput").ap()
    xck = nc.dram_tensor("B_xck", [S, 512], F32, kind="ExternalInput").ap()
    xsk = nc.dram_tensor("B_xsk", [S, 512], F32, kind="ExternalInput").ap()
    mB0 = nc.dram_tensor("B_mB0", [512, 512], BF16, kind="ExternalInput").ap()
    mB = nc.dram_tensor("B_mB", [512, 1024], BF16, kind="ExternalInput").ap()
    mB0T = nc.dram_tensor("B_mB0T", [512, 512], BF16, kind="ExternalInput").ap()
    mBT = nc.dram_tensor("B_mBT", [1024, 512], BF16, kind="ExternalInput").ap()
    z_h = nc.dram_tensor("B_z_h", [NB * S, D], F32, kind="Internal").ap()
    z2_h = nc.dram_tensor("B_z2_h", [NB * S, D], F32, kind="Internal").ap()
    h_h = nc.dram_tensor("B_h_h", [NB * S, D], BF16, kind="Internal").ap()
    qT_h = nc.dram_tensor("B_qT_h", [512, NB * S], BF16, kind="Internal").ap()
    kT_h = nc.dram_tensor("B_kT_h", [512, NB * S], BF16, kind="Internal").ap()
    v_h = nc.dram_tensor("B_v_h", [NB * S, 512], BF16, kind="Internal").ap()
    oT_h = nc.dram_tensor("B_oT_h", [512, NB * S], BF16, kind="Internal").ap()
    gT_h = nc.dram_tensor("B_gT_h", [2730, NB * S], BF16, kind="Internal").ap()

    with ExitStack() as stk:
        E = stk.enter_context
        wA = E(nc.sbuf_tensor("wA", [128, 8 * 5460], BF16))
        Wfs = E(nc.sbuf_tensor("B_Wfs", [128, 1536], F32))
        b1S = E(nc.sbuf_tensor("b1S", [1, 5460], BF16))
        b2S = E(nc.sbuf_tensor("b2S", [1, D], BF16))
        rowf = E(nc.sbuf_tensor("B_rowf", [1, 1536], F32))
        qscB = E(nc.sbuf_tensor("qscB", [128, 512], F32))
        kscB = E(nc.sbuf_tensor("kscB", [128, 512], F32))
        lngB = E(nc.sbuf_tensor("lngB", [128, D], F32))
        lnbB = E(nc.sbuf_tensor("lnbB", [128, D], F32))
        xt = E(nc.sbuf_tensor("B_xt", [128, D], F32))
        ht = E(nc.sbuf_tensor("B_ht", [128, D], F32))
        lnt = E(nc.sbuf_tensor("B_lnt", [128, D], F32))
        lnb = E(nc.sbuf_tensor("B_lnb", [128, D], BF16))
        lnT = E(nc.sbuf_tensor("B_lnT", [128, 8 * 128], BF16))
        cost = E(nc.sbuf_tensor("B_cost", [128, 512], F32))
        sint = E(nc.sbuf_tensor("B_sint", [128, 512], F32))
        qs0 = E(nc.sbuf_tensor("B_qs0", [128, 512], F32))
        nrm = E(nc.sbuf_tensor("B_nrm", [128, 16], F32))
        qn = E(nc.sbuf_tensor("B_qn", [128, 512], F32))
        qa = E(nc.sbuf_tensor("B_qa", [128, 512], F32))
        shq = E(nc.sbuf_tensor("B_shq", [128, 512], F32))
        qf = E(nc.sbuf_tensor("B_qf", [128, 512], F32))
        qfb = E(nc.sbuf_tensor("B_qfb", [128, 512], BF16))
        tstage = E(nc.sbuf_tensor("B_tstage", [128, 4 * 128], BF16))
        vb = E(nc.sbuf_tensor("B_vb", [128, 512], BF16))
        maskP = E(nc.sbuf_tensor("maskP", [128, 4 * 1024], BF16))
        maskT = E(nc.sbuf_tensor("maskT", [128, 8 * 512], BF16))
        qsb = E(nc.sbuf_tensor("B_qsb", [64, 512], BF16))
        ksb = E(nc.sbuf_tensor("B_ksb", [64, 1024], BF16))
        qhat = E(nc.sbuf_tensor("qhat", [65, 512], BF16))
        khat = E(nc.sbuf_tensor("khat", [65, 1024], BF16))
        vg = E(nc.sbuf_tensor("B_vg", [128, 8 * 65], BF16))
        ssb = sb["lnscr"]
        nmx = E(nc.sbuf_tensor("B_nmx", [128, 1], F32))
        nmr = E(nc.sbuf_tensor("B_nmr", [1, 128], F32))
        nmrb = E(nc.sbuf_tensor("B_nmrb", [1, 128], BF16))
        expb = E(nc.sbuf_tensor("B_expb", [128, 512], BF16))
        oTsb = E(nc.sbuf_tensor("B_oTsb", [65, 512], F32))
        r0 = E(nc.sbuf_tensor("B_r0", [1, 512], F32))
        rb = E(nc.sbuf_tensor("B_rb", [64, 512], F32))
        oTf = E(nc.sbuf_tensor("B_oTf", [64, 512], BF16))
        oTl = E(nc.sbuf_tensor("B_oTl", [128, 4 * 128], BF16))
        hb = E(nc.sbuf_tensor("B_hb", [128, D], BF16))
        gel = E(nc.sbuf_tensor("B_gel", [128, 512], F32))
        prodb = E(nc.sbuf_tensor("B_prodb", [128, 512], BF16))
        gTl = E(nc.sbuf_tensor("gTl", [128, 21 * 128], BF16))
        gTll = E(nc.sbuf_tensor("gTll", [42, 128], BF16))
        yt = lnt

        G = [sb[f"G{i}"] for i in range(6)]
        vgv = vg[:, :].rearrange("p (t c) -> p t c", c=65)

        def conv_load(dst_bf, src_ap, rows, cols):
            c0 = 0
            while c0 < cols:
                cw = min(1536, cols - c0)
                P.dma(Wfs[0:rows, 0:cw], src_ap[:, c0:c0 + cw] if rows == 128
                      else src_ap[0:rows, c0:c0 + cw])
                P.v("tensor_copy", dst_bf[0:rows, c0:c0 + cw], Wfs[0:rows, 0:cw])
                c0 += cw

        def bcast_row(src_l_ap, dst, w_):
            c0 = 0
            while c0 < w_:
                cw = min(512, w_ - c0)
                P.dma(rowf[:, 0:cw], src_l_ap[:, c0:c0 + cw])
                P.mm(G[0][:, 0:cw], sb["onesfr"][:, :], rowf[:, 0:cw],
                     start=True, stop=True)
                P.cp(dst[:, c0:c0 + cw], G[0][:, 0:cw])
                c0 += cw

        def shifted_load(dst, srcbuf, rr, i):
            P.dma(dst[:, 0:512], srcbuf[rr:rr + 128, 0:512])
            if i == 0:
                P.v("memset", dst[0:1, 512:1024], 0.0)
                P.dma(dst[1:128, 512:1024], srcbuf[rr:rr + 127, 512:1024])
            else:
                P.dma(dst[:, 512:1024], srcbuf[rr - 1:rr + 127, 512:1024])

        for b in range(NB):
            for i in range(NT):
                rr = b * S + i * 128
                P.dma(xt[:, :], x[rr:rr + 128, :])
                emit_ln(P, sb, lnt[:, :], xt[:, :], D)
                P.dma(z_h[rr:rr + 128, :], lnt[:, :])

        for l in range(2):
            for kc in range(8):
                conv_load(wA[:, kc * 1536:(kc + 1) * 1536], wqkv[l, kc * 128:(kc + 1) * 128, :],
                          128, 1536)
            bcast_row(qsc[l], qscB, 512)
            bcast_row(ksc[l], kscB, 512)
            for b in range(NB):
                r0b = b * S
                for i in range(NT):
                    rr = r0b + i * 128
                    shifted_load(ht, z_h, rr, i)
                    P.v("tensor_copy", lnb[:, :], ht[:, :])
                    emit_transpose_row(P, sb, lnb, lnT, 8)
                    for kc in range(8):
                        for n in range(3):
                            P.mm(G[n][:, :], lnT[:, kc * 128:(kc + 1) * 128],
                                 wA[:, kc * 1536 + n * 512: kc * 1536 + (n + 1) * 512],
                                 start=(kc == 0), stop=(kc == 7))
                    for (gi, scB, cO, sO, Tdst) in ((0, qscB, xc_, xs_, qT_h),
                                                    (1, kscB, xck, xsk, kT_h)):
                        P.cp(qs0[:, :], G[gi][:, :])
                        for hh in range(8):
                            P.act(sb["lnscr"][:, hh * 64:(hh + 1) * 64],
                                  qs0[:, hh * 64:(hh + 1) * 64], AF.Square,
                                  accum_out=nrm[:, hh:hh + 1])
                        P.act(nrm[:, 0:8], nrm[:, 0:8], AF.Sqrt)
                        P.v("tensor_scalar_max", nrm[:, 0:8], nrm[:, 0:8], 1e-12)
                        P.v("reciprocal", nrm[:, 8:16], nrm[:, 0:8])
                        for hh in range(8):
                            P.v("tensor_scalar_mul", qn[:, hh * 64:(hh + 1) * 64],
                                qs0[:, hh * 64:(hh + 1) * 64], nrm[:, 8 + hh:9 + hh])
                        P.v("tensor_mul", qn[:, :], qn[:, :], scB[:, :])
                        P.dma(cost[:, :], cO[i * 128:(i + 1) * 128, :])
                        P.dma(sint[:, :], sO[i * 128:(i + 1) * 128, :])
                        P.v("tensor_mul", qa[:, :], qn[:, :], cost[:, :])
                        qv = qn[:, :].rearrange("p (h a c) -> p h a c", a=2, c=32)
                        sv = shq[:, :].rearrange("p (h a c) -> p h a c", a=2, c=32)
                        P.v("tensor_copy", sv[:, :, 0, :], qv[:, :, 1, :])
                        P.v("tensor_copy", sv[:, :, 1, :], qv[:, :, 0, :])
                        P.v("tensor_mul", shq[:, :], shq[:, :], sint[:, :])
                        P.v("tensor_add", qf[:, :], qa[:, :], shq[:, :])
                        P.v("tensor_copy", qfb[:, :], qf[:, :])
                        for hp in range(4):
                            ps = sb["psT" if hp % 2 == 0 else "psT2"]
                            P.T(ps[:, :], qfb[:, hp * 128:(hp + 1) * 128],
                                sb["identb"][:, :])
                            P.cp(tstage[:, hp * 128:(hp + 1) * 128], ps[:, :])
                        P.dma(Tdst[:, rr:rr + 128].rearrange("(a p) c -> p a c", p=128),
                              tstage[:, :].rearrange("p (a c) -> p a c", c=128))
                    P.v("tensor_copy", vb[:, :], G[2][:, :])
                    P.dma(v_h[rr:rr + 128, :], vb[:, :])
            for w in range(4):
                kw = 512 if w == 0 else 1024
                nkt = kw // 128
                if w == 0:
                    P.dma(maskP[:, 0:4 * 512].rearrange("p (a c) -> p a c", c=512),
                          mB0.rearrange("(a p) c -> p a c", p=128))
                    P.dma(maskT[:, 0:4 * 512].rearrange("p (a c) -> p a c", c=512),
                          mB0T.rearrange("(a p) c -> p a c", p=128))
                elif w == 1:
                    P.dma(maskP[:, :].rearrange("p (a c) -> p a c", c=1024),
                          mB.rearrange("(a p) c -> p a c", p=128))
                    P.dma(maskT[:, :].rearrange("p (a c) -> p a c", c=512),
                          mBT.rearrange("(a p) c -> p a c", p=128))
                for b in range(NB):
                    r0b = b * S
                    k0g = r0b + (w * 512 - (0 if w == 0 else 512))
                    q0g = r0b + w * 512
                    for h in range(8):
                        P.dma(qsb[:, :], qT_h[h * 64:(h + 1) * 64, q0g:q0g + 512])
                        P.dma(ksb[:, 0:kw], kT_h[h * 64:(h + 1) * 64, k0g:k0g + kw])
                        P.dma(vg[:, 0:nkt * 65].rearrange("p (t c) -> p t c", c=65)[:, :, 0:64],
                              v_h[k0g:k0g + kw, h * 64:(h + 1) * 64]
                              .rearrange("(t p) c -> p t c", p=128))
                        P.v("memset",
                            vg[:, 0:nkt * 65].rearrange("p (t c) -> p t c", c=65)[:, :, 64:65],
                            1.0)
                        P.v("tensor_copy", qhat[0:64, :], qsb[:, :])
                        P.v("tensor_copy", khat[0:64, 0:kw], ksb[:, 0:kw])
                        P.v("memset", khat[64:65, 0:kw], 1.0)
                        for t in range(4):
                            for n in range(kw // 512):
                                P.mm(G[n][:, :], qsb[:, t * 128:(t + 1) * 128],
                                     ksb[:, n * 512:(n + 1) * 512], start=True, stop=True)
                            for n in range(kw // 512):
                                P.v("tensor_add", ssb[:, n * 512:(n + 1) * 512],
                                    G[n][:, :],
                                    maskP[:, t * kw + n * 512: t * kw + (n + 1) * 512])
                            P.v("reduce_max", nmx[:, :], ssb[:, 0:kw], axis=AX.X,
                                negate=True)
                            P.T(sb["psTf"][0:1, :], nmx[:, :], sb["identf"][:, :])
                            P.cp(nmr[:, :], sb["psTf"][0:1, :])
                            P.v("tensor_copy", nmrb[:, :], nmr[:, :])
                            P.dma(qhat[64:65, t * 128:(t + 1) * 128], nmrb[:, :])
                        for kt in range(nkt):
                            bank = G[0] if kt % 2 == 0 else G[2]
                            sbuf_s = ssb[:, 0:512] if kt % 2 == 0 else qa[:, :]
                            eb = expb if kt % 2 == 0 else qfb
                            P.mm(bank[:, :], khat[:, kt * 128:(kt + 1) * 128],
                                 qhat[:, :], start=True, stop=True)
                            if kt > 0:
                                peb = expb if (kt - 1) % 2 == 0 else qfb
                                P.mm(G[1][0:65, :],
                                     vg[:, 0:nkt * 65]
                                     .rearrange("p (t c) -> p t c", c=65)[:, kt - 1, :],
                                     peb[:, :], start=(kt == 1), stop=False)
                            P.v("tensor_add", sbuf_s, bank[:, :],
                                maskT[:, kt * 512:(kt + 1) * 512])
                            P.act(eb[:, :], sbuf_s, AF.Exp)
                        leb = expb if (nkt - 1) % 2 == 0 else qfb
                        P.mm(G[1][0:65, :],
                             vg[:, 0:nkt * 65].rearrange("p (t c) -> p t c", c=65)[:, nkt - 1, :],
                             leb[:, :], start=False, stop=True)
                        P.cp(oTsb[:, :], G[1][0:65, :])
                        P.dma(r0[:, :], oTsb[64:65, :])
                        P.v("reciprocal", r0[:, :], r0[:, :])
                        P.mm(G[0][0:64, :], sb["onesfr"][:, 0:64], r0[:, :],
                             start=True, stop=True)
                        P.cp(rb[:, :], G[0][0:64, :])
                        P.v("tensor_mul", oTf[:, :], oTsb[0:64, :], rb[:, :])
                        P.dma(oT_h[h * 64:(h + 1) * 64, q0g:q0g + 512], oTf[:, :])
            for kc in range(4):
                conv_load(wA[:, kc * D:(kc + 1) * D], wo[l, kc * 128:(kc + 1) * 128, :],
                          128, D)
            for b in range(NB):
                r0b = b * S
                for i in range(NT):
                    rr = r0b + i * 128
                    g0 = (i % 2) * 2
                    P.dma(oTl[:, :].rearrange("p (a c) -> p a c", c=128),
                          oT_h[:, rr:rr + 128].rearrange("(a p) c -> p a c", p=128))
                    for kc in range(4):
                        for n in range(2):
                            P.mm(G[g0 + n][:, :], oTl[:, kc * 128:(kc + 1) * 128],
                                 wA[:, kc * D + n * 512: kc * D + (n + 1) * 512],
                                 start=(kc == 0), stop=(kc == 3))
                    P.dma(xt[:, :], z_h[rr:rr + 128, :])
                    for n in range(2):
                        P.v("tensor_add", yt[:, n * 512:(n + 1) * 512], G[g0 + n][:, :],
                            xt[:, n * 512:(n + 1) * 512])
                    P.dma(z2_h[rr:rr + 128, :], yt[:, :])
            bcast_row(lng[l], lngB, D)
            bcast_row(lnbi[l], lnbB, D)
            for b in range(NB):
                r0b = b * S
                for i in range(NT):
                    rr = r0b + i * 128
                    shifted_load(ht, z2_h, rr, i)
                    emit_ln(P, sb, lnt[:, :], ht[:, :], D, gain=lngB[:, :],
                            bias_t=lnbB[:, :])
                    P.v("tensor_copy", hb[:, :], lnt[:, :])
                    P.dma(h_h[rr:rr + 128, :], hb[:, :])
            for kc in range(8):
                conv_load(wA[:, kc * 5460:(kc + 1) * 5460],
                          w1[l, kc * 128:(kc + 1) * 128, :], 128, 5460)
            conv_load(b1S, b1[l], 1, 5460)
            for b in range(NB):
                r0b = b * S
                for i in range(NT):
                    rr = r0b + i * 128
                    P.dma(hb[:, :], h_h[rr:rr + 128, :])
                    emit_transpose_row(P, sb, hb, lnT, 8)
                    for c in range(6):
                        c0 = c * 512
                        cw = min(512, 2730 - c0)
                        ga, gb_ = (G[0], G[1]) if c % 2 == 0 else (G[2], G[3])
                        gelc = gel if c % 2 == 0 else qf
                        prodc = prodb if c % 2 == 0 else vb
                        for kc in range(8):
                            P.mm(ga[:, 0:cw], lnT[:, kc * 128:(kc + 1) * 128],
                                 wA[:, kc * 5460 + c0: kc * 5460 + c0 + cw],
                                 start=(kc == 0), stop=False)
                            P.mm(gb_[:, 0:cw], lnT[:, kc * 128:(kc + 1) * 128],
                                 wA[:, kc * 5460 + 2730 + c0: kc * 5460 + 2730 + c0 + cw],
                                 start=(kc == 0), stop=False)
                        P.mm(ga[:, 0:cw], sb["onesb"][:, :], b1S[:, c0:c0 + cw],
                             start=False, stop=True)
                        P.mm(gb_[:, 0:cw], sb["onesb"][:, :],
                             b1S[:, 2730 + c0:2730 + c0 + cw], start=False, stop=True)
                        P.act(gelc[:, 0:cw], gb_[:, 0:cw], AF.Gelu)
                        P.v("tensor_mul", prodc[:, 0:cw], ga[:, 0:cw], gelc[:, 0:cw])
                        nsub = (cw + 127) // 128
                        for tsub in range(nsub):
                            tw = min(128, cw - tsub * 128)
                            ps = sb["psT" if tsub % 2 == 0 else "psT2"]
                            P.T(ps[0:tw, :], prodc[:, tsub * 128:tsub * 128 + tw],
                                sb["identb"][:, :])
                            P.cp(tstage[0:tw, tsub * 128:(tsub + 1) * 128],
                                 ps[0:tw, :])
                            P.dma(gT_h[c0 + tsub * 128: c0 + tsub * 128 + tw, rr:rr + 128],
                                  tstage[0:tw, tsub * 128:(tsub + 1) * 128])
            for kc in range(22):
                r = kc * 128
                nr = min(128, 2730 - r)
                conv_load(wA[:, kc * D:(kc + 1) * D], w2[l, r:r + nr, :], nr, D)
            conv_load(b2S, b2[l], 1, D)
            for b in range(NB):
                r0b = b * S
                for i in range(NT):
                    rr = r0b + i * 128
                    P.dma(gTl[:, :].rearrange("p (a c) -> p a c", c=128),
                          gT_h[0:2688, rr:rr + 128].rearrange("(a p) c -> p a c", p=128))
                    P.dma(gTll[:, :], gT_h[2688:2730, rr:rr + 128])
                    g0 = 2 if i % 2 == 0 else 4
                    for kc in range(22):
                        nr = min(128, 2730 - kc * 128)
                        lhs = (gTl[:, kc * 128:(kc + 1) * 128] if kc < 21
                               else gTll[:, :])
                        for n in range(2):
                            P.mm(G[g0 + n][:, :], lhs,
                                 wA[0:nr, kc * D + n * 512: kc * D + (n + 1) * 512],
                                 start=(kc == 0), stop=False)
                    for n in range(2):
                        P.mm(G[g0 + n][:, :], sb["onesb"][:, :],
                             b2S[:, n * 512:(n + 1) * 512], start=False, stop=True)
                    P.dma(xt[:, :], z2_h[rr:rr + 128, :])
                    for n in range(2):
                        P.v("tensor_add", yt[:, n * 512:(n + 1) * 512], G[g0 + n][:, :],
                            xt[:, n * 512:(n + 1) * 512])
                    if l == 1:
                        P.dma(xt[:, :], x[rr:rr + 128, :])
                        P.v("tensor_add", yt[:, :], yt[:, :], xt[:, :])
                        P.dma(y[rr:rr + 128, :], yt[:, :])
                    else:
                        P.dma(z_h[rr:rr + 128, :], yt[:, :])


def map_B(inp):
    m = {}
    m["B_wqkv"] = np.ascontiguousarray(inp["L0_ffn_wqkv"])
    m["B_qsc"] = np.ascontiguousarray(
        np.stack([np.tile(np.asarray(inp["L0_ffn_q_scale"])[l], 8)[None, :]
                  for l in range(2)]))
    m["B_ksc"] = np.ascontiguousarray(
        np.stack([np.tile(np.asarray(inp["L0_ffn_k_scale"])[l], 8)[None, :]
                  for l in range(2)]))
    m["B_wo"] = np.ascontiguousarray(inp["L0_ffn_wo"])
    m["B_lng"] = np.ascontiguousarray(np.asarray(inp["L0_ffn_ln_g"])[:, None, :])
    m["B_lnbi"] = np.ascontiguousarray(np.asarray(inp["L0_ffn_ln_b"])[:, None, :])
    m["B_w1"] = np.ascontiguousarray(inp["L0_ffn_w1"])
    m["B_b1"] = np.ascontiguousarray(np.asarray(inp["L0_ffn_b1"])[:, None, :])
    m["B_w2"] = np.ascontiguousarray(inp["L0_ffn_w2"])
    m["B_b2"] = np.ascontiguousarray(np.asarray(inp["L0_ffn_b2"])[:, None, :])
    m["B_xc"] = _T["xpos_cq"]
    m["B_xs"] = _T["xpos_sq"]
    m["B_xck"] = _T["xpos_ck"]
    m["B_xsk"] = _T["xpos_sk"]
    m["B_mB0"] = _T["maskB0"]
    m["B_mB"] = _T["maskB"]
    m["B_mB0T"] = _T["maskB0T"]
    m["B_mBT"] = _T["maskBT"]
    return m


EMITS = [emit_A, emit_B, emit_C, emit_D]
MAPS = [map_A, map_B, map_C, map_D]


def build_fused(stages=(0, 1, 2, 3)):
    nc = bass.Bass("TRN2", target_bir_lowering=False)
    aps = {}
    decl_common_inputs(nc, aps)
    x = nc.dram_tensor("x", [NB * S, D], F32, kind="ExternalInput").ap()
    y = nc.dram_tensor("y", [NB * S, D], F32, kind="ExternalOutput").ap()
    mids = [nc.dram_tensor(f"mid{i}", [NB * S, D], F32, kind="Internal").ap()
            for i in range(len(stages) - 1)]
    chain = [x] + mids + [y]
    P = Prog()
    with ExitStack() as stk:
        sb = {}
        alloc_common(stk.enter_context, nc, sb)
        sems = {e: stk.enter_context(nc.semaphore(f"sem_{e}")) for e in ENGINES}
        load_common(P, sb, aps)
        for k, si in enumerate(stages):
            P.barrier()
            EMITS[si](P, nc, sb, chain[k], chain[k + 1])
        finalize(nc, P, sems)
    return nc


def map_fused(xc, inp, stages=(0, 1, 2, 3)):
    m = common_in_map()
    m["x"] = xc
    for si in stages:
        m.update(MAPS[si](inp))
    return m


def kernel(**inp):
    x = np.ascontiguousarray(np.asarray(inp["L0_attn_x"], dtype=np.float32))
    xs = [np.ascontiguousarray(x[2 * c:2 * c + 2].reshape(NB * S, D))
          for c in range(NCORES)]
    nc = build_fused()
    maps = [map_fused(xs[c], inp) for c in range(NCORES)]
    res = run_bass_kernel_spmd(nc, maps, core_ids=list(range(NCORES)))
    out = np.stack([np.asarray(r["y"], np.float32).reshape(NB, S, D)
                    for r in res.results]).reshape(16, S, D)
    return out


# version 05
# speedup vs baseline: 1.5222x; 1.0005x over previous
import numpy as np
import ml_dtypes
import concourse.bass as bass
import concourse.mybir as mybir
from concourse.bass_utils import run_bass_kernel_spmd
from contextlib import ExitStack

F32 = mybir.dt.float32
BF16 = mybir.dt.bfloat16
AF = mybir.ActivationFunctionType
ALU = mybir.AluOpType
AX = mybir.AxisListType
BF = ml_dtypes.bfloat16

ENGINES = ("sync", "scalar", "vector", "tensor", "gpsimd")
NCORES = 8
S = 2048
NB = 2
NT = S // 128
D = 1024


import bass_rust as _br

SELF_SYNC = ("scalar", "vector", "gpsimd")
CHAIN_SERIAL = ("sync",)


class Prog:
    def __init__(self):
        self.ops = []

    def dma(self, out, in_, eng="sync", **kw):
        self.ops.append((eng, "dma_start", (out, in_), kw, 16))

    def op(self, eng, name, *args, **kw):
        self.ops.append((eng, name, args, kw, 1))

    def barrier(self):
        self.ops.append(("BARRIER", None, None, None, 0))

    def mm(self, out, lhsT, rhs, start, stop):
        self.op("tensor", "matmul", out, lhsT, rhs, start=start, stop=stop)

    def T(self, out, in_, ident):
        self.op("tensor", "transpose", out, in_, ident)

    def cp(self, dst, src):
        self.op("scalar", "copy", dst, src)

    def act(self, out, in_, func, **kw):
        self.op("scalar", "activation", out, in_, func, **kw)

    def v(self, name, *args, **kw):
        self.op("vector", name, *args, **kw)


def _rw_sets(opname, args, kw):
    aps = [a for a in args if isinstance(a, _br.AP)]
    writes = [aps[0].tensor.name] if aps else []
    reads = [a.tensor.name for a in aps[1:]]
    for k, v in kw.items():
        if isinstance(v, _br.AP):
            if k == "accum_out":
                writes.append(v.tensor.name)
            else:
                reads.append(v.tensor.name)
    if opname == "matmul" and not kw.get("start", True):
        reads.append(writes[0])
    return reads, writes


def finalize(nc, P, sems):
    ops = P.ops
    n = len(ops)
    cum = {e: 0 for e in ENGINES}
    opval = [0] * n
    openg = [None] * n
    waits = [None] * n
    last_w = {}
    last_r = {}
    prev_op = {}
    bar = {e: 0 for e in ENGINES}
    for i, (e, opname, args, kw, d) in enumerate(ops):
        if e == "BARRIER":
            bar = dict(cum)
            last_w.clear()
            last_r.clear()
            continue
        cum[e] += d
        opval[i] = cum[e]
        openg[i] = e
        dep = {}
        if e in CHAIN_SERIAL and prev_op.get(e) is not None:
            dep[e] = opval[prev_op[e]]
        prev_op[e] = i
        def add(j):
            f = openg[j]
            if f != e or e in SELF_SYNC:
                dep[f] = max(dep.get(f, 0), opval[j])
        reads, writes = _rw_sets(opname, args, kw)
        for b in reads:
            for f, j in last_w.get(b, {}).items():
                add(j)
        for b in writes:
            for f, j in last_w.get(b, {}).items():
                add(j)
            for f, j in last_r.get(b, {}).items():
                add(j)
        for f, v in bar.items():
            if f != e and v > 0:
                dep[f] = max(dep.get(f, 0), v)
        waits[i] = dep
        for b in reads:
            last_r.setdefault(b, {})[e] = i
        for b in writes:
            last_w.setdefault(b, {})[e] = i
            lr = last_r.get(b)
            if lr is not None:
                lr.clear()
            last_r.setdefault(b, {})[e] = i
    total = dict(cum)
    with nc.Block() as block:
        def mk(eng_name):
            def body(E):
                water = {f: 0 for f in ENGINES}
                for i, (e, opname, args, kw, d) in enumerate(ops):
                    if e != eng_name:
                        continue
                    for f, v in waits[i].items():
                        if v > water[f]:
                            E.wait_ge(sems[f], v)
                            water[f] = v
                    getattr(E, opname)(*args, **kw).then_inc(sems[e], d)
                for f in ENGINES:
                    if f != eng_name and total[f] > water[f]:
                        E.wait_ge(sems[f], total[f])
                        water[f] = total[f]
            return body
        for en in ENGINES:
            getattr(block, en)(mk(en))


def _tables():
    t = {}
    t["identb"] = np.eye(128, dtype=np.float32).astype(BF)
    t["identf"] = np.eye(128, dtype=np.float32)
    t["onesb"] = np.ones((1, 128), dtype=np.float32).astype(BF)
    t["onesfr"] = np.ones((1, 128), dtype=np.float32)
    t["onesfc"] = np.ones((128, 1), dtype=np.float32)

    hd = 64
    inv = 1.0 / 10000.0 ** (np.arange(0, hd, 2, dtype=np.float64) / hd)
    pos = np.arange(S, dtype=np.float64)
    fr = np.outer(pos, inv)
    emb = np.concatenate([fr, fr], -1)
    cos = np.cos(emb)
    sin = np.sin(emb)
    sgn = np.where(np.arange(hd) % 2 == 0, -1.0, 1.0)
    c = 8.0 ** -0.5
    t["ropeA_cos"] = np.tile(cos * c, (1, 16)).astype(np.float32)
    t["ropeA_sin"] = np.tile(sin * sgn * c, (1, 16)).astype(np.float32)

    WIN = 512
    half = np.arange(0, hd, 2, dtype=np.float64)
    invf = 1.0 / 10000.0 ** (half / hd)
    tt = np.arange(S, dtype=np.float64)
    fr2 = tt[:, None] * invf[None, :]
    fr2 = np.concatenate([fr2, fr2], -1)
    base = (half + 0.4 * hd) / (1.4 * hd)
    power = (tt - S // 2) / float(WIN // 2)
    scale = base[None, :] ** power[:, None]
    scale = np.concatenate([scale, scale], -1)
    cos2 = np.cos(fr2)
    sin2 = np.sin(fr2)
    sgn2 = np.where(np.arange(hd) < 32, -1.0, 1.0)
    t["xpos_cq"] = np.tile(cos2 * scale * 8.0, (1, 8)).astype(np.float32)
    t["xpos_sq"] = np.tile(sin2 * scale * sgn2 * 8.0, (1, 8)).astype(np.float32)
    t["xpos_ck"] = np.tile(cos2 / scale, (1, 8)).astype(np.float32)
    t["xpos_sk"] = np.tile(sin2 * sgn2 / scale, (1, 8)).astype(np.float32)

    NEG = -1.0e30
    n = S
    w = n // WIN
    bt = np.arange(n).reshape(w, WIN)
    btp = np.pad(bt, ((1, 0), (0, 0)), constant_values=-1)
    btk = np.concatenate([btp[:w], btp[1:]], axis=1)
    m_list = []
    for wi in range(w):
        i = bt[wi][:, None]
        j = btk[wi][None, :]
        mask = (i < j) | (i > j + WIN) | (j == -1)
        m_list.append(np.where(mask, NEG, 0.0).astype(np.float32))
    assert np.array_equal(m_list[1], m_list[2])
    t["maskB0"] = m_list[0][:, 512:].astype(BF)
    t["maskB"] = m_list[1].astype(BF)
    t["maskB0T"] = np.ascontiguousarray(t["maskB0"].T)
    t["maskBT"] = np.ascontiguousarray(t["maskB"].T)

    H = 16
    slopes = 2.0 ** (-8.0 * np.arange(1, H + 1, dtype=np.float64) / H)
    j = np.arange(S, dtype=np.float64)
    t["C_Kb"] = np.stack([j, np.ones_like(j)], 0).astype(np.float32)
    Qb = np.zeros((2, H, S))
    for h in range(H):
        Qb[0, h, :] = slopes[h]
        Qb[1, h, :] = -slopes[h] * j
    t["C_Qb"] = Qb.reshape(2, H * S).astype(np.float32)
    CT = np.zeros((16, 128, 512), dtype=np.float32)
    for kt in range(16):
        qc = kt // 4
        jj = kt * 128 + np.arange(128)[:, None]
        ii = qc * 512 + np.arange(512)[None, :]
        CT[kt] = np.where(jj > ii, NEG, 0.0)
    t["C_CT"] = CT
    return t


_T = _tables()


def emit_ln(P, sb, out, in_, cols, gain=None, bias_t=None):
    st = sb["stats"]
    scr = sb["lnscr"]
    P.act(scr[:, 0:cols], in_, AF.Square, accum_out=st[:, 1:2])
    P.v("reduce_sum", st[:, 0:1], in_, axis=AX.X)
    P.v("tensor_scalar_mul", st[:, 2:3], st[:, 0:1], 1.0 / cols)
    P.v("tensor_mul", st[:, 3:4], st[:, 2:3], st[:, 2:3])
    P.v("tensor_scalar", st[:, 4:5], st[:, 1:2], 1.0 / cols, None, ALU.mult)
    P.v("tensor_sub", st[:, 4:5], st[:, 4:5], st[:, 3:4])
    P.v("tensor_scalar_add", st[:, 4:5], st[:, 4:5], 1e-5)
    P.act(st[:, 5:6], st[:, 4:5], AF.Sqrt)
    P.v("reciprocal", st[:, 6:7], st[:, 5:6])
    P.v("tensor_scalar", out, in_, st[:, 2:3], st[:, 6:7], ALU.subtract, ALU.mult)
    if gain is not None:
        P.v("tensor_mul", out, out, gain)
        P.v("tensor_add", out, out, bias_t)


def emit_transpose_row(P, sb, lnb_bf, lnT_bf, nk):
    for kc in range(nk):
        ps = sb["psT" if kc % 2 == 0 else "psT2"]
        P.T(ps[:, :], lnb_bf[:, kc * 128:(kc + 1) * 128], sb["identb"][:, :])
        P.cp(lnT_bf[:, kc * 128:(kc + 1) * 128], ps[:, :])


def alloc_common(E, nc, sb):
    sb["identb"] = E(nc.sbuf_tensor("identb_s", [128, 128], BF16))
    sb["identf"] = E(nc.sbuf_tensor("identf_s", [128, 128], F32))
    sb["onesb"] = E(nc.sbuf_tensor("onesb_s", [1, 128], BF16))
    sb["onesfr"] = E(nc.sbuf_tensor("onesfr_s", [1, 128], F32))
    sb["onesfc"] = E(nc.sbuf_tensor("onesfc_s", [128, 1], F32))
    sb["stats"] = E(nc.sbuf_tensor("stats", [128, 16], F32))
    sb["lnscr"] = E(nc.sbuf_tensor("lnscr", [128, 1024], F32))
    for i in range(6):
        sb[f"G{i}"] = E(nc.psum_tensor(f"G{i}", [128, 512], F32))
    sb["psT"] = E(nc.psum_tensor("psT", [128, 128], BF16))
    sb["psTf"] = E(nc.psum_tensor("psTf", [128, 128], F32))
    sb["psT2"] = sb["psT"]


def load_common(P, sb, aps):
    for nm in ("identb", "identf", "onesb", "onesfr", "onesfc"):
        P.dma(sb[nm][:, :], aps[nm])


def decl_common_inputs(nc, aps):
    for nm in ("identb", "identf", "onesb", "onesfr", "onesfc"):
        arr = _T[nm]
        dt = BF16 if arr.dtype == BF else F32
        aps[nm] = nc.dram_tensor(nm, list(arr.shape), dt, kind="ExternalInput").ap()


def common_in_map():
    return {nm: _T[nm] for nm in ("identb", "identf", "onesb", "onesfr", "onesfc")}


def emit_A(P, nc, sb, x, y):
    W = nc.dram_tensor("A_W", [D, 3 * D], F32, kind="ExternalInput").ap()
    bqkv = nc.dram_tensor("A_bqkv", [1, 3 * D], F32, kind="ExternalInput").ap()
    WoT = nc.dram_tensor("A_WoT", [D, D], F32, kind="ExternalInput").ap()
    bo = nc.dram_tensor("A_bo", [1, D], F32, kind="ExternalInput").ap()
    rc = nc.dram_tensor("A_ropec", [S, D], F32, kind="ExternalInput").ap()
    rs = nc.dram_tensor("A_ropes", [S, D], F32, kind="ExternalInput").ap()
    qT_h = nc.dram_tensor("A_qT_h", [D, S], BF16, kind="Internal").ap()
    kT_h = nc.dram_tensor("A_kT_h", [D, S], BF16, kind="Internal").ap()
    v_h = nc.dram_tensor("A_v_h", [S, D], BF16, kind="Internal").ap()
    oT_h = nc.dram_tensor("A_oT_h", [D, S], BF16, kind="Internal").ap()

    with ExitStack() as stk:
        E = stk.enter_context
        Wsb = E(nc.sbuf_tensor("A_Wsb", [128, 8 * 3 * D], BF16))
        Wfs = E(nc.sbuf_tensor("Wfs", [128, 3 * D], F32))
        WoTsb = E(nc.sbuf_tensor("WoTsb", [128, 8 * D], BF16))
        bsb = E(nc.sbuf_tensor("bsb", [1, 3 * D], BF16))
        bosb = E(nc.sbuf_tensor("bosb", [1, D], BF16))
        bscr = E(nc.sbuf_tensor("bscr", [1, 3 * D], F32))
        xt = E(nc.sbuf_tensor("xt", [128, D], F32))
        lnt = E(nc.sbuf_tensor("lnt", [128, D], F32))
        lnb = E(nc.sbuf_tensor("lnb", [128, D], BF16))
        lnT = E(nc.sbuf_tensor("lnT", [128, 8 * 128], BF16))
        cost = E(nc.sbuf_tensor("cost", [128, D], F32))
        sint = E(nc.sbuf_tensor("sint", [128, D], F32))
        qs0 = E(nc.sbuf_tensor("qs0", [128, D], F32))
        qa = E(nc.sbuf_tensor("qa", [128, D], F32))
        shq = E(nc.sbuf_tensor("shq", [128, D], F32))
        qf = E(nc.sbuf_tensor("qf", [128, D], F32))
        qfb = E(nc.sbuf_tensor("qfb", [128, D], BF16))
        tstage = E(nc.sbuf_tensor("tstage", [128, 8 * 128], BF16))
        vb = E(nc.sbuf_tensor("vb", [128, D], BF16))
        qsb = E(nc.sbuf_tensor("qsb", [64, S], BF16))
        ksb = E(nc.sbuf_tensor("ksb", [64, S], BF16))
        vg = E(nc.sbuf_tensor("vg", [128, 16 * 65], BF16))
        expb = E(nc.sbuf_tensor("expb", [128, 512], BF16))
        expb2 = E(nc.sbuf_tensor("expb2", [128, 512], BF16))
        expbs = [expb, expb2]
        oTsb = E(nc.sbuf_tensor("oTsb", [65, 512], F32))
        r0 = E(nc.sbuf_tensor("r0", [1, 512], F32))
        rb = E(nc.sbuf_tensor("rb", [64, 512], F32))
        oTf = E(nc.sbuf_tensor("oTf", [64, 512], BF16))
        oTl = E(nc.sbuf_tensor("oTl", [128, 8 * 128], BF16))
        yt = E(nc.sbuf_tensor("yt", [128, D], F32))

        for kc in range(8):
            P.dma(Wfs[:, :], W[kc * 128:(kc + 1) * 128, :])
            P.v("tensor_copy", Wsb[:, kc * 3 * D:(kc + 1) * 3 * D], Wfs[:, :])
        for kc in range(8):
            P.dma(Wfs[:, 0:D], WoT[kc * 128:(kc + 1) * 128, :])
            P.v("tensor_copy", WoTsb[:, kc * D:(kc + 1) * D], Wfs[:, 0:D])
        P.dma(bscr[:, :], bqkv)
        P.v("tensor_copy", bsb[:, :], bscr[:, :])
        P.dma(bscr[:, 0:D], bo)
        P.v("tensor_copy", bosb[:, :], bscr[:, 0:D])

        G = [sb[f"G{i}"] for i in range(6)]
        vgv = vg[:, :].rearrange("p (t c) -> p t c", c=65)
        for b in range(NB):
            r0b = b * S
            for i in range(NT):
                rr = r0b + i * 128
                P.dma(xt[:, :], x[rr:rr + 128, :])
                emit_ln(P, sb, lnt[:, :], xt[:, :], D)
                P.v("tensor_copy", lnb[:, :], lnt[:, :])
                emit_transpose_row(P, sb, lnb, lnT, 8)
                P.dma(cost[:, :], rc[i * 128:(i + 1) * 128, :])
                P.dma(sint[:, :], rs[i * 128:(i + 1) * 128, :])
                for kc in range(8):
                    for n in range(6):
                        P.mm(G[n][:, :], lnT[:, kc * 128:(kc + 1) * 128],
                             Wsb[:, kc * 3 * D + n * 512: kc * 3 * D + (n + 1) * 512],
                             start=(kc == 0), stop=False)
                for n in range(6):
                    P.mm(G[n][:, :], sb["onesb"][:, :], bsb[:, n * 512:(n + 1) * 512],
                         start=False, stop=True)
                for (g0, g1, Tdst) in ((0, 1, qT_h), (2, 3, kT_h)):
                    P.cp(qs0[:, 0:512], G[g0][:, :])
                    P.cp(qs0[:, 512:1024], G[g1][:, :])
                    P.v("tensor_mul", qa[:, :], qs0[:, :], cost[:, :])
                    P.v("tensor_copy", shq[:, 0:D:2], qs0[:, 1:D:2])
                    P.v("tensor_copy", shq[:, 1:D:2], qs0[:, 0:D:2])
                    P.v("tensor_mul", shq[:, :], shq[:, :], sint[:, :])
                    P.v("tensor_add", qf[:, :], qa[:, :], shq[:, :])
                    P.v("tensor_copy", qfb[:, :], qf[:, :])
                    for hp in range(8):
                        ps = sb["psT" if hp % 2 == 0 else "psT2"]
                        P.T(ps[:, :], qfb[:, hp * 128:(hp + 1) * 128],
                            sb["identb"][:, :])
                        P.cp(tstage[:, hp * 128:(hp + 1) * 128], ps[:, :])
                    P.dma(Tdst[:, i * 128:(i + 1) * 128]
                          .rearrange("(a p) c -> p a c", p=128),
                          tstage[:, :].rearrange("p (a c) -> p a c", c=128))
                P.cp(vb[:, 0:512], G[4][:, :])
                P.cp(vb[:, 512:1024], G[5][:, :])
                P.dma(v_h[i * 128:(i + 1) * 128, :], vb[:, :])
            for h in range(16):
                P.dma(qsb[:, :], qT_h[h * 64:(h + 1) * 64, :])
                P.dma(ksb[:, :], kT_h[h * 64:(h + 1) * 64, :])
                P.dma(vgv[:, :, 0:64],
                      v_h[:, h * 64:(h + 1) * 64].rearrange("(t p) c -> p t c", p=128))
                P.v("memset", vgv[:, :, 64:65], 1.0)
                for qc in range(4):
                    for kt in range(16):
                        gs = G[kt % 2]
                        P.mm(gs[:, :], ksb[:, kt * 128:(kt + 1) * 128],
                             qsb[:, qc * 512:(qc + 1) * 512], start=True, stop=True)
                        if kt > 0:
                            P.mm(G[2 + qc][0:65, :], vgv[:, kt - 1, :],
                                 expbs[(kt - 1) % 2][:, :],
                                 start=(kt == 1), stop=False)
                        P.act(expbs[kt % 2][:, :], gs[:, :], AF.Exp)
                    P.mm(G[2 + qc][0:65, :], vgv[:, 15, :], expbs[1][:, :],
                         start=False, stop=True)
                for qc in range(4):
                    P.cp(oTsb[:, :], G[2 + qc][0:65, :])
                    P.dma(r0[:, :], oTsb[64:65, :])
                    P.v("reciprocal", r0[:, :], r0[:, :])
                    P.mm(G[0][0:64, :], sb["onesfr"][:, 0:64], r0[:, :],
                         start=True, stop=True)
                    P.cp(rb[:, :], G[0][0:64, :])
                    P.v("tensor_mul", oTf[:, :], oTsb[0:64, :], rb[:, :])
                    P.dma(oT_h[h * 64:(h + 1) * 64, qc * 512:(qc + 1) * 512],
                          oTf[:, :])
            for i in range(NT):
                rr = r0b + i * 128
                g0 = (i % 2) * 2
                P.dma(oTl[:, :].rearrange("p (a c) -> p a c", c=128),
                      oT_h[:, i * 128:(i + 1) * 128].rearrange("(a p) c -> p a c", p=128))
                for kc in range(8):
                    for n in range(2):
                        P.mm(G[g0 + n][:, :], oTl[:, kc * 128:(kc + 1) * 128],
                             WoTsb[:, kc * D + n * 512: kc * D + (n + 1) * 512],
                             start=(kc == 0), stop=False)
                for n in range(2):
                    P.mm(G[g0 + n][:, :], sb["onesb"][:, :], bosb[:, n * 512:(n + 1) * 512],
                         start=False, stop=True)
                P.dma(xt[:, :], x[rr:rr + 128, :])
                for n in range(2):
                    P.v("tensor_add", yt[:, n * 512:(n + 1) * 512], G[g0 + n][:, :],
                        xt[:, n * 512:(n + 1) * 512])
                P.dma(y[rr:rr + 128, :], yt[:, :])


def map_A(inp):
    m = {}
    m["A_W"] = np.ascontiguousarray(
        np.concatenate([np.asarray(inp["L0_attn_Wq"]).T, np.asarray(inp["L0_attn_Wk"]).T,
                        np.asarray(inp["L0_attn_Wv"]).T], axis=1))
    m["A_bqkv"] = np.ascontiguousarray(
        np.concatenate([inp["L0_attn_bq"], inp["L0_attn_bk"],
                        inp["L0_attn_bv"]])[None, :])
    m["A_WoT"] = np.ascontiguousarray(np.asarray(inp["L0_attn_Wo"]).T)
    m["A_bo"] = np.ascontiguousarray(np.asarray(inp["L0_attn_bo"])[None, :])
    m["A_ropec"] = _T["ropeA_cos"]
    m["A_ropes"] = _T["ropeA_sin"]
    return m


def emit_D(P, nc, sb, x, y):
    Win = nc.dram_tensor("D_Win", [D, 64], F32, kind="ExternalInput").ap()
    bin_ = nc.dram_tensor("D_bin", [1, 64], F32, kind="ExternalInput").ap()
    W1r = nc.dram_tensor("D_W1r", [1, 1024], F32, kind="ExternalInput").ap()
    b1r = nc.dram_tensor("D_b1r", [1, 1024], F32, kind="ExternalInput").ap()
    W2r = nc.dram_tensor("D_W2r", [1, 1024], F32, kind="ExternalInput").ap()
    b2r = nc.dram_tensor("D_b2r", [1, 64], F32, kind="ExternalInput").ap()
    Wout = nc.dram_tensor("D_Wout", [64, D], F32, kind="ExternalInput").ap()
    bout = nc.dram_tensor("D_bout", [1, D], F32, kind="ExternalInput").ap()

    with ExitStack() as stk:
        E = stk.enter_context
        Wfs = E(nc.sbuf_tensor("D_Wfs", [128, 1024], F32))
        Winb = E(nc.sbuf_tensor("Winb", [128, 8 * 64], BF16))
        Woutb = E(nc.sbuf_tensor("Woutb", [64, D], BF16))
        binb = E(nc.sbuf_tensor("binb", [1, 64], BF16))
        boutb = E(nc.sbuf_tensor("boutb", [1, D], BF16))
        rowf = E(nc.sbuf_tensor("rowf", [1, 1024], F32))
        W1b = E(nc.sbuf_tensor("W1b", [128, 1024], F32))
        b1b = E(nc.sbuf_tensor("b1b", [128, 1024], F32))
        W2b = E(nc.sbuf_tensor("W2b", [128, 1024], F32))
        b2b = E(nc.sbuf_tensor("b2b", [128, 64], F32))
        xt = E(nc.sbuf_tensor("D_xt", [128, D], F32))
        lnt = E(nc.sbuf_tensor("D_lnt", [128, D], F32))
        lnb = E(nc.sbuf_tensor("D_lnb", [128, D], BF16))
        lnT = E(nc.sbuf_tensor("D_lnT", [128, 8 * 128], BF16))
        st_ = E(nc.sbuf_tensor("st_", [128, 64], F32))
        htmp = E(nc.sbuf_tensor("htmp", [128, 64], F32))
        gacc = E(nc.sbuf_tensor("gacc", [128, 64], F32))
        gb = E(nc.sbuf_tensor("gb", [128, 64], BF16))
        gT = E(nc.sbuf_tensor("gT", [64, 128], BF16))
        yt = E(nc.sbuf_tensor("D_yt", [128, D], F32))

        G = [sb[f"G{i}"] for i in range(6)]
        for kc in range(8):
            P.dma(Wfs[:, 0:64], Win[kc * 128:(kc + 1) * 128, :])
            P.v("tensor_copy", Winb[:, kc * 64:(kc + 1) * 64], Wfs[:, 0:64])
        P.dma(Wfs[0:64, 0:D], Wout)
        P.v("tensor_copy", Woutb[:, :], Wfs[0:64, 0:D])
        P.dma(rowf[:, 0:64], bin_)
        P.v("tensor_copy", binb[:, :], rowf[:, 0:64])
        P.dma(rowf[:, 0:D], bout)
        P.v("tensor_copy", boutb[:, :], rowf[:, 0:D])
        for src, dst in ((W1r, W1b), (b1r, b1b), (W2r, W2b)):
            P.dma(rowf[:, :], src)
            for n in range(2):
                P.mm(G[0][:, :], sb["onesfr"][:, :], rowf[:, n * 512:(n + 1) * 512],
                     start=True, stop=True)
                P.cp(dst[:, n * 512:(n + 1) * 512], G[0][:, :])
        P.dma(rowf[:, 0:64], b2r)
        P.mm(G[0][:, 0:64], sb["onesfr"][:, :], rowf[:, 0:64], start=True, stop=True)
        P.cp(b2b[:, :], G[0][:, 0:64])

        for b in range(NB):
            for i in range(NT):
                rr = b * S + i * 128
                P.dma(xt[:, :], x[rr:rr + 128, :])
                emit_ln(P, sb, lnt[:, :], xt[:, :], D)
                P.v("tensor_copy", lnb[:, :], lnt[:, :])
                emit_transpose_row(P, sb, lnb, lnT, 8)
                for kc in range(8):
                    P.mm(G[0][:, 0:64], lnT[:, kc * 128:(kc + 1) * 128],
                         Winb[:, kc * 64:(kc + 1) * 64], start=(kc == 0), stop=False)
                P.mm(G[0][:, 0:64], sb["onesb"][:, :], binb[:, :],
                     start=False, stop=True)
                P.cp(st_[:, :], G[0][:, 0:64])
                for ii in range(16):
                    P.v("tensor_mul", htmp[:, :], st_[:, :], W1b[:, ii * 64:(ii + 1) * 64])
                    P.v("tensor_add", htmp[:, :], htmp[:, :], b1b[:, ii * 64:(ii + 1) * 64])
                    P.v("tensor_scalar_max", htmp[:, :], htmp[:, :], 0.0)
                    P.v("tensor_mul", htmp[:, :], htmp[:, :], W2b[:, ii * 64:(ii + 1) * 64])
                    if ii == 0:
                        P.v("tensor_copy", gacc[:, :], htmp[:, :])
                    else:
                        P.v("tensor_add", gacc[:, :], gacc[:, :], htmp[:, :])
                P.v("tensor_add", gacc[:, :], gacc[:, :], b2b[:, :])
                P.v("tensor_copy", gb[:, :], gacc[:, :])
                P.T(sb["psT"][0:64, :], gb[:, :], sb["identb"][:, :])
                P.cp(gT[:, :], sb["psT"][0:64, :])
                for n in range(2):
                    P.mm(G[1 + n][:, :], gT[:, :], Woutb[:, n * 512:(n + 1) * 512],
                         start=True, stop=False)
                    P.mm(G[1 + n][:, :], sb["onesb"][:, :], boutb[:, n * 512:(n + 1) * 512],
                         start=False, stop=True)
                    P.v("tensor_add", yt[:, n * 512:(n + 1) * 512], G[1 + n][:, :],
                        xt[:, n * 512:(n + 1) * 512])
                P.dma(y[rr:rr + 128, :], yt[:, :])


def map_D(inp):
    m = {}
    m["D_Win"] = np.ascontiguousarray(inp["L1_ffn_Win"])
    m["D_bin"] = np.ascontiguousarray(np.asarray(inp["L1_ffn_bin_"])[None, :])
    m["D_W1r"] = np.ascontiguousarray(np.asarray(inp["L1_ffn_W1"]).T.reshape(1, 1024))
    m["D_b1r"] = np.ascontiguousarray(np.asarray(inp["L1_ffn_b1"]).T.reshape(1, 1024))
    m["D_W2r"] = np.ascontiguousarray(np.asarray(inp["L1_ffn_W2"]).T.reshape(1, 1024))
    m["D_b2r"] = np.ascontiguousarray(np.asarray(inp["L1_ffn_b2"])[None, :])
    m["D_Wout"] = np.ascontiguousarray(inp["L1_ffn_Wout"])
    m["D_bout"] = np.ascontiguousarray(np.asarray(inp["L1_ffn_bout"])[None, :])
    return m


def emit_C(P, nc, sb, x, y):
    W = nc.dram_tensor("C_W", [D, 5 * D], F32, kind="ExternalInput").ap()
    Wo = nc.dram_tensor("C_Wo", [D, D], F32, kind="ExternalInput").ap()
    lvec = nc.dram_tensor("C_lvec", [64, 4], F32, kind="ExternalInput").ap()
    lnwT = nc.dram_tensor("C_lnwT", [64, 16], F32, kind="ExternalInput").ap()
    lnbT = nc.dram_tensor("C_lnbT", [64, 16], F32, kind="ExternalInput").ap()
    Kb = nc.dram_tensor("C_Kb", [2, S], F32, kind="ExternalInput").ap()
    Qb = nc.dram_tensor("C_Qb", [2, 16 * S], F32, kind="ExternalInput").ap()
    CT = nc.dram_tensor("C_CT", [16, 128, 512], F32, kind="ExternalInput").ap()
    pT_h = nc.dram_tensor("C_pT_h", [4 * D, S], BF16, kind="Internal").ap()
    v_h = nc.dram_tensor("C_v_h", [S, D], BF16, kind="Internal").ap()
    oT_h = nc.dram_tensor("C_oT_h", [D, S], BF16, kind="Internal").ap()

    with ExitStack() as stk:
        E = stk.enter_context
        Wsb = E(nc.sbuf_tensor("C_Wsb", [128, 8 * 5120], BF16))
        Wfs = E(nc.sbuf_tensor("C_Wfs", [128, 5120], F32))
        WoSb = E(nc.sbuf_tensor("WoSb", [128, 8 * D], BF16))
        Kbs = E(nc.sbuf_tensor("Kbs", [2, S], F32))
        Qhs = E(nc.sbuf_tensor("Qhs", [2, S], F32))
        CTs = E(nc.sbuf_tensor("CTs", [128, 512], F32))
        lnwS = E(nc.sbuf_tensor("lnwS", [64, 16], F32))
        lnbS = E(nc.sbuf_tensor("lnbS", [64, 16], F32))
        lvs = E(nc.sbuf_tensor("lvs", [64, 4], F32))
        neglam = E(nc.sbuf_tensor("neglam", [1, 4], F32))
        xt = E(nc.sbuf_tensor("C_xt", [128, D], F32))
        lnt = E(nc.sbuf_tensor("C_lnt", [128, D], F32))
        lnb = E(nc.sbuf_tensor("C_lnb", [128, D], BF16))
        lnT = E(nc.sbuf_tensor("C_lnT", [128, 8 * 128], BF16))
        pstage = E(nc.sbuf_tensor("C_pstage", [128, 512], BF16))
        tstage = E(nc.sbuf_tensor("C_tstage", [128, 4 * 128], BF16))
        vb = E(nc.sbuf_tensor("C_vb", [128, D], BF16))
        q0sb = E(nc.sbuf_tensor("q0sb", [64, S], BF16))
        q1sb = E(nc.sbuf_tensor("q1sb", [64, S], BF16))
        k0sb = E(nc.sbuf_tensor("k0sb", [64, S], BF16))
        k1sb = E(nc.sbuf_tensor("k1sb", [64, S], BF16))
        vg = E(nc.sbuf_tensor("C_vg", [128, 16 * 65], BF16))
        ssb = E(nc.sbuf_tensor("ssb", [128, 512], F32))
        expb = E(nc.sbuf_tensor("C_expb", [128, 512], BF16))
        oT0 = E(nc.sbuf_tensor("oT0", [65, 512], F32))
        oT1 = E(nc.sbuf_tensor("oT1", [65, 512], F32))
        r0 = E(nc.sbuf_tensor("C_r0", [1, 512], F32))
        r1 = E(nc.sbuf_tensor("r1", [1, 512], F32))
        rb0 = E(nc.sbuf_tensor("rb0", [64, 512], F32))
        rb1 = E(nc.sbuf_tensor("rb1", [64, 512], F32))
        hsb = E(nc.sbuf_tensor("hsb", [64, 512], F32))
        hscr = E(nc.sbuf_tensor("hscr", [64, 512], F32))
        rowa = E(nc.sbuf_tensor("rowa", [1, 512], F32))
        rowb = E(nc.sbuf_tensor("rowb", [1, 512], F32))
        mub = E(nc.sbuf_tensor("mub", [64, 512], F32))
        oTf = E(nc.sbuf_tensor("C_oTf", [64, 512], BF16))
        oTl = E(nc.sbuf_tensor("C_oTl", [128, 8 * 128], BF16))
        yt = E(nc.sbuf_tensor("C_yt", [128, D], F32))

        for kc in range(8):
            P.dma(Wfs[:, :], W[kc * 128:(kc + 1) * 128, :])
            P.v("tensor_copy", Wsb[:, kc * 5120:(kc + 1) * 5120], Wfs[:, :])
        for kc in range(8):
            P.dma(Wfs[:, 0:D], Wo[kc * 128:(kc + 1) * 128, :])
            P.v("tensor_copy", WoSb[:, kc * D:(kc + 1) * D], Wfs[:, 0:D])
        P.dma(Kbs[:, :], Kb)
        P.dma(lnwS[:, :], lnwT)
        P.dma(lnbS[:, :], lnbT)
        P.dma(lvs[:, :], lvec)
        P.mm(sb["G0"][0:1, 0:1], lvs[:, 0:1], lvs[:, 1:2], start=True, stop=True)
        P.mm(sb["G0"][0:1, 1:2], lvs[:, 2:3], lvs[:, 3:4], start=True, stop=True)
        P.cp(neglam[:, 0:2], sb["G0"][0:1, 0:2])
        P.act(neglam[:, 2:4], neglam[:, 0:2], AF.Exp)
        P.v("tensor_add", neglam[:, 0:1], neglam[:, 2:3], neglam[:, 3:4])
        P.v("tensor_scalar_add", neglam[:, 0:1], neglam[:, 0:1], -1.6)

        G = [sb[f"G{i}"] for i in range(6)]
        vgv = vg[:, :].rearrange("p (t c) -> p t c", c=65)
        CTs2 = xt[:, 0:512]
        ssbs = [ssb[:, 0:512], lnt[:, 0:512]]
        expbs = [expb, pstage]
        SC = 0.125 ** 0.5
        for b in range(NB):
            r0b = b * S
            for i in range(NT):
                rr = r0b + i * 128
                P.dma(xt[:, :], x[rr:rr + 128, :])
                emit_ln(P, sb, lnt[:, :], xt[:, :], D)
                P.v("tensor_copy", lnb[:, :], lnt[:, :])
                emit_transpose_row(P, sb, lnb, lnT, 8)
                for half in range(2):
                    for kc in range(8):
                        for n in range(5):
                            nn = half * 5 + n
                            P.mm(G[n][:, :], lnT[:, kc * 128:(kc + 1) * 128],
                                 Wsb[:, kc * 5120 + nn * 512: kc * 5120 + (nn + 1) * 512],
                                 start=(kc == 0), stop=(kc == 7))
                    for n in range(5):
                        nn = half * 5 + n
                        proj = nn // 2
                        if proj < 4:
                            P.v("tensor_scalar", pstage[:, :], G[n][:, :], SC, None,
                                ALU.mult)
                            for hp in range(4):
                                ps = sb["psT" if hp % 2 == 0 else "psT2"]
                                P.T(ps[:, :], pstage[:, hp * 128:(hp + 1) * 128],
                                    sb["identb"][:, :])
                                P.cp(tstage[:, hp * 128:(hp + 1) * 128], ps[:, :])
                            ro = proj * D + (nn % 2) * 512
                            P.dma(pT_h[ro:ro + 512, i * 128:(i + 1) * 128]
                                  .rearrange("(a p) c -> p a c", p=128),
                                  tstage[:, :].rearrange("p (a c) -> p a c", c=128))
                        else:
                            P.v("tensor_copy", vb[:, (nn % 2) * 512:(nn % 2) * 512 + 512],
                                G[n][:, :])
                P.dma(v_h[i * 128:(i + 1) * 128, :], vb[:, :])
            for h in range(16):
                P.dma(q0sb[:, :], pT_h[0 * D + h * 64: 0 * D + (h + 1) * 64, :])
                P.dma(q1sb[:, :], pT_h[1 * D + h * 64: 1 * D + (h + 1) * 64, :])
                P.dma(k0sb[:, :], pT_h[2 * D + h * 64: 2 * D + (h + 1) * 64, :])
                P.dma(k1sb[:, :], pT_h[3 * D + h * 64: 3 * D + (h + 1) * 64, :])
                P.dma(Qhs[:, :], Qb[:, h * S:(h + 1) * S])
                P.dma(vgv[:, :, 0:64],
                      v_h[:, h * 64:(h + 1) * 64].rearrange("(t p) c -> p t c", p=128))
                P.v("memset", vgv[:, :, 64:65], 1.0)
                for qc in range(4):
                    nkt = 4 * (qc + 1)
                    sc_banks = [G[0], G[3], G[4], G[5]]
                    pend = None
                    j = 0
                    for kt in range(nkt):
                        diag = (kt // 4 == qc)
                        cts = CTs[:, :] if kt % 2 == 0 else CTs2
                        if diag:
                            P.dma(cts, CT[kt, :, :])
                        for (qq, kk, gi) in ((q0sb, k0sb, 1), (q1sb, k1sb, 2)):
                            bank = sc_banks[j % 4]
                            P.mm(bank[:, :], kk[:, kt * 128:(kt + 1) * 128],
                                 qq[:, qc * 512:(qc + 1) * 512], start=True, stop=False)
                            P.mm(bank[:, :], Kbs[:, kt * 128:(kt + 1) * 128],
                                 Qhs[:, qc * 512:(qc + 1) * 512], start=False, stop=True)
                            if pend is not None:
                                pgi, pkt, pj = pend
                                P.mm(G[pgi][0:65, :], vgv[:, pkt, :],
                                     expbs[pj % 2][:, :],
                                     start=(pkt == 0),
                                     stop=(pkt == nkt - 1))
                            if diag:
                                P.v("tensor_add", ssbs[j % 2], bank[:, :], cts)
                                P.act(expbs[j % 2][:, :], ssbs[j % 2], AF.Exp)
                            else:
                                P.act(expbs[j % 2][:, :], bank[:, :], AF.Exp)
                            pend = (gi, kt, j)
                            j += 1
                    pgi, pkt, pj = pend
                    P.mm(G[pgi][0:65, :], vgv[:, pkt, :], expbs[pj % 2][:, :],
                         start=(pkt == 0), stop=True)
                    P.cp(oT0[:, :], G[1][0:65, :])
                    P.cp(oT1[:, :], G[2][0:65, :])
                    P.dma(r0[:, :], oT0[64:65, :])
                    P.dma(r1[:, :], oT1[64:65, :])
                    P.v("reciprocal", r0[:, :], r0[:, :])
                    P.v("reciprocal", r1[:, :], r1[:, :])
                    P.v("tensor_scalar_mul", r1[:, :], r1[:, :], neglam[:, 0:1])
                    P.mm(G[0][0:64, :], sb["onesfr"][:, 0:64], r0[:, :],
                         start=True, stop=True)
                    P.cp(rb0[:, :], G[0][0:64, :])
                    P.mm(G[0][0:64, :], sb["onesfr"][:, 0:64], r1[:, :],
                         start=True, stop=True)
                    P.cp(rb1[:, :], G[0][0:64, :])
                    P.v("tensor_mul", hsb[:, :], oT0[0:64, :], rb0[:, :])
                    P.v("tensor_mul", hscr[:, :], oT1[0:64, :], rb1[:, :])
                    P.v("tensor_add", hsb[:, :], hsb[:, :], hscr[:, :])
                    P.mm(G[0][0:1, :], sb["onesfc"][0:64, :], hsb[:, :],
                         start=True, stop=True)
                    P.act(hscr[:, :], hsb[:, :], AF.Square)
                    P.mm(G[3][0:1, :], sb["onesfc"][0:64, :], hscr[:, :],
                         start=True, stop=True)
                    P.cp(rowa[:, :], G[0][0:1, :])
                    P.cp(rowb[:, :], G[3][0:1, :])
                    P.v("tensor_scalar_mul", rowa[:, :], rowa[:, :], 1.0 / 64)
                    P.v("tensor_scalar_mul", rowb[:, :], rowb[:, :], 1.0 / 64)
                    P.v("tensor_mul", r0[:, :], rowa[:, :], rowa[:, :])
                    P.v("tensor_sub", rowb[:, :], rowb[:, :], r0[:, :])
                    P.v("tensor_scalar_add", rowb[:, :], rowb[:, :], 1e-5)
                    P.act(r0[:, :], rowb[:, :], AF.Ln)
                    P.v("tensor_scalar_mul", r0[:, :], r0[:, :], -0.5)
                    P.act(rowb[:, :], r0[:, :], AF.Exp)
                    P.mm(G[0][0:64, :], sb["onesfr"][:, 0:64], rowa[:, :],
                         start=True, stop=True)
                    P.cp(mub[:, :], G[0][0:64, :])
                    P.mm(G[0][0:64, :], sb["onesfr"][:, 0:64], rowb[:, :],
                         start=True, stop=True)
                    P.cp(rb0[:, :], G[0][0:64, :])
                    P.v("tensor_sub", hsb[:, :], hsb[:, :], mub[:, :])
                    P.v("tensor_mul", hsb[:, :], hsb[:, :], rb0[:, :])
                    P.v("tensor_scalar", hsb[:, :], hsb[:, :], lnwS[:, h:h + 1],
                        lnbS[:, h:h + 1], ALU.mult, ALU.add)
                    P.v("tensor_copy", oTf[:, :], hsb[:, :])
                    P.dma(oT_h[h * 64:(h + 1) * 64, qc * 512:(qc + 1) * 512], oTf[:, :])
            for i in range(NT):
                rr = r0b + i * 128
                g0 = (i % 2) * 2
                P.dma(oTl[:, :].rearrange("p (a c) -> p a c", c=128),
                      oT_h[:, i * 128:(i + 1) * 128].rearrange("(a p) c -> p a c", p=128))
                for kc in range(8):
                    for n in range(2):
                        P.mm(G[g0 + n][:, :], oTl[:, kc * 128:(kc + 1) * 128],
                             WoSb[:, kc * D + n * 512: kc * D + (n + 1) * 512],
                             start=(kc == 0), stop=(kc == 7))
                P.dma(xt[:, :], x[rr:rr + 128, :])
                for n in range(2):
                    P.v("tensor_add", yt[:, n * 512:(n + 1) * 512], G[g0 + n][:, :],
                        xt[:, n * 512:(n + 1) * 512])
                P.dma(y[rr:rr + 128, :], yt[:, :])


def map_C(inp):
    m = {}
    m["C_W"] = np.ascontiguousarray(
        np.concatenate([inp["L1_attn_wq0"], inp["L1_attn_wq1"],
                        inp["L1_attn_wk0"], inp["L1_attn_wk1"],
                        inp["L1_attn_wv"]], axis=1))
    m["C_Wo"] = np.ascontiguousarray(inp["L1_attn_wo"])
    m["C_lvec"] = np.ascontiguousarray(
        np.stack([inp["L1_attn_l0"], inp["L1_attn_l1"],
                  inp["L1_attn_l2"], inp["L1_attn_l3"]], axis=1))
    m["C_lnwT"] = np.ascontiguousarray(np.asarray(inp["L1_attn_ln_w"]).T)
    m["C_lnbT"] = np.ascontiguousarray(np.asarray(inp["L1_attn_ln_b"]).T)
    m["C_Kb"] = _T["C_Kb"]
    m["C_Qb"] = _T["C_Qb"]
    m["C_CT"] = _T["C_CT"]
    return m


def emit_B(P, nc, sb, x, y):
    wqkv = nc.dram_tensor("B_wqkv", [2, D, 1536], F32, kind="ExternalInput").ap()
    qsc = nc.dram_tensor("B_qsc", [2, 1, 512], F32, kind="ExternalInput").ap()
    ksc = nc.dram_tensor("B_ksc", [2, 1, 512], F32, kind="ExternalInput").ap()
    wo = nc.dram_tensor("B_wo", [2, 512, D], F32, kind="ExternalInput").ap()
    lng = nc.dram_tensor("B_lng", [2, 1, D], F32, kind="ExternalInput").ap()
    lnbi = nc.dram_tensor("B_lnbi", [2, 1, D], F32, kind="ExternalInput").ap()
    w1 = nc.dram_tensor("B_w1", [2, D, 5460], F32, kind="ExternalInput").ap()
    b1 = nc.dram_tensor("B_b1", [2, 1, 5460], F32, kind="ExternalInput").ap()
    w2 = nc.dram_tensor("B_w2", [2, 2730, D], F32, kind="ExternalInput").ap()
    b2 = nc.dram_tensor("B_b2", [2, 1, D], F32, kind="ExternalInput").ap()
    xc_ = nc.dram_tensor("B_xc", [S, 512], F32, kind="ExternalInput").ap()
    xs_ = nc.dram_tensor("B_xs", [S, 512], F32, kind="ExternalInput").ap()
    xck = nc.dram_tensor("B_xck", [S, 512], F32, kind="ExternalInput").ap()
    xsk = nc.dram_tensor("B_xsk", [S, 512], F32, kind="ExternalInput").ap()
    mB0 = nc.dram_tensor("B_mB0", [512, 512], BF16, kind="ExternalInput").ap()
    mB = nc.dram_tensor("B_mB", [512, 1024], BF16, kind="ExternalInput").ap()
    mB0T = nc.dram_tensor("B_mB0T", [512, 512], BF16, kind="ExternalInput").ap()
    mBT = nc.dram_tensor("B_mBT", [1024, 512], BF16, kind="ExternalInput").ap()
    z_h = nc.dram_tensor("B_z_h", [NB * S, D], F32, kind="Internal").ap()
    z2_h = nc.dram_tensor("B_z2_h", [NB * S, D], F32, kind="Internal").ap()
    h_h = nc.dram_tensor("B_h_h", [NB * S, D], BF16, kind="Internal").ap()
    qT_h = nc.dram_tensor("B_qT_h", [512, NB * S], BF16, kind="Internal").ap()
    kT_h = nc.dram_tensor("B_kT_h", [512, NB * S], BF16, kind="Internal").ap()
    v_h = nc.dram_tensor("B_v_h", [NB * S, 512], BF16, kind="Internal").ap()
    oT_h = nc.dram_tensor("B_oT_h", [512, NB * S], BF16, kind="Internal").ap()
    gT_h = nc.dram_tensor("B_gT_h", [2730, NB * S], BF16, kind="Internal").ap()

    with ExitStack() as stk:
        E = stk.enter_context
        wA = E(nc.sbuf_tensor("wA", [128, 8 * 5460], BF16))
        Wfs = E(nc.sbuf_tensor("B_Wfs", [128, 1536], F32))
        b1S = E(nc.sbuf_tensor("b1S", [1, 5460], BF16))
        b2S = E(nc.sbuf_tensor("b2S", [1, D], BF16))
        rowf = E(nc.sbuf_tensor("B_rowf", [1, 1536], F32))
        qscB = E(nc.sbuf_tensor("qscB", [128, 512], F32))
        kscB = E(nc.sbuf_tensor("kscB", [128, 512], F32))
        lngB = E(nc.sbuf_tensor("lngB", [128, D], F32))
        lnbB = E(nc.sbuf_tensor("lnbB", [128, D], F32))
        xt = E(nc.sbuf_tensor("B_xt", [128, D], F32))
        ht = E(nc.sbuf_tensor("B_ht", [128, D], F32))
        lnt = E(nc.sbuf_tensor("B_lnt", [128, D], F32))
        lnb = E(nc.sbuf_tensor("B_lnb", [128, D], BF16))
        lnT = E(nc.sbuf_tensor("B_lnT", [128, 8 * 128], BF16))
        cost = E(nc.sbuf_tensor("B_cost", [128, 512], F32))
        sint = E(nc.sbuf_tensor("B_sint", [128, 512], F32))
        qs0 = E(nc.sbuf_tensor("B_qs0", [128, 512], F32))
        nrm = E(nc.sbuf_tensor("B_nrm", [128, 16], F32))
        qn = E(nc.sbuf_tensor("B_qn", [128, 512], F32))
        qa = E(nc.sbuf_tensor("B_qa", [128, 512], F32))
        shq = E(nc.sbuf_tensor("B_shq", [128, 512], F32))
        qf = E(nc.sbuf_tensor("B_qf", [128, 512], F32))
        qfb = E(nc.sbuf_tensor("B_qfb", [128, 512], BF16))
        tstage = E(nc.sbuf_tensor("B_tstage", [128, 4 * 128], BF16))
        vb = E(nc.sbuf_tensor("B_vb", [128, 512], BF16))
        maskP = E(nc.sbuf_tensor("maskP", [128, 4 * 1024], BF16))
        maskT = E(nc.sbuf_tensor("maskT", [128, 8 * 512], BF16))
        qsb = E(nc.sbuf_tensor("B_qsb", [64, 512], BF16))
        ksb = E(nc.sbuf_tensor("B_ksb", [64, 1024], BF16))
        qhat = E(nc.sbuf_tensor("qhat", [65, 512], BF16))
        khat = E(nc.sbuf_tensor("khat", [65, 1024], BF16))
        vg = E(nc.sbuf_tensor("B_vg", [128, 8 * 65], BF16))
        ssb = sb["lnscr"]
        nmx = E(nc.sbuf_tensor("B_nmx", [128, 1], F32))
        nmr = E(nc.sbuf_tensor("B_nmr", [1, 128], F32))
        nmrb = E(nc.sbuf_tensor("B_nmrb", [1, 128], BF16))
        expb = E(nc.sbuf_tensor("B_expb", [128, 512], BF16))
        oTsb = E(nc.sbuf_tensor("B_oTsb", [65, 512], F32))
        r0 = E(nc.sbuf_tensor("B_r0", [1, 512], F32))
        rb = E(nc.sbuf_tensor("B_rb", [64, 512], F32))
        oTf = E(nc.sbuf_tensor("B_oTf", [64, 512], BF16))
        oTl = E(nc.sbuf_tensor("B_oTl", [128, 4 * 128], BF16))
        hb = E(nc.sbuf_tensor("B_hb", [128, D], BF16))
        gel = E(nc.sbuf_tensor("B_gel", [128, 512], F32))
        prodb = E(nc.sbuf_tensor("B_prodb", [128, 512], BF16))
        gTl = E(nc.sbuf_tensor("gTl", [128, 21 * 128], BF16))
        gTll = E(nc.sbuf_tensor("gTll", [42, 128], BF16))
        yt = lnt

        G = [sb[f"G{i}"] for i in range(6)]
        vgv = vg[:, :].rearrange("p (t c) -> p t c", c=65)

        def conv_load(dst_bf, src_ap, rows, cols):
            c0 = 0
            while c0 < cols:
                cw = min(1536, cols - c0)
                P.dma(Wfs[0:rows, 0:cw], src_ap[:, c0:c0 + cw] if rows == 128
                      else src_ap[0:rows, c0:c0 + cw])
                P.v("tensor_copy", dst_bf[0:rows, c0:c0 + cw], Wfs[0:rows, 0:cw])
                c0 += cw

        def bcast_row(src_l_ap, dst, w_):
            c0 = 0
            while c0 < w_:
                cw = min(512, w_ - c0)
                P.dma(rowf[:, 0:cw], src_l_ap[:, c0:c0 + cw])
                P.mm(G[0][:, 0:cw], sb["onesfr"][:, :], rowf[:, 0:cw],
                     start=True, stop=True)
                P.cp(dst[:, c0:c0 + cw], G[0][:, 0:cw])
                c0 += cw

        def shifted_load(dst, srcbuf, rr, i):
            P.dma(dst[:, 0:512], srcbuf[rr:rr + 128, 0:512])
            if i == 0:
                P.v("memset", dst[0:1, 512:1024], 0.0)
                P.dma(dst[1:128, 512:1024], srcbuf[rr:rr + 127, 512:1024])
            else:
                P.dma(dst[:, 512:1024], srcbuf[rr - 1:rr + 127, 512:1024])

        for b in range(NB):
            for i in range(NT):
                rr = b * S + i * 128
                P.dma(xt[:, :], x[rr:rr + 128, :])
                emit_ln(P, sb, lnt[:, :], xt[:, :], D)
                P.dma(z_h[rr:rr + 128, :], lnt[:, :])

        for l in range(2):
            for kc in range(8):
                conv_load(wA[:, kc * 1536:(kc + 1) * 1536], wqkv[l, kc * 128:(kc + 1) * 128, :],
                          128, 1536)
            bcast_row(qsc[l], qscB, 512)
            bcast_row(ksc[l], kscB, 512)
            for b in range(NB):
                r0b = b * S
                for i in range(NT):
                    rr = r0b + i * 128
                    shifted_load(ht, z_h, rr, i)
                    P.v("tensor_copy", lnb[:, :], ht[:, :])
                    emit_transpose_row(P, sb, lnb, lnT, 8)
                    for kc in range(8):
                        for n in range(3):
                            P.mm(G[n][:, :], lnT[:, kc * 128:(kc + 1) * 128],
                                 wA[:, kc * 1536 + n * 512: kc * 1536 + (n + 1) * 512],
                                 start=(kc == 0), stop=(kc == 7))
                    for (gi, scB, cO, sO, Tdst) in ((0, qscB, xc_, xs_, qT_h),
                                                    (1, kscB, xck, xsk, kT_h)):
                        P.cp(qs0[:, :], G[gi][:, :])
                        for hh in range(8):
                            P.act(sb["lnscr"][:, hh * 64:(hh + 1) * 64],
                                  qs0[:, hh * 64:(hh + 1) * 64], AF.Square,
                                  accum_out=nrm[:, hh:hh + 1])
                        P.act(nrm[:, 0:8], nrm[:, 0:8], AF.Sqrt)
                        P.v("tensor_scalar_max", nrm[:, 0:8], nrm[:, 0:8], 1e-12)
                        P.v("reciprocal", nrm[:, 8:16], nrm[:, 0:8])
                        for hh in range(8):
                            P.v("tensor_scalar_mul", qn[:, hh * 64:(hh + 1) * 64],
                                qs0[:, hh * 64:(hh + 1) * 64], nrm[:, 8 + hh:9 + hh])
                        P.v("tensor_mul", qn[:, :], qn[:, :], scB[:, :])
                        P.dma(cost[:, :], cO[i * 128:(i + 1) * 128, :])
                        P.dma(sint[:, :], sO[i * 128:(i + 1) * 128, :])
                        P.v("tensor_mul", qa[:, :], qn[:, :], cost[:, :])
                        qv = qn[:, :].rearrange("p (h a c) -> p h a c", a=2, c=32)
                        sv = shq[:, :].rearrange("p (h a c) -> p h a c", a=2, c=32)
                        P.v("tensor_copy", sv[:, :, 0, :], qv[:, :, 1, :])
                        P.v("tensor_copy", sv[:, :, 1, :], qv[:, :, 0, :])
                        P.v("tensor_mul", shq[:, :], shq[:, :], sint[:, :])
                        P.v("tensor_add", qf[:, :], qa[:, :], shq[:, :])
                        P.v("tensor_copy", qfb[:, :], qf[:, :])
                        for hp in range(4):
                            ps = sb["psT" if hp % 2 == 0 else "psT2"]
                            P.T(ps[:, :], qfb[:, hp * 128:(hp + 1) * 128],
                                sb["identb"][:, :])
                            P.cp(tstage[:, hp * 128:(hp + 1) * 128], ps[:, :])
                        P.dma(Tdst[:, rr:rr + 128].rearrange("(a p) c -> p a c", p=128),
                              tstage[:, :].rearrange("p (a c) -> p a c", c=128))
                    P.v("tensor_copy", vb[:, :], G[2][:, :])
                    P.dma(v_h[rr:rr + 128, :], vb[:, :])
            for w in range(4):
                kw = 512 if w == 0 else 1024
                nkt = kw // 128
                if w == 0:
                    P.dma(maskP[:, 0:4 * 512].rearrange("p (a c) -> p a c", c=512),
                          mB0.rearrange("(a p) c -> p a c", p=128))
                    P.dma(maskT[:, 0:4 * 512].rearrange("p (a c) -> p a c", c=512),
                          mB0T.rearrange("(a p) c -> p a c", p=128))
                elif w == 1:
                    P.dma(maskP[:, :].rearrange("p (a c) -> p a c", c=1024),
                          mB.rearrange("(a p) c -> p a c", p=128))
                    P.dma(maskT[:, :].rearrange("p (a c) -> p a c", c=512),
                          mBT.rearrange("(a p) c -> p a c", p=128))
                for b in range(NB):
                    r0b = b * S
                    k0g = r0b + (w * 512 - (0 if w == 0 else 512))
                    q0g = r0b + w * 512
                    for h in range(8):
                        P.dma(qsb[:, :], qT_h[h * 64:(h + 1) * 64, q0g:q0g + 512])
                        P.dma(ksb[:, 0:kw], kT_h[h * 64:(h + 1) * 64, k0g:k0g + kw])
                        P.dma(vg[:, 0:nkt * 65].rearrange("p (t c) -> p t c", c=65)[:, :, 0:64],
                              v_h[k0g:k0g + kw, h * 64:(h + 1) * 64]
                              .rearrange("(t p) c -> p t c", p=128))
                        P.v("memset",
                            vg[:, 0:nkt * 65].rearrange("p (t c) -> p t c", c=65)[:, :, 64:65],
                            1.0)
                        P.v("tensor_copy", qhat[0:64, :], qsb[:, :])
                        P.v("tensor_copy", khat[0:64, 0:kw], ksb[:, 0:kw])
                        P.v("memset", khat[64:65, 0:kw], 1.0)
                        for t in range(4):
                            for n in range(kw // 512):
                                P.mm(G[n][:, :], qsb[:, t * 128:(t + 1) * 128],
                                     ksb[:, n * 512:(n + 1) * 512], start=True, stop=True)
                            for n in range(kw // 512):
                                P.v("tensor_add", ssb[:, n * 512:(n + 1) * 512],
                                    G[n][:, :],
                                    maskP[:, t * kw + n * 512: t * kw + (n + 1) * 512])
                            P.v("reduce_max", nmx[:, :], ssb[:, 0:kw], axis=AX.X,
                                negate=True)
                            P.T(sb["psTf"][0:1, :], nmx[:, :], sb["identf"][:, :])
                            P.cp(nmr[:, :], sb["psTf"][0:1, :])
                            P.v("tensor_copy", nmrb[:, :], nmr[:, :])
                            P.dma(qhat[64:65, t * 128:(t + 1) * 128], nmrb[:, :])
                        for kt in range(nkt):
                            bank = G[0] if kt % 2 == 0 else G[2]
                            sbuf_s = ssb[:, 0:512] if kt % 2 == 0 else qa[:, :]
                            eb = expb if kt % 2 == 0 else qfb
                            P.mm(bank[:, :], khat[:, kt * 128:(kt + 1) * 128],
                                 qhat[:, :], start=True, stop=True)
                            if kt > 0:
                                peb = expb if (kt - 1) % 2 == 0 else qfb
                                P.mm(G[1][0:65, :],
                                     vg[:, 0:nkt * 65]
                                     .rearrange("p (t c) -> p t c", c=65)[:, kt - 1, :],
                                     peb[:, :], start=(kt == 1), stop=False)
                            P.v("tensor_add", sbuf_s, bank[:, :],
                                maskT[:, kt * 512:(kt + 1) * 512])
                            P.act(eb[:, :], sbuf_s, AF.Exp)
                        leb = expb if (nkt - 1) % 2 == 0 else qfb
                        P.mm(G[1][0:65, :],
                             vg[:, 0:nkt * 65].rearrange("p (t c) -> p t c", c=65)[:, nkt - 1, :],
                             leb[:, :], start=False, stop=True)
                        P.cp(oTsb[:, :], G[1][0:65, :])
                        P.dma(r0[:, :], oTsb[64:65, :])
                        P.v("reciprocal", r0[:, :], r0[:, :])
                        P.mm(G[0][0:64, :], sb["onesfr"][:, 0:64], r0[:, :],
                             start=True, stop=True)
                        P.cp(rb[:, :], G[0][0:64, :])
                        P.v("tensor_mul", oTf[:, :], oTsb[0:64, :], rb[:, :])
                        P.dma(oT_h[h * 64:(h + 1) * 64, q0g:q0g + 512], oTf[:, :])
            for kc in range(4):
                conv_load(wA[:, kc * D:(kc + 1) * D], wo[l, kc * 128:(kc + 1) * 128, :],
                          128, D)
            for b in range(NB):
                r0b = b * S
                for i in range(NT):
                    rr = r0b + i * 128
                    g0 = (i % 2) * 2
                    P.dma(oTl[:, :].rearrange("p (a c) -> p a c", c=128),
                          oT_h[:, rr:rr + 128].rearrange("(a p) c -> p a c", p=128))
                    for kc in range(4):
                        for n in range(2):
                            P.mm(G[g0 + n][:, :], oTl[:, kc * 128:(kc + 1) * 128],
                                 wA[:, kc * D + n * 512: kc * D + (n + 1) * 512],
                                 start=(kc == 0), stop=(kc == 3))
                    P.dma(xt[:, :], z_h[rr:rr + 128, :])
                    for n in range(2):
                        P.v("tensor_add", yt[:, n * 512:(n + 1) * 512], G[g0 + n][:, :],
                            xt[:, n * 512:(n + 1) * 512])
                    P.dma(z2_h[rr:rr + 128, :], yt[:, :])
            bcast_row(lng[l], lngB, D)
            bcast_row(lnbi[l], lnbB, D)
            for b in range(NB):
                r0b = b * S
                for i in range(NT):
                    rr = r0b + i * 128
                    shifted_load(ht, z2_h, rr, i)
                    emit_ln(P, sb, lnt[:, :], ht[:, :], D, gain=lngB[:, :],
                            bias_t=lnbB[:, :])
                    P.v("tensor_copy", hb[:, :], lnt[:, :])
                    P.dma(h_h[rr:rr + 128, :], hb[:, :])
            for kc in range(8):
                conv_load(wA[:, kc * 5460:(kc + 1) * 5460],
                          w1[l, kc * 128:(kc + 1) * 128, :], 128, 5460)
            conv_load(b1S, b1[l], 1, 5460)
            for b in range(NB):
                r0b = b * S
                for i in range(NT):
                    rr = r0b + i * 128
                    P.dma(hb[:, :], h_h[rr:rr + 128, :])
                    emit_transpose_row(P, sb, hb, lnT, 8)
                    for c in range(6):
                        c0 = c * 512
                        cw = min(512, 2730 - c0)
                        ga, gb_ = (G[0], G[1]) if c % 2 == 0 else (G[2], G[3])
                        gelc = gel if c % 2 == 0 else qf
                        prodc = prodb if c % 2 == 0 else vb
                        for kc in range(8):
                            P.mm(ga[:, 0:cw], lnT[:, kc * 128:(kc + 1) * 128],
                                 wA[:, kc * 5460 + c0: kc * 5460 + c0 + cw],
                                 start=(kc == 0), stop=False)
                            P.mm(gb_[:, 0:cw], lnT[:, kc * 128:(kc + 1) * 128],
                                 wA[:, kc * 5460 + 2730 + c0: kc * 5460 + 2730 + c0 + cw],
                                 start=(kc == 0), stop=False)
                        P.mm(ga[:, 0:cw], sb["onesb"][:, :], b1S[:, c0:c0 + cw],
                             start=False, stop=True)
                        P.mm(gb_[:, 0:cw], sb["onesb"][:, :],
                             b1S[:, 2730 + c0:2730 + c0 + cw], start=False, stop=True)
                        P.act(gelc[:, 0:cw], gb_[:, 0:cw], AF.Gelu)
                        P.v("tensor_mul", prodc[:, 0:cw], ga[:, 0:cw], gelc[:, 0:cw])
                        nsub = (cw + 127) // 128
                        for tsub in range(nsub):
                            tw = min(128, cw - tsub * 128)
                            ps = sb["psT" if tsub % 2 == 0 else "psT2"]
                            P.T(ps[0:tw, :], prodc[:, tsub * 128:tsub * 128 + tw],
                                sb["identb"][:, :])
                            P.cp(tstage[0:tw, tsub * 128:(tsub + 1) * 128],
                                 ps[0:tw, :])
                            P.dma(gT_h[c0 + tsub * 128: c0 + tsub * 128 + tw, rr:rr + 128],
                                  tstage[0:tw, tsub * 128:(tsub + 1) * 128])
            for kc in range(22):
                r = kc * 128
                nr = min(128, 2730 - r)
                conv_load(wA[:, kc * D:(kc + 1) * D], w2[l, r:r + nr, :], nr, D)
            conv_load(b2S, b2[l], 1, D)
            for b in range(NB):
                r0b = b * S
                for i in range(NT):
                    rr = r0b + i * 128
                    p = i % 2
                    gTlc = gTl[:, :] if p == 0 else maskP[:, 0:2688]
                    gTllc = gTll[:, :] if p == 0 else maskT[0:42, 0:128]
                    xtc = xt if p == 0 else ht
                    P.dma(gTlc.rearrange("p (a c) -> p a c", c=128),
                          gT_h[0:2688, rr:rr + 128].rearrange("(a p) c -> p a c", p=128))
                    P.dma(gTllc, gT_h[2688:2730, rr:rr + 128])
                    g0 = 2 if p == 0 else 4
                    for kc in range(22):
                        nr = min(128, 2730 - kc * 128)
                        lhs = (gTlc[:, kc * 128:(kc + 1) * 128] if kc < 21
                               else gTllc)
                        for n in range(2):
                            P.mm(G[g0 + n][:, :], lhs,
                                 wA[0:nr, kc * D + n * 512: kc * D + (n + 1) * 512],
                                 start=(kc == 0), stop=False)
                    for n in range(2):
                        P.mm(G[g0 + n][:, :], sb["onesb"][:, :],
                             b2S[:, n * 512:(n + 1) * 512], start=False, stop=True)
                    P.dma(xtc[:, :], z2_h[rr:rr + 128, :])
                    for n in range(2):
                        P.v("tensor_add", yt[:, n * 512:(n + 1) * 512], G[g0 + n][:, :],
                            xtc[:, n * 512:(n + 1) * 512])
                    if l == 1:
                        P.dma(xtc[:, :], x[rr:rr + 128, :])
                        P.v("tensor_add", yt[:, :], yt[:, :], xtc[:, :])
                        P.dma(y[rr:rr + 128, :], yt[:, :])
                    else:
                        P.dma(z_h[rr:rr + 128, :], yt[:, :])


def map_B(inp):
    m = {}
    m["B_wqkv"] = np.ascontiguousarray(inp["L0_ffn_wqkv"])
    m["B_qsc"] = np.ascontiguousarray(
        np.stack([np.tile(np.asarray(inp["L0_ffn_q_scale"])[l], 8)[None, :]
                  for l in range(2)]))
    m["B_ksc"] = np.ascontiguousarray(
        np.stack([np.tile(np.asarray(inp["L0_ffn_k_scale"])[l], 8)[None, :]
                  for l in range(2)]))
    m["B_wo"] = np.ascontiguousarray(inp["L0_ffn_wo"])
    m["B_lng"] = np.ascontiguousarray(np.asarray(inp["L0_ffn_ln_g"])[:, None, :])
    m["B_lnbi"] = np.ascontiguousarray(np.asarray(inp["L0_ffn_ln_b"])[:, None, :])
    m["B_w1"] = np.ascontiguousarray(inp["L0_ffn_w1"])
    m["B_b1"] = np.ascontiguousarray(np.asarray(inp["L0_ffn_b1"])[:, None, :])
    m["B_w2"] = np.ascontiguousarray(inp["L0_ffn_w2"])
    m["B_b2"] = np.ascontiguousarray(np.asarray(inp["L0_ffn_b2"])[:, None, :])
    m["B_xc"] = _T["xpos_cq"]
    m["B_xs"] = _T["xpos_sq"]
    m["B_xck"] = _T["xpos_ck"]
    m["B_xsk"] = _T["xpos_sk"]
    m["B_mB0"] = _T["maskB0"]
    m["B_mB"] = _T["maskB"]
    m["B_mB0T"] = _T["maskB0T"]
    m["B_mBT"] = _T["maskBT"]
    return m


EMITS = [emit_A, emit_B, emit_C, emit_D]
MAPS = [map_A, map_B, map_C, map_D]


def build_fused(stages=(0, 1, 2, 3)):
    nc = bass.Bass("TRN2", target_bir_lowering=False)
    aps = {}
    decl_common_inputs(nc, aps)
    x = nc.dram_tensor("x", [NB * S, D], F32, kind="ExternalInput").ap()
    y = nc.dram_tensor("y", [NB * S, D], F32, kind="ExternalOutput").ap()
    mids = [nc.dram_tensor(f"mid{i}", [NB * S, D], F32, kind="Internal").ap()
            for i in range(len(stages) - 1)]
    chain = [x] + mids + [y]
    P = Prog()
    with ExitStack() as stk:
        sb = {}
        alloc_common(stk.enter_context, nc, sb)
        sems = {e: stk.enter_context(nc.semaphore(f"sem_{e}")) for e in ENGINES}
        load_common(P, sb, aps)
        for k, si in enumerate(stages):
            P.barrier()
            EMITS[si](P, nc, sb, chain[k], chain[k + 1])
        finalize(nc, P, sems)
    return nc


def map_fused(xc, inp, stages=(0, 1, 2, 3)):
    m = common_in_map()
    m["x"] = xc
    for si in stages:
        m.update(MAPS[si](inp))
    return m


def kernel(**inp):
    x = np.ascontiguousarray(np.asarray(inp["L0_attn_x"], dtype=np.float32))
    xs = [np.ascontiguousarray(x[2 * c:2 * c + 2].reshape(NB * S, D))
          for c in range(NCORES)]
    nc = build_fused()
    maps = [map_fused(xs[c], inp) for c in range(NCORES)]
    res = run_bass_kernel_spmd(nc, maps, core_ids=list(range(NCORES)))
    out = np.stack([np.asarray(r["y"], np.float32).reshape(NB, S, D)
                    for r in res.results]).reshape(16, S, D)
    return out
